# version 18 (final)
# baseline (speedup 1.0000x reference)
import contextlib
import numpy as np
import ml_dtypes
import concourse.bass as bass
import concourse.mybir as mybir
from concourse.bass_utils import run_bass_kernel_spmd

F32 = mybir.dt.float32
BF16 = mybir.dt.bfloat16
AF = mybir.ActivationFunctionType
ALU = mybir.AluOpType
AX = mybir.AxisListType

NT = 2048
D = 2048
FC = 16
TC = 16
NH = 16
HD = 128
BLK = 256
NB = 8
ICN = 44
INTER = 5632
BIGNEG = 40000.0

ENGS = ("tensor", "vector", "scalar", "sync", "gpsimd")

C_NEG1, C_EPS_LN, C_EPS_RMS, C_INVD, C_256 = 0, 1, 2, 3, 4


class Sem:
    def __init__(self, h):
        self.h = h
        self.n = 0


class Builder:
    def __init__(self):
        self.streams = {e: [] for e in ENGS}

    def w(self, eng, sem, val):
        if val is None or val <= 0:
            return
        self.streams[eng].append(lambda e, h=sem.h, v=val: e.wait_ge(h, v))

    def op(self, eng, method, *args, inc=None, incn=1, **kwargs):
        def f(e, m=method, a=args, k=kwargs, inc=inc, incn=incn):
            r = getattr(e, m)(*a, **k)
            if inc is not None:
                r.then_inc(inc.h, incn)
        self.streams[eng].append(f)
        if inc is not None:
            inc.n += incn
            return inc.n
        return None


def build_core_program(dbg=False):
    nc = bass.Bass("TRN2", target_bir_lowering=False)

    x_d = nc.dram_tensor("x", [NT, D], F32, kind="ExternalInput").ap()
    wqkv_d = nc.dram_tensor("wqkv", [NH, 128, 3 * FC * 128], F32, kind="ExternalInput").ap()
    wgate_d = nc.dram_tensor("wgate", [FC, 128, FC * 128], F32, kind="ExternalInput").ap()
    wout_d = nc.dram_tensor("wout", [FC, 128, D], F32, kind="ExternalInput").ap()
    wg_d = nc.dram_tensor("wgff", [ICN, 128, FC * 128], F32, kind="ExternalInput").ap()
    wu_d = nc.dram_tensor("wuff", [ICN, 128, FC * 128], F32, kind="ExternalInput").ap()
    wd_d = nc.dram_tensor("wdff", [ICN, 128, D], F32, kind="ExternalInput").ap()
    slopes_d = nc.dram_tensor("slopes", [128, NH], F32, kind="ExternalInput").ap()
    nw_d = nc.dram_tensor("nw", [128, FC], F32, kind="ExternalInput").ap()
    gs_d = nc.dram_tensor("gs", [128, ICN], F32, kind="ExternalInput").ap()
    us_d = nc.dram_tensor("us", [128, ICN], F32, kind="ExternalInput").ap()
    dsb_d = nc.dram_tensor("dsb", [128, D], F32, kind="ExternalInput").ap()

    out_d = nc.dram_tensor("out", [NT, D], F32, kind="ExternalOutput").ap()
    scratch_kind = "ExternalOutput" if dbg else "Internal"
    if dbg:
        xn1_dbg = nc.dram_tensor("xn1_dbg", [128, FC * NT], BF16, kind="ExternalOutput").ap()
        xn2_dbg = nc.dram_tensor("xn2_dbg", [128, FC * NT], BF16, kind="ExternalOutput").ap()
        istd_dbg = nc.dram_tensor("istd_dbg", [128, NT], F32, kind="ExternalOutput").ap()
    o_scr = nc.dram_tensor("o_scr", [NH, 128, NT], BF16, kind=scratch_kind).ap()
    go_scr = nc.dram_tensor("go_scr", [FC, 128, NT], BF16, kind=scratch_kind).ap()
    x1_scr = nc.dram_tensor("x1_scr", [NT, D], F32, kind=scratch_kind).ap()
    o2_scr = nc.dram_tensor("o2_scr", [NH, 128, NT], BF16, kind="Internal").ap()
    h_scr = nc.dram_tensor("h_scr", [ICN, 128, NT], BF16, kind=scratch_kind).ap()

    ident_np = np.eye(128, dtype=ml_dtypes.bfloat16)
    ones_np = np.ones((128, 128), dtype=ml_dtypes.bfloat16)
    consts_np = np.broadcast_to(
        np.array([-1.0, 1e-5, 1e-6, 1.0 / 2048.0, 256.0, 0.0, 1.0, 0.0], dtype=np.float32),
        (128, 8)).copy()
    arr_np = np.broadcast_to(np.arange(1, BLK + 1, dtype=np.float32), (128, BLK)).copy()
    karr_np = np.empty((128, 2), dtype=np.float32)
    for sub in range(2):
        karr_np[:, sub] = 255.0 - (sub * 128 + np.arange(128))
    idxt_np = np.empty((128, 2 * BLK), dtype=np.float32)
    for sub in range(2):
        tk = sub * 128 + np.arange(128)[:, None]
        tq = np.arange(BLK)[None, :]
        dif = tq - tk
        idxt_np[:, sub * BLK:(sub + 1) * BLK] = np.where(dif >= 0, dif, BIGNEG)
    ident_dr = nc.inline_tensor(ident_np, name="identc").ap()
    ones_dr = nc.inline_tensor(ones_np, name="onesc").ap()
    consts_dr = nc.inline_tensor(consts_np, name="constsc").ap()
    arr_dr = nc.inline_tensor(arr_np, name="arrc").ap()
    karr_dr = nc.inline_tensor(karr_np, name="karrc").ap()
    idxt_dr = nc.inline_tensor(idxt_np, name="idxtc").ap()

    b = Builder()

    with contextlib.ExitStack() as es0:
        ec = es0.enter_context
        hT = ec(nc.semaphore("sT")); hV = ec(nc.semaphore("sV")); hS = ec(nc.semaphore("sS"))
        hW = ec(nc.semaphore("sW")); hA = ec(nc.semaphore("sA")); hO = ec(nc.semaphore("sO"))
        cst = ec(nc.sbuf_tensor("cst", [128, 8], F32))
        ident_s = ec(nc.sbuf_tensor("ident_s", [128, 128], BF16))
        ones_s = ec(nc.sbuf_tensor("ones_s", [128, 128], BF16))
        slopes_s = ec(nc.sbuf_tensor("slopes_s", [128, NH], F32))
        nw_s = ec(nc.sbuf_tensor("nw_s", [128, FC], F32))
        gs_s = ec(nc.sbuf_tensor("gs_s", [128, ICN], F32))
        us_s = ec(nc.sbuf_tensor("us_s", [128, ICN], F32))
        dsb_s = ec(nc.sbuf_tensor("dsb_s", [128, D], F32))
        sT, sV, sS = Sem(hT), Sem(hV), Sem(hS)
        sW, sA, sO = Sem(hW), Sem(hA), Sem(hO)
        sems = {"sT": sT, "sV": sV, "sS": sS, "sW": sW, "sA": sA, "sO": sO}
        own = {"tensor": {"sT"}, "vector": {"sV"}, "scalar": {"sS"},
               "sync": set(), "gpsimd": set()}

        def barrier(skips=None):
            snap = {k: s.n for k, s in sems.items()}
            for eng in ENGS:
                for k, s in sems.items():
                    if k in own[eng]:
                        continue
                    if skips and k in skips.get(eng, ()):
                        continue
                    b.w(eng, s, snap[k])

        for dst, src in ((cst, consts_dr), (ident_s, ident_dr), (ones_s, ones_dr),
                         (slopes_s, slopes_d), (nw_s, nw_d), (gs_s, gs_d),
                         (us_s, us_d), (dsb_s, dsb_d)):
            b.op("gpsimd", "dma_start", dst[:, :], src[:, :], inc=sW, incn=16)
        consts_ld0 = sW.n

        with nc.sbuf_tensor("xn1", [128, FC * NT], BF16) as xn1:
            with contextlib.ExitStack() as esA:
                ec = esA.enter_context
                arr_s = ec(nc.sbuf_tensor("arr_s", [128, BLK], F32))
                karr_s = ec(nc.sbuf_tensor("karr_s", [128, 2], F32))
                idxt_s = ec(nc.sbuf_tensor("idxt_s", [128, 2 * BLK], F32))
                qdec = ec(nc.sbuf_tensor("qdec", [128, NH * BLK], BF16))
                ddt = ec(nc.sbuf_tensor("ddt", [128, NH * 2 * BLK], BF16))
                kdec = ec(nc.sbuf_tensor("kdec", [128, NH * 2], F32))
                bdall = ec(nc.sbuf_tensor("bdall", [128, NH], F32))
                dtmpA = ec(nc.sbuf_tensor("dtmpA", [128, 2 * BLK], F32))
                wsb_g = nc.sbuf_tensor("wsb", [128, 2 * 3 * FC * 128], BF16,
                                       side="right")
                wsb = wsb_g.__enter__()
                xA = ec(nc.sbuf_tensor("xA", [128, 2 * D], F32))
                xntm = ec(nc.sbuf_tensor("xntm", [128, 2 * D], BF16))
                stat = ec(nc.sbuf_tensor("stat", [128, 64], F32))
                q_sb = ec(nc.sbuf_tensor("q_sb", [128, 2 * NT], BF16))
                k_sb = ec(nc.sbuf_tensor("k_sb", [128, 2 * NT], BF16))
                v_sb = ec(nc.sbuf_tensor("v_sb", [128, 2 * NT], BF16))
                kkT = ec(nc.sbuf_tensor("kkT", [128, 2 * NT], BF16))
                vT = ec(nc.sbuf_tensor("vT", [128, 2 * NT], BF16))
                qd_sb = ec(nc.sbuf_tensor("qd_sb", [128, 2 * BLK], BF16))
                qkTd = ec(nc.sbuf_tensor("qkTd", [128, 2 * 2 * BLK], BF16))
                kv_sb = ec(nc.sbuf_tensor("kv_sb", [128, HD], F32))
                kv_bf = ec(nc.sbuf_tensor("kv_bf", [128, HD], BF16))
                o_sb = ec(nc.sbuf_tensor("o_sb", [128, 2 * NT], BF16))
                o2st = ec(nc.sbuf_tensor("o2st", [128, NT], BF16))
                pp0 = ec(nc.psum_tensor("pp0", [128, 512], F32))
                pp1 = ec(nc.psum_tensor("pp1", [128, 512], F32))
                qk0 = ec(nc.psum_tensor("qk0", [128, 512], F32))
                qk1 = ec(nc.psum_tensor("qk1", [128, 512], F32))
                po0 = ec(nc.psum_tensor("po0", [128, BLK], F32))
                po1 = ec(nc.psum_tensor("po1", [128, BLK], F32))
                pkv0 = ec(nc.psum_tensor("pkv0", [128, HD], F32))
                pkv1 = ec(nc.psum_tensor("pkv1", [128, HD], F32))
                pp = (pp0, pp1)
                qkp = (qk0, qk1)
                pop = (po0, po1)
                pkvp = (pkv0, pkv1)

                for dst, src in ((arr_s, arr_dr), (karr_s, karr_dr), (idxt_s, idxt_dr)):
                    b.op("gpsimd", "dma_start", dst[:, :], src[:, :], inc=sW, incn=16)
                consts_ld = sW.n
                b.w("vector", sW, consts_ld)
                b.w("scalar", sW, consts_ld)
                b.op("vector", "tensor_scalar", dtmpA[:, 0:NH], slopes_s[:, :],
                     cst[:, C_NEG1:C_NEG1 + 1], None, ALU.mult)
                b.op("vector", "tensor_scalar", dtmpA[:, NH:2 * NH], slopes_s[:, :],
                     cst[:, C_256:C_256 + 1], None, ALU.mult, inc=sV)
                negs_done = sV.n
                b.w("scalar", sV, negs_done)
                b.op("scalar", "activation", bdall[:, :], dtmpA[:, NH:2 * NH], AF.Exp,
                     scale=cst[:, C_NEG1:C_NEG1 + 1], inc=sS)
                for h in range(NH):
                    nscol = dtmpA[:, h:h + 1]
                    b.op("scalar", "activation", qdec[:, h * BLK:(h + 1) * BLK],
                         arr_s[:, :], AF.Exp, scale=nscol)
                    b.op("scalar", "activation", kdec[:, 2 * h:2 * h + 2],
                         karr_s[:, :], AF.Exp, scale=nscol)
                    b.op("scalar", "activation", ddt[:, h * 2 * BLK:(h + 1) * 2 * BLK],
                         idxt_s[:, :], AF.Exp, scale=nscol,
                         inc=(sS if h == NH - 1 else None))
                decay_done_s = sS.n

                pp_free_v = [0, 0]
                pp_free_s = [0, 0]
                tr4 = (qk0, qk1, po0, po1)
                tr4_free = [0, 0, 0, 0]
                w_ld = [0] * NH
                proj_done = [0] * NH
                attn_mm_done = [0] * NH
                qread_done = [0] * NH
                o_store = [0] * NH
                o2_store = [0] * NH
                o_sq_done = [0] * NH
                ev_k_done = [0] * NH
                ev_v_done = [0] * NH
                ev_all_done = [0] * NH
                for hh in range(min(2, NH)):
                    b.op("gpsimd", "dma_start",
                         wsb[:, (hh % 2) * 6144:(hh % 2 + 1) * 6144],
                         wqkv_d[hh, :, :], inc=sW, incn=16)
                    w_ld[hh] = sW.n

                grp = [0]
                pkv_free = [0, 0]
                o_evac = {}
                qkTd_done = {}
                qd_done = {}
                kv_bf_done = {}
                scalar_head_gate = [False] * NH

                def emit_proj_group(hh, part, slab, xn1_wait=None):
                    hp = hh % 2
                    pi = grp[0] % 2
                    grp[0] += 1
                    if part == 0 and slab == 0:
                        b.w("tensor", sW, w_ld[hh])
                    if xn1_wait is not None:
                        b.w("tensor", sV, xn1_wait)
                    b.w("tensor", sS, pp_free_s[pi])
                    b.w("tensor", sV, pp_free_v[pi])
                    for fc in range(FC):
                        b.op("tensor", "matmul", pp[pi][:, :],
                             wsb[:, hp * 6144 + (part * FC + fc) * 128:
                                 hp * 6144 + (part * FC + fc + 1) * 128],
                             xn1[:, fc * NT + slab * 512: fc * NT + (slab + 1) * 512],
                             start=(fc == 0), stop=(fc == FC - 1),
                             skip_group_check=True,
                             inc=(sT if fc == FC - 1 else None))
                    mm_done = sT.n
                    if not scalar_head_gate[hh]:
                        scalar_head_gate[hh] = True
                        if hh >= 2:
                            b.w("scalar", sT, attn_mm_done[hh - 2])
                            b.w("scalar", sV, qread_done[hh - 2])
                    b.w("scalar", sT, mm_done)
                    dst = (q_sb, k_sb, v_sb)[part]
                    b.op("scalar", "activation",
                         dst[:, hp * NT + slab * 512: hp * NT + (slab + 1) * 512],
                         pp[pi][:, :], AF.Silu, inc=sS)
                    pp_free_s[pi] = sS.n
                    if part == 1 and slab == 3:
                        ev_k_done[hh] = sS.n
                    if part == 2 and slab == 3:
                        ev_v_done[hh] = sS.n
                        ev_all_done[hh] = sS.n
                    if part == 2 and slab == 3:
                        proj_done[hh] = sT.n

                x_free = [0, 0]
                xn_free = [0, 0]
                for tcp in range(TC // 2):
                    tcs = (2 * tcp, 2 * tcp + 1)
                    lds = [0, 0]
                    for i, tc in enumerate(tcs):
                        p = tc % 2
                        b.w("sync", sW, consts_ld0)
                        b.w("sync", sV, x_free[p])
                        b.op("sync", "dma_start", xA[:, p * D:(p + 1) * D],
                             x_d[tc * 128:(tc + 1) * 128, :], inc=sA, incn=16)
                        lds[i] = sA.n
                    for i, tc in enumerate(tcs):
                        p = tc % 2
                        b.w("vector", sA, lds[i])
                        for j in range(4):
                            b.op("vector", "bn_stats",
                                 stat[:, p * 32 + j * 6: p * 32 + (j + 1) * 6],
                                 xA[:, p * D + j * 512: p * D + (j + 1) * 512],
                                 inc=(sV if j == 3 else None))
                    b.w("vector", sV, sV.n)
                    for tc in tcs:
                        p = tc % 2
                        b.op("vector", "bn_aggr", stat[:, p * 32 + 24: p * 32 + 26],
                             stat[:, p * 32: p * 32 + 24], inc=(sV if p == 1 else None))
                    var_done = sV.n
                    b.w("scalar", sV, var_done)
                    for tc in tcs:
                        p = tc % 2
                        b.op("scalar", "activation", stat[:, p * 32 + 26: p * 32 + 27],
                             stat[:, p * 32 + 25: p * 32 + 26], AF.Sqrt,
                             bias=cst[:, C_EPS_LN:C_EPS_LN + 1],
                             inc=(sS if p == 1 else None))
                    std_done = sS.n
                    b.w("vector", sS, std_done)
                    for tc in tcs:
                        p = tc % 2
                        b.op("vector", "reciprocal", stat[:, p * 32 + 27: p * 32 + 28],
                             stat[:, p * 32 + 26: p * 32 + 27],
                             inc=(sV if p == 1 else None))
                    b.w("vector", sV, sV.n)
                    for tc in tcs:
                        p = tc % 2
                        b.w("vector", sT, xn_free[p])
                        b.op("vector", "tensor_scalar", xntm[:, p * D:(p + 1) * D],
                             xA[:, p * D:(p + 1) * D], stat[:, p * 32 + 24: p * 32 + 25],
                             stat[:, p * 32 + 27: p * 32 + 28], ALU.subtract, ALU.mult,
                             inc=sV)
                        if tc == tcs[0]:
                            xn_done_a = sV.n
                        else:
                            xn_done_b = sV.n
                        x_free[p] = sV.n
                    for i, tc in enumerate(tcs):
                        p = tc % 2
                        b.w("tensor", sV, xn_done_a if i == 0 else xn_done_b)
                        xn1_r = xn1[:, :].rearrange("q (c t) -> q c t", c=FC)
                        for pj in range(FC // 2):
                            r = pj % 4
                            b.w("tensor", sV, tr4_free[r])
                            for half in range(2):
                                fc = 2 * pj + half
                                b.op("tensor", "matmul",
                                     tr4[r][:, half * 128:(half + 1) * 128],
                                     xntm[:, p * D + fc * 128: p * D + (fc + 1) * 128],
                                     ident_s[:, :], start=True, stop=True,
                                     skip_group_check=True,
                                     inc=(sT if half == 1 else None))
                            mm_done = sT.n
                            b.w("vector", sT, mm_done)
                            b.op("vector", "tensor_copy",
                                 xn1_r[:, 2 * pj:2 * pj + 2, tc * 128:(tc + 1) * 128],
                                 tr4[r][:, 0:256].rearrange("q (c t) -> q c t", c=2),
                                 inc=sV)
                            tr4_free[r] = sV.n
                        xn_free[p] = sT.n
                    if tcs[1] % 4 == 3:
                        sl = tcs[1] // 4
                        xn1_slab = sV.n
                        for part in range(3):
                            emit_proj_group(0, part, sl, xn1_wait=xn1_slab)
                xn1_done_v = sV.n
                xn1_done_s = sS.n
                qk_free0 = [tr4_free[0], tr4_free[1]]
                po_free0 = [tr4_free[2], tr4_free[3]]

                qk_free = list(qk_free0)
                po_free = list(po_free0)

                def emit_transposes(hh):
                    hp = hh % 2
                    if hh >= 2:
                        b.w("vector", sT, attn_mm_done[hh - 2])
                    b.w("tensor", sS, ev_k_done[hh])
                    for j in range(2 * NB):
                        r = j % 2
                        b.w("tensor", sV, po_free[r])
                        b.op("tensor", "matmul", pop[r][:, 0:128],
                             k_sb[:, hp * NT + j * 128: hp * NT + (j + 1) * 128],
                             ident_s[:, :], start=True, stop=True,
                             skip_group_check=True, inc=sT)
                        mm_done = sT.n
                        sub = j % 2
                        b.w("vector", sT, mm_done)
                        b.op("vector", "tensor_scalar",
                             kkT[:, hp * NT + j * 128: hp * NT + (j + 1) * 128],
                             pop[r][:, 0:128], kdec[:, 2 * hh + sub:2 * hh + sub + 1],
                             None, ALU.mult, inc=sV)
                        po_free[r] = sV.n
                    b.w("tensor", sS, ev_v_done[hh])
                    for pj in range(NB):
                        r = pj % 2
                        b.w("tensor", sV, po_free[r])
                        for half in range(2):
                            j = 2 * pj + half
                            b.op("tensor", "matmul",
                                 pop[r][:, half * 128:(half + 1) * 128],
                                 v_sb[:, hp * NT + j * 128: hp * NT + (j + 1) * 128],
                                 ident_s[:, :], start=True, stop=True,
                                 skip_group_check=True,
                                 inc=(sT if half == 1 else None))
                        mm_done = sT.n
                        b.w("vector", sT, mm_done)
                        b.op("vector", "tensor_copy",
                             vT[:, hp * NT + pj * 256: hp * NT + (pj + 1) * 256],
                             pop[r][:, 0:256], inc=sV)
                        po_free[r] = sV.n

                emit_transposes(0)

                for h in range(NH):
                    hp = h % 2
                    qkT_done = {}
                    o_done = {}
                    kvmm_done = {}
                    groups = ([(part, slab) for part in range(3) for slab in range(4)]
                              if h + 1 < NH else [])
                    gi = [0]

                    def drain_groups(target):
                        while gi[0] < min(target, len(groups)):
                            part, slab = groups[gi[0]]
                            emit_proj_group(h + 1, part, slab)
                            gi[0] += 1

                    def emit_qkT(bb, h=h, hp=hp, qkT_done=qkT_done):
                        pi = bb % 2
                        b.w("tensor", sV, qk_free[pi])
                        for sub in range(2):
                            b.op("tensor", "matmul",
                                 qkp[pi][:, sub * BLK:(sub + 1) * BLK],
                                 k_sb[:, hp * NT + bb * BLK + sub * 128:
                                      hp * NT + bb * BLK + (sub + 1) * 128],
                                 q_sb[:, hp * NT + bb * BLK: hp * NT + (bb + 1) * BLK],
                                 start=True, stop=True, skip_group_check=True,
                                 inc=(sT if sub == 1 else None))
                        qkT_done[bb] = sT.n

                    emit_qkT(0)
                    emit_qkT(1)
                    b.w("vector", sS, ev_all_done[h])
                    b.op("vector", "tensor_mul", qd_sb[:, 0:BLK],
                         q_sb[:, hp * NT: hp * NT + BLK],
                         qdec[:, h * BLK:(h + 1) * BLK], inc=sV)
                    qd_done[(h, 0)] = sV.n
                    b.w("vector", sT, qkT_done[0])
                    b.op("vector", "tensor_mul", qkTd[:, 0:512],
                         qkp[0][:, :], ddt[:, h * 2 * BLK:(h + 1) * 2 * BLK], inc=sV)
                    qkTd_done[(h, 0)] = sV.n
                    qk_free[0] = sV.n

                    for bb in range(NB):
                        pi = bb % 2
                        b.w("tensor", sV, po_free[pi])
                        b.w("tensor", sV, qkTd_done[(h, bb)])
                        if bb > 0:
                            b.w("tensor", sV, kv_bf_done[(h, bb - 1)])
                            b.w("tensor", sV, qd_done[(h, bb)])
                            b.op("tensor", "matmul", pop[pi][:, :], kv_bf[:, :],
                                 qd_sb[:, pi * BLK:(pi + 1) * BLK], start=True,
                                 stop=False, skip_group_check=True)
                        for sub in range(2):
                            b.op("tensor", "matmul", pop[pi][:, :],
                                 vT[:, hp * NT + (2 * bb + sub) * 128:
                                     hp * NT + (2 * bb + sub + 1) * 128],
                                 qkTd[:, pi * 512 + sub * BLK: pi * 512 + (sub + 1) * BLK],
                                 start=(bb == 0 and sub == 0), stop=(sub == 1),
                                 skip_group_check=True, inc=(sT if sub == 1 else None))
                        o_done[bb] = sT.n
                        b.w("tensor", sV, pkv_free[pi])
                        for sub in range(2):
                            b.op("tensor", "matmul", pkvp[pi][:, :],
                                 kkT[:, hp * NT + (2 * bb + sub) * 128:
                                     hp * NT + (2 * bb + sub + 1) * 128],
                                 vT[:, hp * NT + (2 * bb + sub) * 128:
                                     hp * NT + (2 * bb + sub + 1) * 128],
                                 start=(sub == 0), stop=(sub == 1),
                                 skip_group_check=True, inc=(sT if sub == 1 else None))
                        kvmm_done[bb] = sT.n
                        drain_groups(((bb + 1) * 12 + 7) // 8)
                        b.w("vector", sT, kvmm_done[bb])
                        if bb == 0:
                            b.op("vector", "tensor_copy", kv_bf[:, :], pkvp[pi][:, :],
                                 inc=sV)
                            kv_bf_done[(h, bb)] = sV.n
                            b.op("vector", "tensor_copy", kv_sb[:, :], pkvp[pi][:, :],
                                 inc=sV)
                        else:
                            b.op("vector", "scalar_tensor_tensor", kv_bf[:, :],
                                 kv_sb[:, :], bdall[:, h:h + 1], pkvp[pi][:, :],
                                 ALU.mult, ALU.add, inc=sV)
                            kv_bf_done[(h, bb)] = sV.n
                            b.op("vector", "scalar_tensor_tensor", kv_sb[:, :],
                                 kv_sb[:, :], bdall[:, h:h + 1], pkvp[pi][:, :],
                                 ALU.mult, ALU.add, inc=sV)
                        pkv_free[pi] = sV.n
                        if h >= 2 and bb == 0:
                            b.w("vector", sO, o_store[h - 2])
                        b.w("vector", sT, o_done[bb])
                        b.op("vector", "tensor_copy",
                             o_sb[:, hp * NT + bb * BLK: hp * NT + (bb + 1) * BLK],
                             pop[pi][:, :], inc=sV)
                        o_evac[(h, bb)] = sV.n
                        po_free[pi] = sV.n
                        if bb + 1 < NB:
                            b.w("vector", sT, o_done.get(bb - 1, 0))
                            b.op("vector", "tensor_mul",
                                 qd_sb[:, ((bb + 1) % 2) * BLK:((bb + 1) % 2 + 1) * BLK],
                                 q_sb[:, hp * NT + (bb + 1) * BLK:
                                      hp * NT + (bb + 2) * BLK],
                                 qdec[:, h * BLK:(h + 1) * BLK], inc=sV)
                            qd_done[(h, bb + 1)] = sV.n
                        if bb + 1 < NB:
                            pj = (bb + 1) % 2
                            b.w("vector", sT, qkT_done[bb + 1])
                            b.op("vector", "tensor_mul",
                                 qkTd[:, pj * 512:(pj + 1) * 512], qkp[pj][:, :],
                                 ddt[:, h * 2 * BLK:(h + 1) * 2 * BLK], inc=sV)
                            qkTd_done[(h, bb + 1)] = sV.n
                            qk_free[pj] = sV.n
                        if bb + 2 < NB:
                            emit_qkT(bb + 2)
                    attn_mm_done[h] = sT.n
                    qread_done[h] = qd_done[(h, NB - 1)]
                    drain_groups(len(groups))
                    b.w("sync", sV, o_evac[(h, NB - 1)])
                    b.op("sync", "dma_start", o_scr[h, :, :],
                         o_sb[:, hp * NT:(hp + 1) * NT], inc=sO, incn=16)
                    b.w("scalar", sV, o_evac[(h, NB - 1)])
                    if h >= 1:
                        b.w("scalar", sO, o2_store[h - 1])
                    b.op("scalar", "activation", o2st[:, :],
                         o_sb[:, hp * NT:(hp + 1) * NT], AF.Square, inc=sS)
                    o_sq_done[h] = sS.n
                    b.w("sync", sS, o_sq_done[h])
                    b.op("sync", "dma_start", o2_scr[h, :, :],
                         o2st[:, :], inc=sO, incn=16)
                    o2_store[h] = sO.n
                    o_store[h] = sO.n
                    if h + 1 < NH:
                        emit_transposes(h + 1)
                    if h + 2 < NH:
                        b.w("gpsimd", sT, proj_done[h])
                        b.op("gpsimd", "dma_start",
                             wsb[:, ((h + 2) % 2) * 6144:((h + 2) % 2 + 1) * 6144],
                             wqkv_d[h + 2, :, :], inc=sW, incn=16)
                        w_ld[h + 2] = sW.n
                    if h == NH - 2:
                        b.w("gpsimd", sT, proj_done[NH - 2])
                        for wo in range(2):
                            b.op("gpsimd", "dma_start",
                                 wsb[:, ((NH - 2) % 2) * 6144 + wo * 2048:
                                     ((NH - 2) % 2) * 6144 + (wo + 1) * 2048],
                                 wgate_d[wo, :, :], inc=sW, incn=16)
                        wg_pre = sW.n

            wsb_g.__exit__(None, None, None)
            if dbg:
                b.w("sync", sV, xn1_done_v)
                b.w("sync", sS, xn1_done_s)
                b.op("sync", "dma_start", xn1_dbg[:, :], xn1[:, :], inc=sO, incn=16)
            barrier()
            woutsb_g = nc.sbuf_tensor("woutsb", [128, FC * D], BF16, side="right")
            woutsb = woutsb_g.__enter__()
            go_t_g = nc.sbuf_tensor("go_t", [128, 2 * FC * 256], BF16, side="right")
            go_t = go_t_g.__enter__()

            with contextlib.ExitStack() as esB:
                ec = esB.enter_context
                o_in = ec(nc.sbuf_tensor("o_in", [128, 2 * NT], BF16))
                o2_sb = ec(nc.sbuf_tensor("o2_sb", [128, 2 * NT], BF16))
                wgsb = ec(nc.sbuf_tensor("wgsb", [128, 2 * D], BF16))
                gatef = ec(nc.sbuf_tensor("gatef", [128, 2 * 512], F32))
                gotmp = ec(nc.sbuf_tensor("gotmp", [128, 512], F32))
                go_sb = ec(nc.sbuf_tensor("go_sb", [128, 2 * NT], BF16))
                istd_b = ec(nc.sbuf_tensor("istd_b", [128, NT], F32))
                psR0 = ec(nc.psum_tensor("psR0", [128, 512], F32))
                psR1 = ec(nc.psum_tensor("psR1", [128, 512], F32))
                psR2 = ec(nc.psum_tensor("psR2", [128, 512], F32))
                psR3 = ec(nc.psum_tensor("psR3", [128, 512], F32))
                psG0 = ec(nc.psum_tensor("psG0", [128, 512], F32))
                psG1 = ec(nc.psum_tensor("psG1", [128, 512], F32))
                psR = (psR0, psR1, psR2, psR3)
                psG = (psG0, psG1)

                sq_free_s = [0, 0]
                sq_free_t = [0, 0]
                o_lds = [0] * NH
                for oc in range(min(2, NH)):
                    b.op("sync", "dma_start", o2_sb[:, (oc % 2) * NT:(oc % 2 + 1) * NT],
                         o2_scr[oc, :, :], inc=sA, incn=16)
                    o_lds[oc] = sA.n
                for oc in range(NH):
                    p = oc % 2
                    b.w("tensor", sA, o_lds[oc])
                    for slab in range(4):
                        b.op("tensor", "matmul", psR[slab][:, :], ones_s[:, :],
                             o2_sb[:, p * NT + slab * 512: p * NT + (slab + 1) * 512],
                             start=(oc == 0), stop=(oc == NH - 1),
                             skip_group_check=True, inc=(sT if slab == 3 else None))
                    sq_free_t[p] = sT.n
                    if oc + 2 < NH:
                        b.w("sync", sT, sq_free_t[p])
                        b.op("sync", "dma_start", o2_sb[:, p * NT:(p + 1) * NT],
                             o2_scr[oc + 2, :, :], inc=sA, incn=16)
                        o_lds[oc + 2] = sA.n
                rms_done = sT.n
                b.w("scalar", sT, rms_done)
                for slab in range(4):
                    b.op("scalar", "activation", istd_b[:, slab * 512:(slab + 1) * 512],
                         psR[slab][:, :], AF.Sqrt, scale=cst[:, C_INVD:C_INVD + 1],
                         bias=cst[:, C_EPS_RMS:C_EPS_RMS + 1],
                         inc=(sS if slab == 3 else None))
                sqrt_done = sS.n
                b.w("vector", sS, sqrt_done)
                b.op("vector", "reciprocal", istd_b[:, :], istd_b[:, :], inc=sV)
                istd_done = sV.n
                b.w("vector", sV, istd_done)

                wg_ld = [wg_pre, wg_pre]
                b.op("gpsimd", "dma_start", woutsb[:, 0:D], wout_d[0, :, :],
                     inc=sW, incn=16)
                go_done = [0] * FC
                go_store = [0] * FC
                gate_mm_done = [0] * FC
                sig_free = [0, 0]
                gate_read = [0, 0]
                o_lds2 = [0] * FC
                for oc in range(min(2, FC)):
                    b.w("sync", sS, sq_free_s[oc % 2])
                    b.op("sync", "dma_start", o_in[:, (oc % 2) * NT:(oc % 2 + 1) * NT],
                         o_scr[oc, :, :], inc=sA, incn=16)
                    o_lds2[oc] = sA.n
                wsb_base = ((NH - 2) % 2) * 6144
                for oc in range(FC):
                    p = oc % 2
                    o_ld2 = o_lds2[oc]
                    b.w("tensor", sW, wg_ld[p])
                    wg_src = (wsb[:, wsb_base + oc * 2048: wsb_base + (oc + 1) * 2048]
                              if oc < 2 else wgsb[:, p * D:(p + 1) * D])
                    for slab in range(4):
                        gp = slab % 2
                        b.w("tensor", sS, sig_free[gp])
                        for fc in range(FC):
                            b.op("tensor", "matmul", psG[gp][:, :],
                                 wg_src[:, fc * 128:(fc + 1) * 128],
                                 xn1[:, fc * NT + slab * 512: fc * NT + (slab + 1) * 512],
                                 start=(fc == 0), stop=(fc == FC - 1),
                                 skip_group_check=True, inc=(sT if fc == FC - 1 else None))
                        g_done = sT.n
                        b.w("scalar", sT, g_done)
                        b.w("scalar", sV, gate_read[gp])
                        b.op("scalar", "activation", gatef[:, gp * 512:(gp + 1) * 512],
                             psG[gp][:, :], AF.Sigmoid, inc=sS)
                        sig_done = sS.n
                        sig_free[gp] = sS.n
                        b.w("vector", sS, sig_done)
                        if slab == 0:
                            b.w("vector", sA, o_ld2)
                            if oc == 0:
                                b.w("vector", sV, istd_done)
                            if oc >= 2:
                                b.w("vector", sO, go_store[oc - 2])
                        b.op("vector", "scalar_tensor_tensor", gotmp[:, :],
                             gatef[:, gp * 512:(gp + 1) * 512], nw_s[:, oc:oc + 1],
                             o_in[:, p * NT + slab * 512: p * NT + (slab + 1) * 512],
                             ALU.mult, ALU.mult)
                        b.op("vector", "tensor_mul",
                             go_sb[:, p * NT + slab * 512: p * NT + (slab + 1) * 512],
                             gotmp[:, :], istd_b[:, slab * 512:(slab + 1) * 512], inc=sV)
                        gate_read[gp] = sV.n
                    gate_mm_done[oc] = sT.n
                    go_done[oc] = sV.n
                    if oc + 2 < FC:
                        b.w("sync", sV, go_done[oc])
                        b.op("sync", "dma_start", o_in[:, p * NT:(p + 1) * NT],
                             o_scr[oc + 2, :, :], inc=sA, incn=16)
                        o_lds2[oc + 2] = sA.n
                    if oc + 2 < FC:
                        b.w("gpsimd", sT, gate_mm_done[oc])
                        b.op("gpsimd", "dma_start", wgsb[:, p * D:(p + 1) * D],
                             wgate_d[oc + 2, :, :], inc=sW, incn=16)
                        wg_ld[p] = sW.n
                    for wo in (2 * oc + 1, 2 * oc + 2):
                        if wo < FC:
                            if wo == 10:
                                b.w("gpsimd", sT, gate_mm_done[1])
                            b.op("gpsimd", "dma_start",
                                 woutsb[:, wo * D:(wo + 1) * D],
                                 wout_d[wo, :, :], inc=sW, incn=16)
                    b.w("sync", sV, go_done[oc])
                    b.op("sync", "dma_start", go_scr[oc, :, :],
                         go_sb[:, p * NT:(p + 1) * NT], inc=sO, incn=16)
                    go_store[oc] = sO.n
                wout_ld = sW.n
                if dbg:
                    b.w("sync", sV, istd_done)
                    b.op("sync", "dma_start", istd_dbg[:, :], istd_b[:, :], inc=sO, incn=16)

            barrier(skips={"tensor": {"sO", "sA"}})

        with nc.sbuf_tensor("xn2", [128, FC * NT], BF16) as xn2:
            with contextlib.ExitStack() as esB2:
                ec = esB2.enter_context

                xB = ec(nc.sbuf_tensor("xB", [128, 2 * D], F32))
                x1sb = ec(nc.sbuf_tensor("x1sb", [128, 2 * D], F32))
                xn2tm = ec(nc.sbuf_tensor("xn2tm", [128, 2 * D], BF16))
                sqd2 = ec(nc.sbuf_tensor("sqd2", [128, D], BF16))
                stat2 = ec(nc.sbuf_tensor("stat2", [128, 64], F32))
                psO0 = ec(nc.psum_tensor("psO0", [128, 512], F32))
                psO1 = ec(nc.psum_tensor("psO1", [128, 512], F32))
                psO2 = ec(nc.psum_tensor("psO2", [128, 512], F32))
                psO3 = ec(nc.psum_tensor("psO3", [128, 512], F32))
                psT0 = ec(nc.psum_tensor("psT0", [128, 512], F32))
                psT1 = ec(nc.psum_tensor("psT1", [128, 512], F32))
                psT2 = ec(nc.psum_tensor("psT2", [128, 512], F32))
                psT3 = ec(nc.psum_tensor("psT3", [128, 512], F32))
                psO = (psO0, psO1, psO2, psO3)
                psT = (psT0, psT1, psT2, psT3)

                outproj_done = [0] * TC
                x1_evac = [0] * TC
                x1_store = [0] * TC
                xn2tm_done = [0] * TC
                ln2_tr_done = [0] * TC
                go_ld = [0] * TC
                x1_half_evac = [[0] * TC, [0] * TC]
                for pr0 in range(2):
                    b.w("gpsimd", sO, go_store[2 * pr0 + 1])
                    b.op("gpsimd", "dma_start",
                         go_t[:, pr0 * FC * 256:(pr0 + 1) * FC * 256].rearrange(
                             "q (c j) -> q c j", c=FC),
                         go_scr[:, :, pr0 * 256:(pr0 + 1) * 256].rearrange(
                             "c q j -> q c j"),
                         inc=sW, incn=16)
                    go_ld[2 * pr0] = go_ld[2 * pr0 + 1] = sW.n
                sq_done_l = [0] * TC
                psT_free = [0, 0, 0, 0]
                xn2_r = xn2[:, :].rearrange("q (c t) -> q c t", c=FC)

                def emit_ln2_chain(tc):
                    p = tc % 2
                    for i in range(4):
                        b.op("vector", "bn_stats",
                             stat2[:, p * 32 + i * 6: p * 32 + (i + 1) * 6],
                             x1sb[:, p * D + i * 512: p * D + (i + 1) * 512],
                             inc=(sV if i == 3 else None))
                    b.w("vector", sV, sV.n)
                    b.op("vector", "bn_aggr", stat2[:, p * 32 + 24: p * 32 + 26],
                         stat2[:, p * 32: p * 32 + 24], inc=sV)
                    var_done = sV.n
                    sq_done_l[tc] = 0
                    b.w("scalar", sV, var_done)
                    b.op("scalar", "activation", stat2[:, p * 32 + 26: p * 32 + 27],
                         stat2[:, p * 32 + 25: p * 32 + 26], AF.Sqrt,
                         bias=cst[:, C_EPS_LN:C_EPS_LN + 1], inc=sS)
                    std_done = sS.n
                    b.w("vector", sS, std_done)
                    b.op("vector", "reciprocal", stat2[:, p * 32 + 27: p * 32 + 28],
                         stat2[:, p * 32 + 26: p * 32 + 27], inc=sV)
                    b.w("vector", sV, sV.n)
                    if tc >= 2:
                        b.w("vector", sT, ln2_tr_done[tc - 2])
                    b.op("vector", "tensor_scalar", xn2tm[:, p * D:(p + 1) * D],
                         x1sb[:, p * D:(p + 1) * D], stat2[:, p * 32 + 24: p * 32 + 25],
                         stat2[:, p * 32 + 27: p * 32 + 28], ALU.subtract, ALU.mult, inc=sV)
                    xn2tm_done[tc] = sV.n

                def emit_ln2_transposes(tc):
                    p = tc % 2
                    b.w("tensor", sV, xn2tm_done[tc])
                    for pj in range(FC // 4):
                        r = pj % 4
                        b.w("tensor", sV, psT_free[r])
                        for half in range(4):
                            fc = 4 * pj + half
                            b.op("tensor", "matmul", psT[r][:, half * 128:(half + 1) * 128],
                                 xn2tm[:, p * D + fc * 128: p * D + (fc + 1) * 128],
                                 ident_s[:, :], start=True, stop=True,
                                 skip_group_check=True, inc=(sT if half == 3 else None))
                        mm_done = sT.n
                        b.w("vector", sT, mm_done)
                        b.op("vector", "tensor_copy",
                             xn2_r[:, 4 * pj:4 * pj + 4, tc * 128:(tc + 1) * 128],
                             psT[r][:, 0:512].rearrange("q (c t) -> q c t", c=4), inc=sV)
                        psT_free[r] = sV.n
                    ln2_tr_done[tc] = sT.n

                for tc in range(TC):
                    p = tc % 2
                    pr = tc // 2
                    if tc % 2 == 0 and pr >= 2:
                        b.w("gpsimd", sT, outproj_done[2 * (pr - 2) + 1])
                        b.op("gpsimd", "dma_start",
                             go_t[:, (pr % 2) * FC * 256:(pr % 2 + 1) * FC * 256].rearrange(
                                 "q (c j) -> q c j", c=FC),
                             go_scr[:, :, tc * 128:(tc + 2) * 128].rearrange(
                                 "c q j -> q c j"),
                             inc=sW, incn=16)
                        go_ld[tc] = go_ld[tc + 1] = sW.n
                    b.w("sync", sV, x1_evac[tc - 2] if tc >= 2 else 0)
                    b.op("sync", "dma_start", xB[:, p * D:(p + 1) * D],
                         x_d[tc * 128:(tc + 1) * 128, :], inc=sA, incn=16)
                    x_ld = sA.n
                    b.w("tensor", sW, go_ld[tc])
                    if tc == 0:
                        b.w("tensor", sW, wout_ld)
                    gbase = (pr % 2) * FC * 256 + (tc % 2) * 128
                    half_done = [0, 0]
                    for sp in range(2):
                        b.w("tensor", sV, x1_half_evac[sp][tc - 1] if tc >= 1 else 0)
                        for oc in range(FC):
                            for slab in (2 * sp, 2 * sp + 1):
                                b.op("tensor", "matmul", psO[slab][:, :],
                                     go_t[:, gbase + oc * 256: gbase + oc * 256 + 128],
                                     woutsb[:, oc * D + slab * 512: oc * D + (slab + 1) * 512],
                                     start=(oc == 0), stop=(oc == FC - 1),
                                     skip_group_check=True,
                                     inc=(sT if (oc == FC - 1 and slab == 2 * sp + 1) else None))
                        half_done[sp] = sT.n
                        b.w("vector", sT, half_done[sp])
                        if sp == 0:
                            b.w("vector", sA, x_ld)
                            if tc >= 2:
                                b.w("vector", sO, x1_store[tc - 2])
                        for slab in (2 * sp, 2 * sp + 1):
                            b.op("vector", "tensor_add",
                                 x1sb[:, p * D + slab * 512: p * D + (slab + 1) * 512],
                                 psO[slab][:, :],
                                 xB[:, p * D + slab * 512: p * D + (slab + 1) * 512],
                                 inc=(sV if slab == 2 * sp + 1 else None))
                        x1_half_evac[sp][tc] = sV.n
                    outproj_done[tc] = sT.n
                    x1_evac[tc] = sV.n
                    b.w("sync", sV, x1_evac[tc])
                    b.op("sync", "dma_start", x1_scr[tc * 128:(tc + 1) * 128, :],
                         x1sb[:, p * D:(p + 1) * D], inc=sO, incn=16)
                    x1_store[tc] = sO.n
                    if tc >= 2:
                        emit_ln2_transposes(tc - 2)
                    if tc >= 1:
                        emit_ln2_chain(tc - 1)
                emit_ln2_chain(TC - 1)
                emit_ln2_transposes(TC - 2)
                emit_ln2_transposes(TC - 1)
                xn2_done_v = sV.n
                if dbg:
                    b.w("sync", sV, xn2_done_v)
                    b.op("sync", "dma_start", xn2_dbg[:, :], xn2[:, :], inc=sO, incn=16)

            go_t_g.__exit__(None, None, None)
            woutsb_g.__exit__(None, None, None)
            barrier()
            wdsb_g = nc.sbuf_tensor("wdsb", [128, 2 * ICN * 512], BF16, side="right")
            wdsb = wdsb_g.__enter__()

            with contextlib.ExitStack() as esC:
                ec = esC.enter_context
                wgt = ec(nc.sbuf_tensor("wgt", [128, 2 * D], BF16))
                wut = ec(nc.sbuf_tensor("wut", [128, 2 * D], BF16))
                tg = ec(nc.sbuf_tensor("tg", [128, 2 * 512], F32))
                h_sb = ec(nc.sbuf_tensor("h_sb", [128, 2 * NT], BF16))
                psg0 = ec(nc.psum_tensor("psg0", [128, 512], F32))
                psg1 = ec(nc.psum_tensor("psg1", [128, 512], F32))
                psu0 = ec(nc.psum_tensor("psu0", [128, 512], F32))
                psu1 = ec(nc.psum_tensor("psu1", [128, 512], F32))
                psg = (psg0, psg1)
                psu = (psu0, psu1)
                wld = [0, 0]
                for ic in range(min(2, ICN)):
                    b.op("gpsimd", "dma_start", wgt[:, (ic % 2) * D:(ic % 2 + 1) * D],
                         wg_d[ic, :, :], inc=sW, incn=16)
                    b.op("gpsimd", "dma_start", wut[:, (ic % 2) * D:(ic % 2 + 1) * D],
                         wu_d[ic, :, :], inc=sW, incn=16)
                    wld[ic % 2] = sW.n
                gu_mm_done = [0] * ICN
                h_done = [0] * ICN
                h_store = [0] * ICN
                tg_free = [0, 0]
                psg_free = [0, 0]
                psu_free = [0, 0]
                for ic in range(ICN):
                    p = ic % 2
                    b.w("tensor", sW, wld[p])
                    if ic == 0:
                        b.w("tensor", sV, xn2_done_v)
                    for slab in range(4):
                        gp = slab % 2
                        b.w("tensor", sS, psg_free[gp])
                        for fc in range(FC):
                            b.op("tensor", "matmul", psg[gp][:, :],
                                 wgt[:, p * D + fc * 128: p * D + (fc + 1) * 128],
                                 xn2[:, fc * NT + slab * 512: fc * NT + (slab + 1) * 512],
                                 start=(fc == 0), stop=(fc == FC - 1),
                                 skip_group_check=True, inc=(sT if fc == FC - 1 else None))
                        g_mm = sT.n
                        b.w("tensor", sV, psu_free[gp])
                        for fc in range(FC):
                            b.op("tensor", "matmul", psu[gp][:, :],
                                 wut[:, p * D + fc * 128: p * D + (fc + 1) * 128],
                                 xn2[:, fc * NT + slab * 512: fc * NT + (slab + 1) * 512],
                                 start=(fc == 0), stop=(fc == FC - 1),
                                 skip_group_check=True, inc=(sT if fc == FC - 1 else None))
                        u_mm = sT.n
                        b.w("scalar", sT, g_mm)
                        b.w("scalar", sV, tg_free[gp])
                        b.op("scalar", "activation", tg[:, gp * 512:(gp + 1) * 512],
                             psg[gp][:, :], AF.Silu, scale=gs_s[:, ic:ic + 1], inc=sS)
                        t_done = sS.n
                        psg_free[gp] = sS.n
                        b.w("vector", sT, u_mm)
                        b.w("vector", sS, t_done)
                        if slab == 0 and ic >= 2:
                            b.w("vector", sO, h_store[ic - 2])
                        b.op("vector", "scalar_tensor_tensor",
                             h_sb[:, p * NT + slab * 512: p * NT + (slab + 1) * 512],
                             psu[gp][:, :], us_s[:, ic:ic + 1],
                             tg[:, gp * 512:(gp + 1) * 512], ALU.mult, ALU.mult, inc=sV)
                        tg_free[gp] = sV.n
                        psu_free[gp] = sV.n
                    gu_mm_done[ic] = sT.n
                    h_done[ic] = sV.n
                    if ic + 2 < ICN:
                        b.w("gpsimd", sT, gu_mm_done[ic])
                        b.op("gpsimd", "dma_start", wgt[:, p * D:(p + 1) * D],
                             wg_d[ic + 2, :, :], inc=sW, incn=16)
                        b.op("gpsimd", "dma_start", wut[:, p * D:(p + 1) * D],
                             wu_d[ic + 2, :, :], inc=sW, incn=16)
                        wld[p] = sW.n
                    if ic in (2, 4, 6, 8):
                        jj = (ic - 2) // 2
                        qq, piece = jj // 2, jj % 2
                        ic0, ic1 = piece * 22, (piece + 1) * 22
                        base = (qq % 2) * ICN * 512
                        b.op("gpsimd", "dma_start",
                             wdsb[:, base + ic0 * 512: base + ic1 * 512].rearrange(
                                 "q (c j) -> q c j", j=512),
                             wd_d[ic0:ic1, :, qq * 512:(qq + 1) * 512].rearrange(
                                 "c q j -> q c j"),
                             inc=sW, incn=16)
                    b.w("sync", sV, h_done[ic])
                    b.op("sync", "dma_start", h_scr[ic, :, :],
                         h_sb[:, p * NT:(p + 1) * NT], inc=sO, incn=16)
                    h_store[ic] = sO.n

            barrier()

        with contextlib.ExitStack() as esD:
            ec = esD.enter_context
            HTP = ICN * 256
            h_t = ec(nc.sbuf_tensor("h_t", [128, 4 * HTP], BF16))
            x1t = ec(nc.sbuf_tensor("x1t", [128, 2 * 512], F32))
            outsb = ec(nc.sbuf_tensor("outsb", [128, 2 * 512], F32))
            dtmp = ec(nc.sbuf_tensor("dtmp", [128, 512], F32))
            psD0 = ec(nc.psum_tensor("psD0", [128, 512], F32))
            psD1 = ec(nc.psum_tensor("psD1", [128, 512], F32))
            psD = (psD0, psD1)
            out_evac = {}
            out_store = {}
            d_mm_done = {}
            h_ld = {}
            wdq_ld = [0, 0]
            NQ = 4
            for q in range(NQ):
                if q >= 2:
                    b.w("gpsimd", sT, d_mm_done[(q - 2, TC - 1)])
                    base = (q % 2) * ICN * 512
                    for piece in range(2):
                        ic0, ic1 = piece * 22, (piece + 1) * 22
                        b.op("gpsimd", "dma_start",
                             wdsb[:, base + ic0 * 512: base + ic1 * 512].rearrange(
                                 "q (c j) -> q c j", j=512),
                             wd_d[ic0:ic1, :, q * 512:(q + 1) * 512].rearrange(
                                 "c q j -> q c j"),
                             inc=sW, incn=16)
                    wdq_ld[q % 2] = sW.n
                for tc in range(TC):
                    p = tc % 2
                    pr = tc // 2
                    pd = psD[p]
                    if tc >= 2:
                        key_prev = (q, tc - 2)
                    elif q >= 1:
                        key_prev = (q - 1, TC + tc - 2)
                    else:
                        key_prev = None
                    if tc % 2 == 0:
                        g = q * 8 + pr
                        if g >= 4:
                            qq, pp2 = divmod(g - 4, 8)
                            b.w("gpsimd", sT, d_mm_done[(qq, 2 * pp2 + 1)])
                        slot = g % 4
                        for jj in range(2):
                            ic0, ic1 = jj * 22, (jj + 1) * 22
                            b.op("gpsimd", "dma_start",
                                 h_t[:, slot * HTP + ic0 * 256:
                                     slot * HTP + ic1 * 256].rearrange(
                                     "q (c j) -> q c j", j=256),
                                 h_scr[ic0:ic1, :, tc * 128:(tc + 2) * 128].rearrange(
                                     "c q j -> q c j"),
                                 inc=sW, incn=16)
                        h_ld[(q, pr)] = sW.n
                    b.w("sync", sV, out_evac.get(key_prev, 0) if key_prev else 0)
                    b.op("sync", "dma_start", x1t[:, p * 512:(p + 1) * 512],
                         x1_scr[tc * 128:(tc + 1) * 128, q * 512:(q + 1) * 512],
                         inc=sA, incn=16)
                    x1_ld = sA.n
                    b.w("tensor", sW, h_ld[(q, pr)])
                    if q >= 2 and tc == 0:
                        b.w("tensor", sW, wdq_ld[q % 2])
                    b.w("tensor", sV, out_evac.get(key_prev, 0) if key_prev else 0)
                    hbase = ((q * 8 + pr) % 4) * HTP + (tc % 2) * 128
                    wbase = (q % 2) * ICN * 512
                    for ic in range(ICN):
                        b.op("tensor", "matmul", pd[:, :],
                             h_t[:, hbase + ic * 256: hbase + ic * 256 + 128],
                             wdsb[:, wbase + ic * 512: wbase + (ic + 1) * 512],
                             start=(ic == 0), stop=(ic == ICN - 1),
                             skip_group_check=True,
                             inc=(sT if ic == ICN - 1 else None))
                    d_mm_done[(q, tc)] = sT.n
                    b.w("vector", sT, d_mm_done[(q, tc)])
                    b.w("vector", sA, x1_ld)
                    prev_store = out_store.get((q, tc - 2) if tc >= 2 else
                                               ((q - 1, TC + tc - 2) if q >= 1 else None), 0)
                    b.w("vector", sO, prev_store)
                    b.op("vector", "tensor_mul", dtmp[:, :], pd[:, :],
                         dsb_s[:, q * 512:(q + 1) * 512])
                    b.op("vector", "tensor_add",
                         outsb[:, p * 512:(p + 1) * 512], dtmp[:, :],
                         x1t[:, p * 512:(p + 1) * 512], inc=sV)
                    out_evac[(q, tc)] = sV.n
                    b.w("sync", sV, out_evac[(q, tc)])
                    b.op("sync", "dma_start",
                         out_d[tc * 128:(tc + 1) * 128, q * 512:(q + 1) * 512],
                         outsb[:, p * 512:(p + 1) * 512], inc=sO, incn=16)
                    out_store[(q, tc)] = sO.n
        wdsb_g.__exit__(None, None, None)

        b.w("sync", sO, sO.n)

        with nc.Block() as block:
            @block.tensor
            def _(e):
                for f in b.streams["tensor"]:
                    f(e)

            @block.vector
            def _(e):
                for f in b.streams["vector"]:
                    f(e)

            @block.scalar
            def _(e):
                for f in b.streams["scalar"]:
                    f(e)

            @block.sync
            def _(e):
                for f in b.streams["sync"]:
                    f(e)

            @block.gpsimd
            def _(e):
                for f in b.streams["gpsimd"]:
                    f(e)

    return nc


def prep_inputs(L0_attn_hidden_states, L0_attn_slope_rate, L0_attn_w_qkv,
                L0_attn_w_gate, L0_attn_w_out, L0_attn_norm_w, L0_ffn_gate_w,
                L0_ffn_up_w, L0_ffn_down_w, L0_ffn_gate_s, L0_ffn_up_s, L0_ffn_down_s):
    f32 = lambda a: np.ascontiguousarray(np.asarray(a), dtype=np.float32)
    hs = f32(L0_attn_hidden_states)
    wqkv = f32(L0_attn_w_qkv)
    wgate = f32(L0_attn_w_gate)
    wout = f32(L0_attn_w_out)
    wg = f32(L0_ffn_gate_w)
    wu = f32(L0_ffn_up_w)
    wd = f32(L0_ffn_down_w)
    slope = f32(L0_attn_slope_rate).reshape(1, NH)
    nw = f32(L0_attn_norm_w)
    gsv = f32(L0_ffn_gate_s)
    usv = f32(L0_ffn_up_s)
    dsv = f32(L0_ffn_down_s)

    wqkv_img = np.ascontiguousarray(
        wqkv.reshape(NH, 3, HD, FC, 128).transpose(0, 4, 1, 3, 2).reshape(NH, 128, 3 * FC * 128))
    wgate_img = np.ascontiguousarray(
        wgate.reshape(FC, 128, FC, 128).transpose(0, 3, 2, 1).reshape(FC, 128, FC * 128))
    wout_img = np.ascontiguousarray(wout.T).reshape(FC, 128, D)
    wg_img = np.ascontiguousarray(
        wg.reshape(ICN, 128, FC, 128).transpose(0, 3, 2, 1).reshape(ICN, 128, FC * 128))
    wu_img = np.ascontiguousarray(
        wu.reshape(ICN, 128, FC, 128).transpose(0, 3, 2, 1).reshape(ICN, 128, FC * 128))
    wd_img = np.ascontiguousarray(wd.T).reshape(ICN, 128, D)

    common = {
        "wqkv": wqkv_img, "wgate": wgate_img, "wout": wout_img,
        "wgff": wg_img, "wuff": wu_img, "wdff": wd_img,
        "slopes": np.ascontiguousarray(np.broadcast_to(slope, (128, NH))),
        "nw": np.ascontiguousarray(nw.reshape(FC, 128).T),
        "gs": np.ascontiguousarray(gsv.reshape(ICN, 128).T),
        "us": np.ascontiguousarray(usv.reshape(ICN, 128).T),
        "dsb": np.ascontiguousarray(np.broadcast_to(dsv.reshape(1, D), (128, D))),
    }
    in_maps = []
    for c in range(8):
        m = dict(common)
        m["x"] = np.ascontiguousarray(hs[c])
        in_maps.append(m)
    return in_maps


def kernel(**inputs):
    in_maps = prep_inputs(**inputs)
    nc = build_core_program()
    res = run_bass_kernel_spmd(nc, in_maps, core_ids=list(range(8)))
    return np.stack([res.results[c]["out"] for c in range(8)], axis=0)


# version 19
# speedup vs baseline: 1.0189x; 1.0189x over previous
import contextlib
import numpy as np
import ml_dtypes
import concourse.bass as bass
import concourse.mybir as mybir
from concourse.bass_utils import run_bass_kernel_spmd

F32 = mybir.dt.float32
BF16 = mybir.dt.bfloat16
AF = mybir.ActivationFunctionType
ALU = mybir.AluOpType
AX = mybir.AxisListType

NT = 2048
D = 2048
FC = 16
TC = 16
NH = 16
HD = 128
BLK = 256
NB = 8
ICN = 44
INTER = 5632
BIGNEG = 40000.0

ENGS = ("tensor", "vector", "scalar", "sync", "gpsimd")

C_NEG1, C_EPS_LN, C_EPS_RMS, C_INVD, C_256 = 0, 1, 2, 3, 4


class Sem:
    def __init__(self, h):
        self.h = h
        self.n = 0


class Builder:
    def __init__(self):
        self.streams = {e: [] for e in ENGS}

    def w(self, eng, sem, val):
        if val is None or val <= 0:
            return
        self.streams[eng].append(lambda e, h=sem.h, v=val: e.wait_ge(h, v))

    def op(self, eng, method, *args, inc=None, incn=1, **kwargs):
        def f(e, m=method, a=args, k=kwargs, inc=inc, incn=incn):
            r = getattr(e, m)(*a, **k)
            if inc is not None:
                r.then_inc(inc.h, incn)
        self.streams[eng].append(f)
        if inc is not None:
            inc.n += incn
            return inc.n
        return None


def build_core_program(dbg=False):
    nc = bass.Bass("TRN2", target_bir_lowering=False)

    x_d = nc.dram_tensor("x", [NT, D], F32, kind="ExternalInput").ap()
    wqkv_d = nc.dram_tensor("wqkv", [NH, 128, 3 * FC * 128], F32, kind="ExternalInput").ap()
    wgate_d = nc.dram_tensor("wgate", [FC, 128, FC * 128], F32, kind="ExternalInput").ap()
    wout_d = nc.dram_tensor("wout", [FC, 128, D], F32, kind="ExternalInput").ap()
    wg_d = nc.dram_tensor("wgff", [ICN, 128, FC * 128], F32, kind="ExternalInput").ap()
    wu_d = nc.dram_tensor("wuff", [ICN, 128, FC * 128], F32, kind="ExternalInput").ap()
    wd_d = nc.dram_tensor("wdff", [ICN, 128, D], F32, kind="ExternalInput").ap()
    slopes_d = nc.dram_tensor("slopes", [128, NH], F32, kind="ExternalInput").ap()
    nw_d = nc.dram_tensor("nw", [128, FC], F32, kind="ExternalInput").ap()
    gs_d = nc.dram_tensor("gs", [128, ICN], F32, kind="ExternalInput").ap()
    us_d = nc.dram_tensor("us", [128, ICN], F32, kind="ExternalInput").ap()
    dsb_d = nc.dram_tensor("dsb", [128, D], F32, kind="ExternalInput").ap()

    out_d = nc.dram_tensor("out", [NT, D], F32, kind="ExternalOutput").ap()
    scratch_kind = "ExternalOutput" if dbg else "Internal"
    if dbg:
        xn1_dbg = nc.dram_tensor("xn1_dbg", [128, FC * NT], BF16, kind="ExternalOutput").ap()
        xn2_dbg = nc.dram_tensor("xn2_dbg", [128, FC * NT], BF16, kind="ExternalOutput").ap()
        istd_dbg = nc.dram_tensor("istd_dbg", [128, NT], F32, kind="ExternalOutput").ap()
    o_scr = nc.dram_tensor("o_scr", [NH, 128, NT], BF16, kind=scratch_kind).ap()
    go_scr = nc.dram_tensor("go_scr", [FC, 128, NT], BF16, kind=scratch_kind).ap()
    x1_scr = nc.dram_tensor("x1_scr", [NT, D], F32, kind=scratch_kind).ap()
    o2_scr = nc.dram_tensor("o2_scr", [NH, 128, NT], BF16, kind="Internal").ap()
    h_scr = nc.dram_tensor("h_scr", [ICN, 128, NT], BF16, kind=scratch_kind).ap()

    ident_np = np.eye(128, dtype=ml_dtypes.bfloat16)
    ones_np = np.ones((128, 128), dtype=ml_dtypes.bfloat16)
    consts_np = np.broadcast_to(
        np.array([-1.0, 1e-5, 1e-6, 1.0 / 2048.0, 256.0, 0.0, 1.0, 0.0], dtype=np.float32),
        (128, 8)).copy()
    arr_np = np.broadcast_to(np.arange(1, BLK + 1, dtype=np.float32), (128, BLK)).copy()
    karr_np = np.empty((128, 2), dtype=np.float32)
    for sub in range(2):
        karr_np[:, sub] = 255.0 - (sub * 128 + np.arange(128))
    idxt_np = np.empty((128, 2 * BLK), dtype=np.float32)
    for sub in range(2):
        tk = sub * 128 + np.arange(128)[:, None]
        tq = np.arange(BLK)[None, :]
        dif = tq - tk
        idxt_np[:, sub * BLK:(sub + 1) * BLK] = np.where(dif >= 0, dif, BIGNEG)
    ident_dr = nc.inline_tensor(ident_np, name="identc").ap()
    ones_dr = nc.inline_tensor(ones_np, name="onesc").ap()
    consts_dr = nc.inline_tensor(consts_np, name="constsc").ap()
    arr_dr = nc.inline_tensor(arr_np, name="arrc").ap()
    karr_dr = nc.inline_tensor(karr_np, name="karrc").ap()
    idxt_dr = nc.inline_tensor(idxt_np, name="idxtc").ap()

    b = Builder()

    with contextlib.ExitStack() as es0:
        ec = es0.enter_context
        hT = ec(nc.semaphore("sT")); hV = ec(nc.semaphore("sV")); hS = ec(nc.semaphore("sS"))
        hW = ec(nc.semaphore("sW")); hA = ec(nc.semaphore("sA")); hO = ec(nc.semaphore("sO"))
        cst = ec(nc.sbuf_tensor("cst", [128, 8], F32))
        ident_s = ec(nc.sbuf_tensor("ident_s", [128, 128], BF16))
        ones_s = ec(nc.sbuf_tensor("ones_s", [128, 128], BF16))
        slopes_s = ec(nc.sbuf_tensor("slopes_s", [128, NH], F32))
        nw_s = ec(nc.sbuf_tensor("nw_s", [128, FC], F32))
        gs_s = ec(nc.sbuf_tensor("gs_s", [128, ICN], F32))
        us_s = ec(nc.sbuf_tensor("us_s", [128, ICN], F32))
        dsb_s = ec(nc.sbuf_tensor("dsb_s", [128, D], F32))
        sT, sV, sS = Sem(hT), Sem(hV), Sem(hS)
        sW, sA, sO = Sem(hW), Sem(hA), Sem(hO)
        sems = {"sT": sT, "sV": sV, "sS": sS, "sW": sW, "sA": sA, "sO": sO}
        own = {"tensor": {"sT"}, "vector": {"sV"}, "scalar": {"sS"},
               "sync": set(), "gpsimd": set()}

        def barrier(skips=None):
            snap = {k: s.n for k, s in sems.items()}
            for eng in ENGS:
                for k, s in sems.items():
                    if k in own[eng]:
                        continue
                    if skips and k in skips.get(eng, ()):
                        continue
                    b.w(eng, s, snap[k])

        for dst, src in ((cst, consts_dr), (ident_s, ident_dr), (ones_s, ones_dr),
                         (slopes_s, slopes_d), (nw_s, nw_d), (gs_s, gs_d),
                         (us_s, us_d), (dsb_s, dsb_d)):
            b.op("gpsimd", "dma_start", dst[:, :], src[:, :], inc=sW, incn=16)
        consts_ld0 = sW.n

        with nc.sbuf_tensor("xn1", [128, FC * NT], BF16) as xn1:
            with contextlib.ExitStack() as esA:
                ec = esA.enter_context
                arr_s = ec(nc.sbuf_tensor("arr_s", [128, BLK], F32))
                karr_s = ec(nc.sbuf_tensor("karr_s", [128, 2], F32))
                idxt_s = ec(nc.sbuf_tensor("idxt_s", [128, 2 * BLK], F32))
                qdec = ec(nc.sbuf_tensor("qdec", [128, NH * BLK], BF16))
                ddt = ec(nc.sbuf_tensor("ddt", [128, NH * 2 * BLK], BF16))
                kdec = ec(nc.sbuf_tensor("kdec", [128, NH * 2], F32))
                bdall = ec(nc.sbuf_tensor("bdall", [128, NH], F32))
                dtmpA = ec(nc.sbuf_tensor("dtmpA", [128, 2 * NH], F32))
                wsb_g = nc.sbuf_tensor("wsb", [128, 2 * 3 * FC * 128], BF16,
                                       side="right")
                wsb = wsb_g.__enter__()
                xA = ec(nc.sbuf_tensor("xA", [128, 2 * D], F32))
                xntm = ec(nc.sbuf_tensor("xntm", [128, 2 * D], BF16))
                stat = ec(nc.sbuf_tensor("stat", [128, 64], F32))
                q_sb = ec(nc.sbuf_tensor("q_sb", [128, 2 * NT], BF16))
                k_sb = ec(nc.sbuf_tensor("k_sb", [128, 2 * NT], BF16))
                v_sb = ec(nc.sbuf_tensor("v_sb", [128, 2 * NT], BF16))
                kkT = ec(nc.sbuf_tensor("kkT", [128, 2 * NT], BF16))
                vT = ec(nc.sbuf_tensor("vT", [128, 2 * NT], BF16))
                qd_sb = ec(nc.sbuf_tensor("qd_sb", [128, 2 * BLK], BF16))
                qd2_sb = ec(nc.sbuf_tensor("qd2_sb", [128, 2 * BLK], BF16))
                qkTd = ec(nc.sbuf_tensor("qkTd", [128, 2 * 2 * BLK], BF16))
                kv_sb = ec(nc.sbuf_tensor("kv_sb", [128, HD], F32))
                kv_bf = ec(nc.sbuf_tensor("kv_bf", [128, 2 * HD], BF16))
                s_bf = ec(nc.sbuf_tensor("s_bf", [128, 2 * HD], BF16))
                o_sb = ec(nc.sbuf_tensor("o_sb", [128, 2 * NT], BF16))
                o2st = ec(nc.sbuf_tensor("o2st", [128, NT], BF16))
                pp0 = ec(nc.psum_tensor("pp0", [128, 512], F32))
                pp1 = ec(nc.psum_tensor("pp1", [128, 512], F32))
                qk0 = ec(nc.psum_tensor("qk0", [128, 512], F32))
                qk1 = ec(nc.psum_tensor("qk1", [128, 512], F32))
                po0 = ec(nc.psum_tensor("po0", [128, BLK], F32))
                po1 = ec(nc.psum_tensor("po1", [128, BLK], F32))
                pkv0 = ec(nc.psum_tensor("pkv0", [128, HD], F32))
                pkv1 = ec(nc.psum_tensor("pkv1", [128, HD], F32))
                pp = (pp0, pp1)
                qkp = (qk0, qk1)
                pop = (po0, po1)
                pkvp = (pkv0, pkv1)

                for dst, src in ((arr_s, arr_dr), (karr_s, karr_dr), (idxt_s, idxt_dr)):
                    b.op("gpsimd", "dma_start", dst[:, :], src[:, :], inc=sW, incn=16)
                consts_ld = sW.n
                b.w("vector", sW, consts_ld)
                b.w("scalar", sW, consts_ld)
                b.op("vector", "tensor_scalar", dtmpA[:, 0:NH], slopes_s[:, :],
                     cst[:, C_NEG1:C_NEG1 + 1], None, ALU.mult)
                b.op("vector", "tensor_scalar", dtmpA[:, NH:2 * NH], slopes_s[:, :],
                     cst[:, C_256:C_256 + 1], None, ALU.mult, inc=sV)
                negs_done = sV.n
                b.w("scalar", sV, negs_done)
                b.op("scalar", "activation", bdall[:, :], dtmpA[:, NH:2 * NH], AF.Exp,
                     scale=cst[:, C_NEG1:C_NEG1 + 1], inc=sS)
                for h in range(NH):
                    nscol = dtmpA[:, h:h + 1]
                    b.op("scalar", "activation", qdec[:, h * BLK:(h + 1) * BLK],
                         arr_s[:, :], AF.Exp, scale=nscol)
                    b.op("scalar", "activation", kdec[:, 2 * h:2 * h + 2],
                         karr_s[:, :], AF.Exp, scale=nscol)
                    b.op("scalar", "activation", ddt[:, h * 2 * BLK:(h + 1) * 2 * BLK],
                         idxt_s[:, :], AF.Exp, scale=nscol,
                         inc=(sS if h == NH - 1 else None))
                decay_done_s = sS.n

                pp_free_v = [0, 0]
                pp_free_s = [0, 0]
                tr4 = (qk0, qk1, po0, po1)
                tr4_free = [0, 0, 0, 0]
                w_ld = [0] * NH
                proj_done = [0] * NH
                attn_mm_done = [0] * NH
                qread_done = [0] * NH
                o_store = [0] * NH
                o2_store = [0] * NH
                o_sq_done = [0] * NH
                ev_k_done = [0] * NH
                ev_v_done = [0] * NH
                ev_all_done = [0] * NH
                for hh in range(min(2, NH)):
                    b.op("gpsimd", "dma_start",
                         wsb[:, (hh % 2) * 6144:(hh % 2 + 1) * 6144],
                         wqkv_d[hh, :, :], inc=sW, incn=16)
                    w_ld[hh] = sW.n

                grp = [0]
                pkv_free = [0, 0]
                o_evac = {}
                qkTd_done = {}
                qd_done = {}
                kv_bf_done = {}
                sbf_done = {}
                scalar_head_gate = [False] * NH

                def emit_proj_group(hh, part, slab, xn1_wait=None):
                    hp = hh % 2
                    pi = grp[0] % 2
                    grp[0] += 1
                    if part == 0 and slab == 0:
                        b.w("tensor", sW, w_ld[hh])
                    if xn1_wait is not None:
                        b.w("tensor", sV, xn1_wait)
                    b.w("tensor", sS, pp_free_s[pi])
                    b.w("tensor", sV, pp_free_v[pi])
                    for fc in range(FC):
                        b.op("tensor", "matmul", pp[pi][:, :],
                             wsb[:, hp * 6144 + (part * FC + fc) * 128:
                                 hp * 6144 + (part * FC + fc + 1) * 128],
                             xn1[:, fc * NT + slab * 512: fc * NT + (slab + 1) * 512],
                             start=(fc == 0), stop=(fc == FC - 1),
                             skip_group_check=True,
                             inc=(sT if fc == FC - 1 else None))
                    mm_done = sT.n
                    if not scalar_head_gate[hh]:
                        scalar_head_gate[hh] = True
                        if hh >= 2:
                            b.w("scalar", sT, attn_mm_done[hh - 2])
                            b.w("scalar", sV, qread_done[hh - 2])
                    b.w("scalar", sT, mm_done)
                    dst = (q_sb, k_sb, v_sb)[part]
                    b.op("scalar", "activation",
                         dst[:, hp * NT + slab * 512: hp * NT + (slab + 1) * 512],
                         pp[pi][:, :], AF.Silu, inc=sS)
                    pp_free_s[pi] = sS.n
                    if part == 1 and slab == 3:
                        ev_k_done[hh] = sS.n
                    if part == 2 and slab == 3:
                        ev_v_done[hh] = sS.n
                        ev_all_done[hh] = sS.n
                    if part == 2 and slab == 3:
                        proj_done[hh] = sT.n

                x_free = [0, 0]
                xn_free = [0, 0]
                for tcp in range(TC // 2):
                    tcs = (2 * tcp, 2 * tcp + 1)
                    lds = [0, 0]
                    for i, tc in enumerate(tcs):
                        p = tc % 2
                        b.w("sync", sW, consts_ld0)
                        b.w("sync", sV, x_free[p])
                        b.op("sync", "dma_start", xA[:, p * D:(p + 1) * D],
                             x_d[tc * 128:(tc + 1) * 128, :], inc=sA, incn=16)
                        lds[i] = sA.n
                    for i, tc in enumerate(tcs):
                        p = tc % 2
                        b.w("vector", sA, lds[i])
                        for j in range(4):
                            b.op("vector", "bn_stats",
                                 stat[:, p * 32 + j * 6: p * 32 + (j + 1) * 6],
                                 xA[:, p * D + j * 512: p * D + (j + 1) * 512],
                                 inc=(sV if j == 3 else None))
                    b.w("vector", sV, sV.n)
                    for tc in tcs:
                        p = tc % 2
                        b.op("vector", "bn_aggr", stat[:, p * 32 + 24: p * 32 + 26],
                             stat[:, p * 32: p * 32 + 24], inc=(sV if p == 1 else None))
                    var_done = sV.n
                    b.w("scalar", sV, var_done)
                    for tc in tcs:
                        p = tc % 2
                        b.op("scalar", "activation", stat[:, p * 32 + 26: p * 32 + 27],
                             stat[:, p * 32 + 25: p * 32 + 26], AF.Sqrt,
                             bias=cst[:, C_EPS_LN:C_EPS_LN + 1],
                             inc=(sS if p == 1 else None))
                    std_done = sS.n
                    b.w("vector", sS, std_done)
                    for tc in tcs:
                        p = tc % 2
                        b.op("vector", "reciprocal", stat[:, p * 32 + 27: p * 32 + 28],
                             stat[:, p * 32 + 26: p * 32 + 27],
                             inc=(sV if p == 1 else None))
                    b.w("vector", sV, sV.n)
                    for tc in tcs:
                        p = tc % 2
                        b.w("vector", sT, xn_free[p])
                        b.op("vector", "tensor_scalar", xntm[:, p * D:(p + 1) * D],
                             xA[:, p * D:(p + 1) * D], stat[:, p * 32 + 24: p * 32 + 25],
                             stat[:, p * 32 + 27: p * 32 + 28], ALU.subtract, ALU.mult,
                             inc=sV)
                        if tc == tcs[0]:
                            xn_done_a = sV.n
                        else:
                            xn_done_b = sV.n
                        x_free[p] = sV.n
                    for i, tc in enumerate(tcs):
                        p = tc % 2
                        b.w("tensor", sV, xn_done_a if i == 0 else xn_done_b)
                        xn1_r = xn1[:, :].rearrange("q (c t) -> q c t", c=FC)
                        for pj in range(FC // 2):
                            r = pj % 4
                            b.w("tensor", sV, tr4_free[r])
                            for half in range(2):
                                fc = 2 * pj + half
                                b.op("tensor", "matmul",
                                     tr4[r][:, half * 128:(half + 1) * 128],
                                     xntm[:, p * D + fc * 128: p * D + (fc + 1) * 128],
                                     ident_s[:, :], start=True, stop=True,
                                     skip_group_check=True,
                                     inc=(sT if half == 1 else None))
                            mm_done = sT.n
                            b.w("vector", sT, mm_done)
                            b.op("vector", "tensor_copy",
                                 xn1_r[:, 2 * pj:2 * pj + 2, tc * 128:(tc + 1) * 128],
                                 tr4[r][:, 0:256].rearrange("q (c t) -> q c t", c=2),
                                 inc=sV)
                            tr4_free[r] = sV.n
                        xn_free[p] = sT.n
                    if tcs[1] % 4 == 3:
                        sl = tcs[1] // 4
                        xn1_slab = sV.n
                        for part in range(3):
                            emit_proj_group(0, part, sl, xn1_wait=xn1_slab)
                xn1_done_v = sV.n
                xn1_done_s = sS.n
                qk_free0 = [tr4_free[0], tr4_free[1]]
                po_free0 = [tr4_free[2], tr4_free[3]]

                qk_free = list(qk_free0)
                po_free = list(po_free0)

                def emit_transposes(hh):
                    hp = hh % 2
                    if hh >= 2:
                        b.w("vector", sT, attn_mm_done[hh - 2])
                    b.w("tensor", sS, ev_k_done[hh])
                    for j in range(2 * NB):
                        r = j % 2
                        b.w("tensor", sV, po_free[r])
                        b.op("tensor", "matmul", pop[r][:, 0:128],
                             k_sb[:, hp * NT + j * 128: hp * NT + (j + 1) * 128],
                             ident_s[:, :], start=True, stop=True,
                             skip_group_check=True, inc=sT)
                        mm_done = sT.n
                        sub = j % 2
                        b.w("vector", sT, mm_done)
                        b.op("vector", "tensor_scalar",
                             kkT[:, hp * NT + j * 128: hp * NT + (j + 1) * 128],
                             pop[r][:, 0:128], kdec[:, 2 * hh + sub:2 * hh + sub + 1],
                             None, ALU.mult, inc=sV)
                        po_free[r] = sV.n
                    b.w("tensor", sS, ev_v_done[hh])
                    for pj in range(NB):
                        r = pj % 2
                        b.w("tensor", sV, po_free[r])
                        for half in range(2):
                            j = 2 * pj + half
                            b.op("tensor", "matmul",
                                 pop[r][:, half * 128:(half + 1) * 128],
                                 v_sb[:, hp * NT + j * 128: hp * NT + (j + 1) * 128],
                                 ident_s[:, :], start=True, stop=True,
                                 skip_group_check=True,
                                 inc=(sT if half == 1 else None))
                        mm_done = sT.n
                        b.w("vector", sT, mm_done)
                        b.op("vector", "tensor_copy",
                             vT[:, hp * NT + pj * 256: hp * NT + (pj + 1) * 256],
                             pop[r][:, 0:256], inc=sV)
                        po_free[r] = sV.n

                emit_transposes(0)

                for h in range(NH):
                    hp = h % 2
                    qkT_done = {}
                    o_done = {}
                    kvmm_done = {}
                    groups = ([(part, slab) for part in range(3) for slab in range(4)]
                              if h + 1 < NH else [])
                    gi = [0]

                    def drain_groups(target):
                        while gi[0] < min(target, len(groups)):
                            part, slab = groups[gi[0]]
                            emit_proj_group(h + 1, part, slab)
                            gi[0] += 1

                    def emit_qkT(bb, h=h, hp=hp, qkT_done=qkT_done):
                        pi = bb % 2
                        b.w("tensor", sV, qk_free[pi])
                        for sub in range(2):
                            b.op("tensor", "matmul",
                                 qkp[pi][:, sub * BLK:(sub + 1) * BLK],
                                 k_sb[:, hp * NT + bb * BLK + sub * 128:
                                      hp * NT + bb * BLK + (sub + 1) * 128],
                                 q_sb[:, hp * NT + bb * BLK: hp * NT + (bb + 1) * BLK],
                                 start=True, stop=True, skip_group_check=True,
                                 inc=(sT if sub == 1 else None))
                        qkT_done[bb] = sT.n

                    emit_qkT(0)
                    emit_qkT(1)
                    b.w("vector", sS, ev_all_done[h])
                    b.op("vector", "tensor_mul", qd_sb[:, 0:BLK],
                         q_sb[:, hp * NT: hp * NT + BLK],
                         qdec[:, h * BLK:(h + 1) * BLK], inc=sV)
                    qd_done[(h, 0)] = sV.n
                    b.w("vector", sT, qkT_done[0])
                    b.op("vector", "tensor_mul", qkTd[:, 0:512],
                         qkp[0][:, :], ddt[:, h * 2 * BLK:(h + 1) * 2 * BLK], inc=sV)
                    qkTd_done[(h, 0)] = sV.n
                    qk_free[0] = sV.n

                    for bb in range(NB):
                        pi = bb % 2
                        b.w("tensor", sV, po_free[pi])
                        b.w("tensor", sV, qkTd_done[(h, bb)])
                        if bb > 0:
                            b.w("tensor", sV, qd_done[(h, bb)])
                            b.w("tensor", sV, sbf_done[(h, bb - 1)])
                            b.op("tensor", "matmul", pop[pi][:, :],
                                 s_bf[:, ((bb - 1) % 2) * HD:((bb - 1) % 2 + 1) * HD],
                                 qd_sb[:, pi * BLK:(pi + 1) * BLK], start=True,
                                 stop=False, skip_group_check=True)
                        if bb > 1:
                            b.w("tensor", sV, kv_bf_done[(h, bb - 2)])
                            b.op("tensor", "matmul", pop[pi][:, :],
                                 kv_bf[:, (bb % 2) * HD:(bb % 2 + 1) * HD],
                                 qd2_sb[:, pi * BLK:(pi + 1) * BLK], start=False,
                                 stop=False, skip_group_check=True)
                        for sub in range(2):
                            b.op("tensor", "matmul", pop[pi][:, :],
                                 vT[:, hp * NT + (2 * bb + sub) * 128:
                                     hp * NT + (2 * bb + sub + 1) * 128],
                                 qkTd[:, pi * 512 + sub * BLK: pi * 512 + (sub + 1) * BLK],
                                 start=(bb == 0 and sub == 0), stop=(sub == 1),
                                 skip_group_check=True, inc=(sT if sub == 1 else None))
                        o_done[bb] = sT.n
                        b.w("tensor", sV, pkv_free[pi])
                        for sub in range(2):
                            b.op("tensor", "matmul", pkvp[pi][:, :],
                                 kkT[:, hp * NT + (2 * bb + sub) * 128:
                                     hp * NT + (2 * bb + sub + 1) * 128],
                                 vT[:, hp * NT + (2 * bb + sub) * 128:
                                     hp * NT + (2 * bb + sub + 1) * 128],
                                 start=(sub == 0), stop=(sub == 1),
                                 skip_group_check=True, inc=(sT if sub == 1 else None))
                        kvmm_done[bb] = sT.n
                        drain_groups(((bb + 1) * 12 + 7) // 8)
                        if bb + 1 < NB:
                            b.w("vector", sT, o_done.get(bb - 1, 0))
                            b.w("vector", sT, kvmm_done[bb])
                            b.op("vector", "tensor_copy",
                                 s_bf[:, (bb % 2) * HD:(bb % 2 + 1) * HD],
                                 pkvp[pi][:, :], inc=sV)
                            sbf_done[(h, bb)] = sV.n
                        b.w("vector", sT, kvmm_done[bb])
                        if bb == 0:
                            b.op("vector", "tensor_copy",
                                 kv_bf[:, (bb % 2) * HD:(bb % 2 + 1) * HD],
                                 pkvp[pi][:, :], inc=sV)
                            kv_bf_done[(h, bb)] = sV.n
                            b.op("vector", "tensor_copy", kv_sb[:, :], pkvp[pi][:, :],
                                 inc=sV)
                        else:
                            b.op("vector", "scalar_tensor_tensor",
                                 kv_bf[:, (bb % 2) * HD:(bb % 2 + 1) * HD],
                                 kv_sb[:, :], bdall[:, h:h + 1], pkvp[pi][:, :],
                                 ALU.mult, ALU.add, inc=sV)
                            kv_bf_done[(h, bb)] = sV.n
                            b.op("vector", "scalar_tensor_tensor", kv_sb[:, :],
                                 kv_sb[:, :], bdall[:, h:h + 1], pkvp[pi][:, :],
                                 ALU.mult, ALU.add, inc=sV)
                        pkv_free[pi] = sV.n
                        if h >= 2 and bb == 0:
                            b.w("vector", sO, o_store[h - 2])
                        b.w("vector", sT, o_done[bb])
                        b.op("vector", "tensor_copy",
                             o_sb[:, hp * NT + bb * BLK: hp * NT + (bb + 1) * BLK],
                             pop[pi][:, :], inc=sV)
                        o_evac[(h, bb)] = sV.n
                        po_free[pi] = sV.n
                        if bb + 1 < NB:
                            b.w("vector", sT, o_done.get(bb - 1, 0))
                            b.op("vector", "tensor_mul",
                                 qd_sb[:, ((bb + 1) % 2) * BLK:((bb + 1) % 2 + 1) * BLK],
                                 q_sb[:, hp * NT + (bb + 1) * BLK:
                                      hp * NT + (bb + 2) * BLK],
                                 qdec[:, h * BLK:(h + 1) * BLK], inc=sV)
                            b.op("vector", "scalar_tensor_tensor",
                                 qd2_sb[:, ((bb + 1) % 2) * BLK:((bb + 1) % 2 + 1) * BLK],
                                 q_sb[:, hp * NT + (bb + 1) * BLK:
                                      hp * NT + (bb + 2) * BLK],
                                 bdall[:, h:h + 1],
                                 qdec[:, h * BLK:(h + 1) * BLK],
                                 ALU.mult, ALU.mult, inc=sV)
                            qd_done[(h, bb + 1)] = sV.n
                        if bb + 1 < NB:
                            pj = (bb + 1) % 2
                            b.w("vector", sT, qkT_done[bb + 1])
                            b.op("vector", "tensor_mul",
                                 qkTd[:, pj * 512:(pj + 1) * 512], qkp[pj][:, :],
                                 ddt[:, h * 2 * BLK:(h + 1) * 2 * BLK], inc=sV)
                            qkTd_done[(h, bb + 1)] = sV.n
                            qk_free[pj] = sV.n
                        if bb + 2 < NB:
                            emit_qkT(bb + 2)
                    attn_mm_done[h] = sT.n
                    qread_done[h] = qd_done[(h, NB - 1)]
                    drain_groups(len(groups))
                    b.w("sync", sV, o_evac[(h, NB - 1)])
                    b.op("sync", "dma_start", o_scr[h, :, :],
                         o_sb[:, hp * NT:(hp + 1) * NT], inc=sO, incn=16)
                    b.w("scalar", sV, o_evac[(h, NB - 1)])
                    if h >= 1:
                        b.w("scalar", sO, o2_store[h - 1])
                    b.op("scalar", "activation", o2st[:, :],
                         o_sb[:, hp * NT:(hp + 1) * NT], AF.Square, inc=sS)
                    o_sq_done[h] = sS.n
                    b.w("sync", sS, o_sq_done[h])
                    b.op("sync", "dma_start", o2_scr[h, :, :],
                         o2st[:, :], inc=sO, incn=16)
                    o2_store[h] = sO.n
                    o_store[h] = sO.n
                    if h + 1 < NH:
                        emit_transposes(h + 1)
                    if h + 2 < NH:
                        b.w("gpsimd", sT, proj_done[h])
                        b.op("gpsimd", "dma_start",
                             wsb[:, ((h + 2) % 2) * 6144:((h + 2) % 2 + 1) * 6144],
                             wqkv_d[h + 2, :, :], inc=sW, incn=16)
                        w_ld[h + 2] = sW.n
                    if h == NH - 2:
                        b.w("gpsimd", sT, proj_done[NH - 2])
                        for wo in range(2):
                            b.op("gpsimd", "dma_start",
                                 wsb[:, ((NH - 2) % 2) * 6144 + wo * 2048:
                                     ((NH - 2) % 2) * 6144 + (wo + 1) * 2048],
                                 wgate_d[wo, :, :], inc=sW, incn=16)
                        wg_pre = sW.n

            wsb_g.__exit__(None, None, None)
            if dbg:
                b.w("sync", sV, xn1_done_v)
                b.w("sync", sS, xn1_done_s)
                b.op("sync", "dma_start", xn1_dbg[:, :], xn1[:, :], inc=sO, incn=16)
            barrier()
            woutsb_g = nc.sbuf_tensor("woutsb", [128, FC * D], BF16, side="right")
            woutsb = woutsb_g.__enter__()
            go_t_g = nc.sbuf_tensor("go_t", [128, 2 * FC * 256], BF16, side="right")
            go_t = go_t_g.__enter__()

            with contextlib.ExitStack() as esB:
                ec = esB.enter_context
                o_in = ec(nc.sbuf_tensor("o_in", [128, 2 * NT], BF16))
                o2_sb = ec(nc.sbuf_tensor("o2_sb", [128, 2 * NT], BF16))
                wgsb = ec(nc.sbuf_tensor("wgsb", [128, 2 * D], BF16))
                gatef = ec(nc.sbuf_tensor("gatef", [128, 2 * 512], F32))
                gotmp = ec(nc.sbuf_tensor("gotmp", [128, 512], F32))
                go_sb = ec(nc.sbuf_tensor("go_sb", [128, 2 * NT], BF16))
                istd_b = ec(nc.sbuf_tensor("istd_b", [128, NT], F32))
                psR0 = ec(nc.psum_tensor("psR0", [128, 512], F32))
                psR1 = ec(nc.psum_tensor("psR1", [128, 512], F32))
                psR2 = ec(nc.psum_tensor("psR2", [128, 512], F32))
                psR3 = ec(nc.psum_tensor("psR3", [128, 512], F32))
                psG0 = ec(nc.psum_tensor("psG0", [128, 512], F32))
                psG1 = ec(nc.psum_tensor("psG1", [128, 512], F32))
                psR = (psR0, psR1, psR2, psR3)
                psG = (psG0, psG1)

                sq_free_s = [0, 0]
                sq_free_t = [0, 0]
                o_lds = [0] * NH
                for oc in range(min(2, NH)):
                    b.op("sync", "dma_start", o2_sb[:, (oc % 2) * NT:(oc % 2 + 1) * NT],
                         o2_scr[oc, :, :], inc=sA, incn=16)
                    o_lds[oc] = sA.n
                for oc in range(NH):
                    p = oc % 2
                    b.w("tensor", sA, o_lds[oc])
                    for slab in range(4):
                        b.op("tensor", "matmul", psR[slab][:, :], ones_s[:, :],
                             o2_sb[:, p * NT + slab * 512: p * NT + (slab + 1) * 512],
                             start=(oc == 0), stop=(oc == NH - 1),
                             skip_group_check=True, inc=(sT if slab == 3 else None))
                    sq_free_t[p] = sT.n
                    if oc + 2 < NH:
                        b.w("sync", sT, sq_free_t[p])
                        b.op("sync", "dma_start", o2_sb[:, p * NT:(p + 1) * NT],
                             o2_scr[oc + 2, :, :], inc=sA, incn=16)
                        o_lds[oc + 2] = sA.n
                rms_done = sT.n
                b.w("scalar", sT, rms_done)
                for slab in range(4):
                    b.op("scalar", "activation", istd_b[:, slab * 512:(slab + 1) * 512],
                         psR[slab][:, :], AF.Sqrt, scale=cst[:, C_INVD:C_INVD + 1],
                         bias=cst[:, C_EPS_RMS:C_EPS_RMS + 1],
                         inc=(sS if slab == 3 else None))
                sqrt_done = sS.n
                b.w("vector", sS, sqrt_done)
                b.op("vector", "reciprocal", istd_b[:, :], istd_b[:, :], inc=sV)
                istd_done = sV.n
                b.w("vector", sV, istd_done)

                wg_ld = [wg_pre, wg_pre]
                b.op("gpsimd", "dma_start", woutsb[:, 0:D], wout_d[0, :, :],
                     inc=sW, incn=16)
                go_done = [0] * FC
                go_store = [0] * FC
                gate_mm_done = [0] * FC
                sig_free = [0, 0]
                gate_read = [0, 0]
                o_lds2 = [0] * FC
                for oc in range(min(2, FC)):
                    b.w("sync", sS, sq_free_s[oc % 2])
                    b.op("sync", "dma_start", o_in[:, (oc % 2) * NT:(oc % 2 + 1) * NT],
                         o_scr[oc, :, :], inc=sA, incn=16)
                    o_lds2[oc] = sA.n
                wsb_base = ((NH - 2) % 2) * 6144
                for oc in range(FC):
                    p = oc % 2
                    o_ld2 = o_lds2[oc]
                    b.w("tensor", sW, wg_ld[p])
                    wg_src = (wsb[:, wsb_base + oc * 2048: wsb_base + (oc + 1) * 2048]
                              if oc < 2 else wgsb[:, p * D:(p + 1) * D])
                    for slab in range(4):
                        gp = slab % 2
                        b.w("tensor", sS, sig_free[gp])
                        for fc in range(FC):
                            b.op("tensor", "matmul", psG[gp][:, :],
                                 wg_src[:, fc * 128:(fc + 1) * 128],
                                 xn1[:, fc * NT + slab * 512: fc * NT + (slab + 1) * 512],
                                 start=(fc == 0), stop=(fc == FC - 1),
                                 skip_group_check=True, inc=(sT if fc == FC - 1 else None))
                        g_done = sT.n
                        b.w("scalar", sT, g_done)
                        b.w("scalar", sV, gate_read[gp])
                        b.op("scalar", "activation", gatef[:, gp * 512:(gp + 1) * 512],
                             psG[gp][:, :], AF.Sigmoid, inc=sS)
                        sig_done = sS.n
                        sig_free[gp] = sS.n
                        b.w("vector", sS, sig_done)
                        if slab == 0:
                            b.w("vector", sA, o_ld2)
                            if oc == 0:
                                b.w("vector", sV, istd_done)
                            if oc >= 2:
                                b.w("vector", sO, go_store[oc - 2])
                        b.op("vector", "scalar_tensor_tensor", gotmp[:, :],
                             gatef[:, gp * 512:(gp + 1) * 512], nw_s[:, oc:oc + 1],
                             o_in[:, p * NT + slab * 512: p * NT + (slab + 1) * 512],
                             ALU.mult, ALU.mult)
                        b.op("vector", "tensor_mul",
                             go_sb[:, p * NT + slab * 512: p * NT + (slab + 1) * 512],
                             gotmp[:, :], istd_b[:, slab * 512:(slab + 1) * 512], inc=sV)
                        gate_read[gp] = sV.n
                    gate_mm_done[oc] = sT.n
                    go_done[oc] = sV.n
                    if oc + 2 < FC:
                        b.w("sync", sV, go_done[oc])
                        b.op("sync", "dma_start", o_in[:, p * NT:(p + 1) * NT],
                             o_scr[oc + 2, :, :], inc=sA, incn=16)
                        o_lds2[oc + 2] = sA.n
                    if oc + 2 < FC:
                        b.w("gpsimd", sT, gate_mm_done[oc])
                        b.op("gpsimd", "dma_start", wgsb[:, p * D:(p + 1) * D],
                             wgate_d[oc + 2, :, :], inc=sW, incn=16)
                        wg_ld[p] = sW.n
                    for wo in (2 * oc + 1, 2 * oc + 2):
                        if wo < FC:
                            if wo == 10:
                                b.w("gpsimd", sT, gate_mm_done[1])
                            b.op("gpsimd", "dma_start",
                                 woutsb[:, wo * D:(wo + 1) * D],
                                 wout_d[wo, :, :], inc=sW, incn=16)
                    b.w("sync", sV, go_done[oc])
                    b.op("sync", "dma_start", go_scr[oc, :, :],
                         go_sb[:, p * NT:(p + 1) * NT], inc=sO, incn=16)
                    go_store[oc] = sO.n
                wout_ld = sW.n
                if dbg:
                    b.w("sync", sV, istd_done)
                    b.op("sync", "dma_start", istd_dbg[:, :], istd_b[:, :], inc=sO, incn=16)

            barrier(skips={"tensor": {"sO", "sA"}})

        with nc.sbuf_tensor("xn2", [128, FC * NT], BF16) as xn2:
            with contextlib.ExitStack() as esB2:
                ec = esB2.enter_context

                xB = ec(nc.sbuf_tensor("xB", [128, 2 * D], F32))
                x1sb = ec(nc.sbuf_tensor("x1sb", [128, 2 * D], F32))
                xn2tm = ec(nc.sbuf_tensor("xn2tm", [128, 2 * D], BF16))
                sqd2 = ec(nc.sbuf_tensor("sqd2", [128, D], BF16))
                stat2 = ec(nc.sbuf_tensor("stat2", [128, 64], F32))
                psO0 = ec(nc.psum_tensor("psO0", [128, 512], F32))
                psO1 = ec(nc.psum_tensor("psO1", [128, 512], F32))
                psO2 = ec(nc.psum_tensor("psO2", [128, 512], F32))
                psO3 = ec(nc.psum_tensor("psO3", [128, 512], F32))
                psT0 = ec(nc.psum_tensor("psT0", [128, 512], F32))
                psT1 = ec(nc.psum_tensor("psT1", [128, 512], F32))
                psT2 = ec(nc.psum_tensor("psT2", [128, 512], F32))
                psT3 = ec(nc.psum_tensor("psT3", [128, 512], F32))
                psO = (psO0, psO1, psO2, psO3)
                psT = (psT0, psT1, psT2, psT3)

                outproj_done = [0] * TC
                x1_evac = [0] * TC
                x1_store = [0] * TC
                xn2tm_done = [0] * TC
                ln2_tr_done = [0] * TC
                go_ld = [0] * TC
                x1_half_evac = [[0] * TC, [0] * TC]
                for pr0 in range(2):
                    b.w("gpsimd", sO, go_store[2 * pr0 + 1])
                    b.op("gpsimd", "dma_start",
                         go_t[:, pr0 * FC * 256:(pr0 + 1) * FC * 256].rearrange(
                             "q (c j) -> q c j", c=FC),
                         go_scr[:, :, pr0 * 256:(pr0 + 1) * 256].rearrange(
                             "c q j -> q c j"),
                         inc=sW, incn=16)
                    go_ld[2 * pr0] = go_ld[2 * pr0 + 1] = sW.n
                sq_done_l = [0] * TC
                psT_free = [0, 0, 0, 0]
                xn2_r = xn2[:, :].rearrange("q (c t) -> q c t", c=FC)

                def emit_ln2_chain(tc):
                    p = tc % 2
                    for i in range(4):
                        b.op("vector", "bn_stats",
                             stat2[:, p * 32 + i * 6: p * 32 + (i + 1) * 6],
                             x1sb[:, p * D + i * 512: p * D + (i + 1) * 512],
                             inc=(sV if i == 3 else None))
                    b.w("vector", sV, sV.n)
                    b.op("vector", "bn_aggr", stat2[:, p * 32 + 24: p * 32 + 26],
                         stat2[:, p * 32: p * 32 + 24], inc=sV)
                    var_done = sV.n
                    sq_done_l[tc] = 0
                    b.w("scalar", sV, var_done)
                    b.op("scalar", "activation", stat2[:, p * 32 + 26: p * 32 + 27],
                         stat2[:, p * 32 + 25: p * 32 + 26], AF.Sqrt,
                         bias=cst[:, C_EPS_LN:C_EPS_LN + 1], inc=sS)
                    std_done = sS.n
                    b.w("vector", sS, std_done)
                    b.op("vector", "reciprocal", stat2[:, p * 32 + 27: p * 32 + 28],
                         stat2[:, p * 32 + 26: p * 32 + 27], inc=sV)
                    b.w("vector", sV, sV.n)
                    if tc >= 2:
                        b.w("vector", sT, ln2_tr_done[tc - 2])
                    b.op("vector", "tensor_scalar", xn2tm[:, p * D:(p + 1) * D],
                         x1sb[:, p * D:(p + 1) * D], stat2[:, p * 32 + 24: p * 32 + 25],
                         stat2[:, p * 32 + 27: p * 32 + 28], ALU.subtract, ALU.mult, inc=sV)
                    xn2tm_done[tc] = sV.n

                def emit_ln2_transposes(tc):
                    p = tc % 2
                    b.w("tensor", sV, xn2tm_done[tc])
                    for pj in range(FC // 4):
                        r = pj % 4
                        b.w("tensor", sV, psT_free[r])
                        for half in range(4):
                            fc = 4 * pj + half
                            b.op("tensor", "matmul", psT[r][:, half * 128:(half + 1) * 128],
                                 xn2tm[:, p * D + fc * 128: p * D + (fc + 1) * 128],
                                 ident_s[:, :], start=True, stop=True,
                                 skip_group_check=True, inc=(sT if half == 3 else None))
                        mm_done = sT.n
                        b.w("vector", sT, mm_done)
                        b.op("vector", "tensor_copy",
                             xn2_r[:, 4 * pj:4 * pj + 4, tc * 128:(tc + 1) * 128],
                             psT[r][:, 0:512].rearrange("q (c t) -> q c t", c=4), inc=sV)
                        psT_free[r] = sV.n
                    ln2_tr_done[tc] = sT.n

                for tc in range(TC):
                    p = tc % 2
                    pr = tc // 2
                    if tc % 2 == 0 and pr >= 2:
                        b.w("gpsimd", sT, outproj_done[2 * (pr - 2) + 1])
                        b.op("gpsimd", "dma_start",
                             go_t[:, (pr % 2) * FC * 256:(pr % 2 + 1) * FC * 256].rearrange(
                                 "q (c j) -> q c j", c=FC),
                             go_scr[:, :, tc * 128:(tc + 2) * 128].rearrange(
                                 "c q j -> q c j"),
                             inc=sW, incn=16)
                        go_ld[tc] = go_ld[tc + 1] = sW.n
                    b.w("sync", sV, x1_evac[tc - 2] if tc >= 2 else 0)
                    b.op("sync", "dma_start", xB[:, p * D:(p + 1) * D],
                         x_d[tc * 128:(tc + 1) * 128, :], inc=sA, incn=16)
                    x_ld = sA.n
                    b.w("tensor", sW, go_ld[tc])
                    if tc == 0:
                        b.w("tensor", sW, wout_ld)
                    gbase = (pr % 2) * FC * 256 + (tc % 2) * 128
                    half_done = [0, 0]
                    for sp in range(2):
                        b.w("tensor", sV, x1_half_evac[sp][tc - 1] if tc >= 1 else 0)
                        for oc in range(FC):
                            for slab in (2 * sp, 2 * sp + 1):
                                b.op("tensor", "matmul", psO[slab][:, :],
                                     go_t[:, gbase + oc * 256: gbase + oc * 256 + 128],
                                     woutsb[:, oc * D + slab * 512: oc * D + (slab + 1) * 512],
                                     start=(oc == 0), stop=(oc == FC - 1),
                                     skip_group_check=True,
                                     inc=(sT if (oc == FC - 1 and slab == 2 * sp + 1) else None))
                        half_done[sp] = sT.n
                        b.w("vector", sT, half_done[sp])
                        if sp == 0:
                            b.w("vector", sA, x_ld)
                            if tc >= 2:
                                b.w("vector", sO, x1_store[tc - 2])
                        for slab in (2 * sp, 2 * sp + 1):
                            b.op("vector", "tensor_add",
                                 x1sb[:, p * D + slab * 512: p * D + (slab + 1) * 512],
                                 psO[slab][:, :],
                                 xB[:, p * D + slab * 512: p * D + (slab + 1) * 512],
                                 inc=(sV if slab == 2 * sp + 1 else None))
                        x1_half_evac[sp][tc] = sV.n
                    outproj_done[tc] = sT.n
                    x1_evac[tc] = sV.n
                    b.w("sync", sV, x1_evac[tc])
                    b.op("sync", "dma_start", x1_scr[tc * 128:(tc + 1) * 128, :],
                         x1sb[:, p * D:(p + 1) * D], inc=sO, incn=16)
                    x1_store[tc] = sO.n
                    if tc >= 2:
                        emit_ln2_transposes(tc - 2)
                    if tc >= 1:
                        emit_ln2_chain(tc - 1)
                emit_ln2_chain(TC - 1)
                emit_ln2_transposes(TC - 2)
                emit_ln2_transposes(TC - 1)
                xn2_done_v = sV.n
                if dbg:
                    b.w("sync", sV, xn2_done_v)
                    b.op("sync", "dma_start", xn2_dbg[:, :], xn2[:, :], inc=sO, incn=16)

            go_t_g.__exit__(None, None, None)
            woutsb_g.__exit__(None, None, None)
            barrier()
            wdsb_g = nc.sbuf_tensor("wdsb", [128, 2 * ICN * 512], BF16, side="right")
            wdsb = wdsb_g.__enter__()

            with contextlib.ExitStack() as esC:
                ec = esC.enter_context
                wgt = ec(nc.sbuf_tensor("wgt", [128, 2 * D], BF16))
                wut = ec(nc.sbuf_tensor("wut", [128, 2 * D], BF16))
                tg = ec(nc.sbuf_tensor("tg", [128, 2 * 512], F32))
                h_sb = ec(nc.sbuf_tensor("h_sb", [128, 2 * NT], BF16))
                psg0 = ec(nc.psum_tensor("psg0", [128, 512], F32))
                psg1 = ec(nc.psum_tensor("psg1", [128, 512], F32))
                psu0 = ec(nc.psum_tensor("psu0", [128, 512], F32))
                psu1 = ec(nc.psum_tensor("psu1", [128, 512], F32))
                psg = (psg0, psg1)
                psu = (psu0, psu1)
                wld = [0, 0]
                for ic in range(min(2, ICN)):
                    b.op("gpsimd", "dma_start", wgt[:, (ic % 2) * D:(ic % 2 + 1) * D],
                         wg_d[ic, :, :], inc=sW, incn=16)
                    b.op("gpsimd", "dma_start", wut[:, (ic % 2) * D:(ic % 2 + 1) * D],
                         wu_d[ic, :, :], inc=sW, incn=16)
                    wld[ic % 2] = sW.n
                gu_mm_done = [0] * ICN
                h_done = [0] * ICN
                h_store = [0] * ICN
                tg_free = [0, 0]
                psg_free = [0, 0]
                psu_free = [0, 0]
                for ic in range(ICN):
                    p = ic % 2
                    b.w("tensor", sW, wld[p])
                    if ic == 0:
                        b.w("tensor", sV, xn2_done_v)
                    for slab in range(4):
                        gp = slab % 2
                        b.w("tensor", sS, psg_free[gp])
                        for fc in range(FC):
                            b.op("tensor", "matmul", psg[gp][:, :],
                                 wgt[:, p * D + fc * 128: p * D + (fc + 1) * 128],
                                 xn2[:, fc * NT + slab * 512: fc * NT + (slab + 1) * 512],
                                 start=(fc == 0), stop=(fc == FC - 1),
                                 skip_group_check=True, inc=(sT if fc == FC - 1 else None))
                        g_mm = sT.n
                        b.w("tensor", sV, psu_free[gp])
                        for fc in range(FC):
                            b.op("tensor", "matmul", psu[gp][:, :],
                                 wut[:, p * D + fc * 128: p * D + (fc + 1) * 128],
                                 xn2[:, fc * NT + slab * 512: fc * NT + (slab + 1) * 512],
                                 start=(fc == 0), stop=(fc == FC - 1),
                                 skip_group_check=True, inc=(sT if fc == FC - 1 else None))
                        u_mm = sT.n
                        b.w("scalar", sT, g_mm)
                        b.w("scalar", sV, tg_free[gp])
                        b.op("scalar", "activation", tg[:, gp * 512:(gp + 1) * 512],
                             psg[gp][:, :], AF.Silu, scale=gs_s[:, ic:ic + 1], inc=sS)
                        t_done = sS.n
                        psg_free[gp] = sS.n
                        b.w("vector", sT, u_mm)
                        b.w("vector", sS, t_done)
                        if slab == 0 and ic >= 2:
                            b.w("vector", sO, h_store[ic - 2])
                        b.op("vector", "scalar_tensor_tensor",
                             h_sb[:, p * NT + slab * 512: p * NT + (slab + 1) * 512],
                             psu[gp][:, :], us_s[:, ic:ic + 1],
                             tg[:, gp * 512:(gp + 1) * 512], ALU.mult, ALU.mult, inc=sV)
                        tg_free[gp] = sV.n
                        psu_free[gp] = sV.n
                    gu_mm_done[ic] = sT.n
                    h_done[ic] = sV.n
                    if ic + 2 < ICN:
                        b.w("gpsimd", sT, gu_mm_done[ic])
                        b.op("gpsimd", "dma_start", wgt[:, p * D:(p + 1) * D],
                             wg_d[ic + 2, :, :], inc=sW, incn=16)
                        b.op("gpsimd", "dma_start", wut[:, p * D:(p + 1) * D],
                             wu_d[ic + 2, :, :], inc=sW, incn=16)
                        wld[p] = sW.n
                    if ic in (2, 4, 6, 8):
                        jj = (ic - 2) // 2
                        qq, piece = jj // 2, jj % 2
                        ic0, ic1 = piece * 22, (piece + 1) * 22
                        base = (qq % 2) * ICN * 512
                        b.op("gpsimd", "dma_start",
                             wdsb[:, base + ic0 * 512: base + ic1 * 512].rearrange(
                                 "q (c j) -> q c j", j=512),
                             wd_d[ic0:ic1, :, qq * 512:(qq + 1) * 512].rearrange(
                                 "c q j -> q c j"),
                             inc=sW, incn=16)
                    b.w("sync", sV, h_done[ic])
                    b.op("sync", "dma_start", h_scr[ic, :, :],
                         h_sb[:, p * NT:(p + 1) * NT], inc=sO, incn=16)
                    h_store[ic] = sO.n

            barrier()

        with contextlib.ExitStack() as esD:
            ec = esD.enter_context
            HTP = ICN * 256
            h_t = ec(nc.sbuf_tensor("h_t", [128, 4 * HTP], BF16))
            x1t = ec(nc.sbuf_tensor("x1t", [128, 2 * 512], F32))
            outsb = ec(nc.sbuf_tensor("outsb", [128, 2 * 512], F32))
            dtmp = ec(nc.sbuf_tensor("dtmp", [128, 512], F32))
            psD0 = ec(nc.psum_tensor("psD0", [128, 512], F32))
            psD1 = ec(nc.psum_tensor("psD1", [128, 512], F32))
            psD = (psD0, psD1)
            out_evac = {}
            out_store = {}
            d_mm_done = {}
            h_ld = {}
            wdq_ld = [0, 0]
            NQ = 4
            for q in range(NQ):
                if q >= 2:
                    b.w("gpsimd", sT, d_mm_done[(q - 2, TC - 1)])
                    base = (q % 2) * ICN * 512
                    for piece in range(2):
                        ic0, ic1 = piece * 22, (piece + 1) * 22
                        b.op("gpsimd", "dma_start",
                             wdsb[:, base + ic0 * 512: base + ic1 * 512].rearrange(
                                 "q (c j) -> q c j", j=512),
                             wd_d[ic0:ic1, :, q * 512:(q + 1) * 512].rearrange(
                                 "c q j -> q c j"),
                             inc=sW, incn=16)
                    wdq_ld[q % 2] = sW.n
                for tc in range(TC):
                    p = tc % 2
                    pr = tc // 2
                    pd = psD[p]
                    if tc >= 2:
                        key_prev = (q, tc - 2)
                    elif q >= 1:
                        key_prev = (q - 1, TC + tc - 2)
                    else:
                        key_prev = None
                    if tc % 2 == 0:
                        g = q * 8 + pr
                        if g >= 4:
                            qq, pp2 = divmod(g - 4, 8)
                            b.w("gpsimd", sT, d_mm_done[(qq, 2 * pp2 + 1)])
                        slot = g % 4
                        for jj in range(2):
                            ic0, ic1 = jj * 22, (jj + 1) * 22
                            b.op("gpsimd", "dma_start",
                                 h_t[:, slot * HTP + ic0 * 256:
                                     slot * HTP + ic1 * 256].rearrange(
                                     "q (c j) -> q c j", j=256),
                                 h_scr[ic0:ic1, :, tc * 128:(tc + 2) * 128].rearrange(
                                     "c q j -> q c j"),
                                 inc=sW, incn=16)
                        h_ld[(q, pr)] = sW.n
                    b.w("sync", sV, out_evac.get(key_prev, 0) if key_prev else 0)
                    b.op("sync", "dma_start", x1t[:, p * 512:(p + 1) * 512],
                         x1_scr[tc * 128:(tc + 1) * 128, q * 512:(q + 1) * 512],
                         inc=sA, incn=16)
                    x1_ld = sA.n
                    b.w("tensor", sW, h_ld[(q, pr)])
                    if q >= 2 and tc == 0:
                        b.w("tensor", sW, wdq_ld[q % 2])
                    b.w("tensor", sV, out_evac.get(key_prev, 0) if key_prev else 0)
                    hbase = ((q * 8 + pr) % 4) * HTP + (tc % 2) * 128
                    wbase = (q % 2) * ICN * 512
                    for ic in range(ICN):
                        b.op("tensor", "matmul", pd[:, :],
                             h_t[:, hbase + ic * 256: hbase + ic * 256 + 128],
                             wdsb[:, wbase + ic * 512: wbase + (ic + 1) * 512],
                             start=(ic == 0), stop=(ic == ICN - 1),
                             skip_group_check=True,
                             inc=(sT if ic == ICN - 1 else None))
                    d_mm_done[(q, tc)] = sT.n
                    b.w("vector", sT, d_mm_done[(q, tc)])
                    b.w("vector", sA, x1_ld)
                    prev_store = out_store.get((q, tc - 2) if tc >= 2 else
                                               ((q - 1, TC + tc - 2) if q >= 1 else None), 0)
                    b.w("vector", sO, prev_store)
                    b.op("vector", "tensor_mul", dtmp[:, :], pd[:, :],
                         dsb_s[:, q * 512:(q + 1) * 512])
                    b.op("vector", "tensor_add",
                         outsb[:, p * 512:(p + 1) * 512], dtmp[:, :],
                         x1t[:, p * 512:(p + 1) * 512], inc=sV)
                    out_evac[(q, tc)] = sV.n
                    b.w("sync", sV, out_evac[(q, tc)])
                    b.op("sync", "dma_start",
                         out_d[tc * 128:(tc + 1) * 128, q * 512:(q + 1) * 512],
                         outsb[:, p * 512:(p + 1) * 512], inc=sO, incn=16)
                    out_store[(q, tc)] = sO.n
        wdsb_g.__exit__(None, None, None)

        b.w("sync", sO, sO.n)

        with nc.Block() as block:
            @block.tensor
            def _(e):
                for f in b.streams["tensor"]:
                    f(e)

            @block.vector
            def _(e):
                for f in b.streams["vector"]:
                    f(e)

            @block.scalar
            def _(e):
                for f in b.streams["scalar"]:
                    f(e)

            @block.sync
            def _(e):
                for f in b.streams["sync"]:
                    f(e)

            @block.gpsimd
            def _(e):
                for f in b.streams["gpsimd"]:
                    f(e)

    return nc


def prep_inputs(L0_attn_hidden_states, L0_attn_slope_rate, L0_attn_w_qkv,
                L0_attn_w_gate, L0_attn_w_out, L0_attn_norm_w, L0_ffn_gate_w,
                L0_ffn_up_w, L0_ffn_down_w, L0_ffn_gate_s, L0_ffn_up_s, L0_ffn_down_s):
    f32 = lambda a: np.ascontiguousarray(np.asarray(a), dtype=np.float32)
    hs = f32(L0_attn_hidden_states)
    wqkv = f32(L0_attn_w_qkv)
    wgate = f32(L0_attn_w_gate)
    wout = f32(L0_attn_w_out)
    wg = f32(L0_ffn_gate_w)
    wu = f32(L0_ffn_up_w)
    wd = f32(L0_ffn_down_w)
    slope = f32(L0_attn_slope_rate).reshape(1, NH)
    nw = f32(L0_attn_norm_w)
    gsv = f32(L0_ffn_gate_s)
    usv = f32(L0_ffn_up_s)
    dsv = f32(L0_ffn_down_s)

    wqkv_img = np.ascontiguousarray(
        wqkv.reshape(NH, 3, HD, FC, 128).transpose(0, 4, 1, 3, 2).reshape(NH, 128, 3 * FC * 128))
    wgate_img = np.ascontiguousarray(
        wgate.reshape(FC, 128, FC, 128).transpose(0, 3, 2, 1).reshape(FC, 128, FC * 128))
    wout_img = np.ascontiguousarray(wout.T).reshape(FC, 128, D)
    wg_img = np.ascontiguousarray(
        wg.reshape(ICN, 128, FC, 128).transpose(0, 3, 2, 1).reshape(ICN, 128, FC * 128))
    wu_img = np.ascontiguousarray(
        wu.reshape(ICN, 128, FC, 128).transpose(0, 3, 2, 1).reshape(ICN, 128, FC * 128))
    wd_img = np.ascontiguousarray(wd.T).reshape(ICN, 128, D)

    common = {
        "wqkv": wqkv_img, "wgate": wgate_img, "wout": wout_img,
        "wgff": wg_img, "wuff": wu_img, "wdff": wd_img,
        "slopes": np.ascontiguousarray(np.broadcast_to(slope, (128, NH))),
        "nw": np.ascontiguousarray(nw.reshape(FC, 128).T),
        "gs": np.ascontiguousarray(gsv.reshape(ICN, 128).T),
        "us": np.ascontiguousarray(usv.reshape(ICN, 128).T),
        "dsb": np.ascontiguousarray(np.broadcast_to(dsv.reshape(1, D), (128, D))),
    }
    in_maps = []
    for c in range(8):
        m = dict(common)
        m["x"] = np.ascontiguousarray(hs[c])
        in_maps.append(m)
    return in_maps


def kernel(**inputs):
    in_maps = prep_inputs(**inputs)
    nc = build_core_program()
    res = run_bass_kernel_spmd(nc, in_maps, core_ids=list(range(8)))
    return np.stack([res.results[c]["out"] for c in range(8)], axis=0)


# version 20
# speedup vs baseline: 1.0633x; 1.0436x over previous
import contextlib
import numpy as np
import ml_dtypes
import concourse.bass as bass
import concourse.mybir as mybir
from concourse.bass_utils import run_bass_kernel_spmd

F32 = mybir.dt.float32
BF16 = mybir.dt.bfloat16
AF = mybir.ActivationFunctionType
ALU = mybir.AluOpType
AX = mybir.AxisListType

NT = 2048
D = 2048
FC = 16
TC = 16
NH = 16
HD = 128
BLK = 256
NB = 8
ICN = 44
INTER = 5632
BIGNEG = 40000.0

ENGS = ("tensor", "vector", "scalar", "sync", "gpsimd")

C_NEG1, C_EPS_LN, C_EPS_RMS, C_INVD, C_256 = 0, 1, 2, 3, 4


class Sem:
    def __init__(self, h):
        self.h = h
        self.n = 0


class Builder:
    def __init__(self):
        self.streams = {e: [] for e in ENGS}

    def w(self, eng, sem, val):
        if val is None or val <= 0:
            return
        self.streams[eng].append(lambda e, h=sem.h, v=val: e.wait_ge(h, v))

    def op(self, eng, method, *args, inc=None, incn=1, **kwargs):
        def f(e, m=method, a=args, k=kwargs, inc=inc, incn=incn):
            r = getattr(e, m)(*a, **k)
            if inc is not None:
                r.then_inc(inc.h, incn)
        self.streams[eng].append(f)
        if inc is not None:
            inc.n += incn
            return inc.n
        return None


def build_core_program(dbg=False):
    nc = bass.Bass("TRN2", target_bir_lowering=False)

    x_d = nc.dram_tensor("x", [NT, D], F32, kind="ExternalInput").ap()
    wqkv_d = nc.dram_tensor("wqkv", [NH, 128, 3 * FC * 128], F32, kind="ExternalInput").ap()
    wgate_d = nc.dram_tensor("wgate", [FC, 128, FC * 128], F32, kind="ExternalInput").ap()
    wout_d = nc.dram_tensor("wout", [FC, 128, D], F32, kind="ExternalInput").ap()
    wg_d = nc.dram_tensor("wgff", [ICN, 128, FC * 128], F32, kind="ExternalInput").ap()
    wu_d = nc.dram_tensor("wuff", [ICN, 128, FC * 128], F32, kind="ExternalInput").ap()
    wd_d = nc.dram_tensor("wdff", [ICN, 128, D], F32, kind="ExternalInput").ap()
    slopes_d = nc.dram_tensor("slopes", [128, NH], F32, kind="ExternalInput").ap()
    nw_d = nc.dram_tensor("nw", [128, FC], F32, kind="ExternalInput").ap()
    gs_d = nc.dram_tensor("gs", [128, ICN], F32, kind="ExternalInput").ap()
    us_d = nc.dram_tensor("us", [128, ICN], F32, kind="ExternalInput").ap()
    dsb_d = nc.dram_tensor("dsb", [128, D], F32, kind="ExternalInput").ap()

    out_d = nc.dram_tensor("out", [NT, D], F32, kind="ExternalOutput").ap()
    scratch_kind = "ExternalOutput" if dbg else "Internal"
    if dbg:
        xn1_dbg = nc.dram_tensor("xn1_dbg", [128, FC * NT], BF16, kind="ExternalOutput").ap()
        xn2_dbg = nc.dram_tensor("xn2_dbg", [128, FC * NT], BF16, kind="ExternalOutput").ap()
        istd_dbg = nc.dram_tensor("istd_dbg", [128, NT], F32, kind="ExternalOutput").ap()
    o_scr = nc.dram_tensor("o_scr", [NH, 128, NT], BF16, kind=scratch_kind).ap()
    go_scr = nc.dram_tensor("go_scr", [FC, 128, NT], BF16, kind=scratch_kind).ap()
    x1_scr = nc.dram_tensor("x1_scr", [NT, D], F32, kind=scratch_kind).ap()
    o2_scr = nc.dram_tensor("o2_scr", [NH, 128, NT], BF16, kind="Internal").ap()
    h_scr = nc.dram_tensor("h_scr", [ICN, 128, NT], BF16, kind=scratch_kind).ap()

    ident_np = np.eye(128, dtype=ml_dtypes.bfloat16)
    ones_np = np.ones((128, 128), dtype=ml_dtypes.bfloat16)
    consts_np = np.broadcast_to(
        np.array([-1.0, 1e-5, 1e-6, 1.0 / 2048.0, 256.0, 0.0, 1.0, 0.0], dtype=np.float32),
        (128, 8)).copy()
    arr_np = np.broadcast_to(np.arange(1, BLK + 1, dtype=np.float32), (128, BLK)).copy()
    karr_np = np.empty((128, 2), dtype=np.float32)
    for sub in range(2):
        karr_np[:, sub] = 255.0 - (sub * 128 + np.arange(128))
    idxt_np = np.empty((128, 2 * BLK), dtype=np.float32)
    for sub in range(2):
        tk = sub * 128 + np.arange(128)[:, None]
        tq = np.arange(BLK)[None, :]
        dif = tq - tk
        idxt_np[:, sub * BLK:(sub + 1) * BLK] = np.where(dif >= 0, dif, BIGNEG)
    ident_dr = nc.inline_tensor(ident_np, name="identc").ap()
    ones_dr = nc.inline_tensor(ones_np, name="onesc").ap()
    consts_dr = nc.inline_tensor(consts_np, name="constsc").ap()
    arr_dr = nc.inline_tensor(arr_np, name="arrc").ap()
    karr_dr = nc.inline_tensor(karr_np, name="karrc").ap()
    idxt_dr = nc.inline_tensor(idxt_np, name="idxtc").ap()

    b = Builder()

    with contextlib.ExitStack() as es0:
        ec = es0.enter_context
        hT = ec(nc.semaphore("sT")); hV = ec(nc.semaphore("sV")); hS = ec(nc.semaphore("sS"))
        hW = ec(nc.semaphore("sW")); hA = ec(nc.semaphore("sA")); hO = ec(nc.semaphore("sO"))
        cst = ec(nc.sbuf_tensor("cst", [128, 8], F32))
        ident_s = ec(nc.sbuf_tensor("ident_s", [128, 128], BF16))
        ones_s = ec(nc.sbuf_tensor("ones_s", [128, 128], BF16))
        slopes_s = ec(nc.sbuf_tensor("slopes_s", [128, NH], F32))
        nw_s = ec(nc.sbuf_tensor("nw_s", [128, FC], F32))
        gs_s = ec(nc.sbuf_tensor("gs_s", [128, ICN], F32))
        us_s = ec(nc.sbuf_tensor("us_s", [128, ICN], F32))
        dsb_s = ec(nc.sbuf_tensor("dsb_s", [128, D], F32))
        sT, sV, sS = Sem(hT), Sem(hV), Sem(hS)
        sW, sA, sO = Sem(hW), Sem(hA), Sem(hO)
        sems = {"sT": sT, "sV": sV, "sS": sS, "sW": sW, "sA": sA, "sO": sO}
        own = {"tensor": {"sT"}, "vector": {"sV"}, "scalar": {"sS"},
               "sync": set(), "gpsimd": set()}

        def barrier(skips=None):
            snap = {k: s.n for k, s in sems.items()}
            for eng in ENGS:
                for k, s in sems.items():
                    if k in own[eng]:
                        continue
                    if skips and k in skips.get(eng, ()):
                        continue
                    b.w(eng, s, snap[k])

        for dst, src in ((cst, consts_dr), (ident_s, ident_dr), (ones_s, ones_dr),
                         (slopes_s, slopes_d), (nw_s, nw_d), (gs_s, gs_d),
                         (us_s, us_d), (dsb_s, dsb_d)):
            b.op("gpsimd", "dma_start", dst[:, :], src[:, :], inc=sW, incn=16)
        consts_ld0 = sW.n

        with nc.sbuf_tensor("xn1", [128, FC * NT], BF16) as xn1:
            with contextlib.ExitStack() as esA:
                ec = esA.enter_context
                arr_s = ec(nc.sbuf_tensor("arr_s", [128, BLK], F32))
                karr_s = ec(nc.sbuf_tensor("karr_s", [128, 2], F32))
                idxt_s = ec(nc.sbuf_tensor("idxt_s", [128, 2 * BLK], F32))
                qdec = ec(nc.sbuf_tensor("qdec", [128, NH * BLK], BF16))
                ddt = ec(nc.sbuf_tensor("ddt", [128, NH * 2 * BLK], BF16))
                kdec = ec(nc.sbuf_tensor("kdec", [128, NH * 2], F32))
                bdall = ec(nc.sbuf_tensor("bdall", [128, NH], F32))
                dtmpA = ec(nc.sbuf_tensor("dtmpA", [128, 2 * BLK], F32))
                wsb_g = nc.sbuf_tensor("wsb", [128, 2 * 3 * FC * 128], BF16,
                                       side="right")
                wsb = wsb_g.__enter__()
                xA = ec(nc.sbuf_tensor("xA", [128, 2 * D], F32))
                xntm = ec(nc.sbuf_tensor("xntm", [128, 2 * D], BF16))
                stat = ec(nc.sbuf_tensor("stat", [128, 64], F32))
                q_sb = ec(nc.sbuf_tensor("q_sb", [128, 2 * NT], BF16))
                k_sb = ec(nc.sbuf_tensor("k_sb", [128, 2 * NT], BF16))
                v_sb = ec(nc.sbuf_tensor("v_sb", [128, 2 * NT], BF16))
                kkT = ec(nc.sbuf_tensor("kkT", [128, 2 * NT], BF16))
                vT = ec(nc.sbuf_tensor("vT", [128, 2 * NT], BF16))
                qd_sb = ec(nc.sbuf_tensor("qd_sb", [128, 2 * BLK], BF16))
                qkTd = ec(nc.sbuf_tensor("qkTd", [128, 2 * 2 * BLK], BF16))
                kv_sb = ec(nc.sbuf_tensor("kv_sb", [128, HD], F32))
                kv_bf = ec(nc.sbuf_tensor("kv_bf", [128, HD], BF16))
                o_sb = ec(nc.sbuf_tensor("o_sb", [128, 2 * NT], BF16))
                o2st = ec(nc.sbuf_tensor("o2st", [128, NT], BF16))
                pp0 = ec(nc.psum_tensor("pp0", [128, 512], F32))
                pp1 = ec(nc.psum_tensor("pp1", [128, 512], F32))
                qk0 = ec(nc.psum_tensor("qk0", [128, 512], F32))
                qk1 = ec(nc.psum_tensor("qk1", [128, 512], F32))
                po0 = ec(nc.psum_tensor("po0", [128, BLK], F32))
                po1 = ec(nc.psum_tensor("po1", [128, BLK], F32))
                pkv0 = ec(nc.psum_tensor("pkv0", [128, HD], F32))
                pkv1 = ec(nc.psum_tensor("pkv1", [128, HD], F32))
                pp = (pp0, pp1)
                qkp = (qk0, qk1)
                pop = (po0, po1)
                pkvp = (pkv0, pkv1)

                for dst, src in ((arr_s, arr_dr), (karr_s, karr_dr), (idxt_s, idxt_dr)):
                    b.op("gpsimd", "dma_start", dst[:, :], src[:, :], inc=sW, incn=16)
                consts_ld = sW.n
                b.w("vector", sW, consts_ld)
                b.w("scalar", sW, consts_ld)
                b.op("vector", "tensor_scalar", dtmpA[:, 0:NH], slopes_s[:, :],
                     cst[:, C_NEG1:C_NEG1 + 1], None, ALU.mult)
                b.op("vector", "tensor_scalar", dtmpA[:, NH:2 * NH], slopes_s[:, :],
                     cst[:, C_256:C_256 + 1], None, ALU.mult, inc=sV)
                negs_done = sV.n
                b.w("scalar", sV, negs_done)
                b.op("scalar", "activation", bdall[:, :], dtmpA[:, NH:2 * NH], AF.Exp,
                     scale=cst[:, C_NEG1:C_NEG1 + 1], inc=sS)
                for h in range(NH):
                    nscol = dtmpA[:, h:h + 1]
                    b.op("scalar", "activation", qdec[:, h * BLK:(h + 1) * BLK],
                         arr_s[:, :], AF.Exp, scale=nscol)
                    b.op("scalar", "activation", kdec[:, 2 * h:2 * h + 2],
                         karr_s[:, :], AF.Exp, scale=nscol)
                    b.op("scalar", "activation", ddt[:, h * 2 * BLK:(h + 1) * 2 * BLK],
                         idxt_s[:, :], AF.Exp, scale=nscol,
                         inc=(sS if h == NH - 1 else None))
                decay_done_s = sS.n

                pp_free_v = [0, 0]
                pp_free_s = [0, 0]
                tr4 = (qk0, qk1, po0, po1)
                tr4_free = [0, 0, 0, 0]
                w_ld = [0] * NH
                proj_done = [0] * NH
                attn_mm_done = [0] * NH
                qread_done = [0] * NH
                o_store = [0] * NH
                o2_store = [0] * NH
                o_sq_done = [0] * NH
                ev_k_done = [0] * NH
                ev_v_done = [0] * NH
                ev_all_done = [0] * NH
                for hh in range(min(2, NH)):
                    b.op("gpsimd", "dma_start",
                         wsb[:, (hh % 2) * 6144:(hh % 2 + 1) * 6144],
                         wqkv_d[hh, :, :], inc=sW, incn=16)
                    w_ld[hh] = sW.n

                grp = [0]
                pkv_free = [0, 0]
                o_evac = {}
                qkTd_done = {}
                qd_done = {}
                kv_bf_done = {}
                scalar_head_gate = [False] * NH

                def emit_proj_group(hh, part, slab, xn1_wait=None):
                    hp = hh % 2
                    pi = grp[0] % 2
                    grp[0] += 1
                    if part == 0 and slab == 0:
                        b.w("tensor", sW, w_ld[hh])
                    if xn1_wait is not None:
                        b.w("tensor", sV, xn1_wait)
                    b.w("tensor", sS, pp_free_s[pi])
                    b.w("tensor", sV, pp_free_v[pi])
                    for fc in range(FC):
                        b.op("tensor", "matmul", pp[pi][:, :],
                             wsb[:, hp * 6144 + (part * FC + fc) * 128:
                                 hp * 6144 + (part * FC + fc + 1) * 128],
                             xn1[:, fc * NT + slab * 512: fc * NT + (slab + 1) * 512],
                             start=(fc == 0), stop=(fc == FC - 1),
                             skip_group_check=True,
                             inc=(sT if fc == FC - 1 else None))
                    mm_done = sT.n
                    if not scalar_head_gate[hh]:
                        scalar_head_gate[hh] = True
                        if hh >= 2:
                            b.w("scalar", sT, attn_mm_done[hh - 2])
                            b.w("scalar", sV, qread_done[hh - 2])
                    b.w("scalar", sT, mm_done)
                    dst = (q_sb, k_sb, v_sb)[part]
                    b.op("scalar", "activation",
                         dst[:, hp * NT + slab * 512: hp * NT + (slab + 1) * 512],
                         pp[pi][:, :], AF.Silu, inc=sS)
                    pp_free_s[pi] = sS.n
                    if part == 1 and slab == 3:
                        ev_k_done[hh] = sS.n
                    if part == 2 and slab == 3:
                        ev_v_done[hh] = sS.n
                        ev_all_done[hh] = sS.n
                    if part == 2 and slab == 3:
                        proj_done[hh] = sT.n

                x_free = [0, 0]
                xn_free = [0, 0]
                for tcp in range(TC // 2):
                    tcs = (2 * tcp, 2 * tcp + 1)
                    lds = [0, 0]
                    for i, tc in enumerate(tcs):
                        p = tc % 2
                        b.w("sync", sW, consts_ld0)
                        b.w("sync", sV, x_free[p])
                        b.op("sync", "dma_start", xA[:, p * D:(p + 1) * D],
                             x_d[tc * 128:(tc + 1) * 128, :], inc=sA, incn=16)
                        lds[i] = sA.n
                    for i, tc in enumerate(tcs):
                        p = tc % 2
                        b.w("vector", sA, lds[i])
                        for j in range(4):
                            b.op("vector", "bn_stats",
                                 stat[:, p * 32 + j * 6: p * 32 + (j + 1) * 6],
                                 xA[:, p * D + j * 512: p * D + (j + 1) * 512],
                                 inc=(sV if j == 3 else None))
                    b.w("vector", sV, sV.n)
                    for tc in tcs:
                        p = tc % 2
                        b.op("vector", "bn_aggr", stat[:, p * 32 + 24: p * 32 + 26],
                             stat[:, p * 32: p * 32 + 24], inc=(sV if p == 1 else None))
                    var_done = sV.n
                    b.w("scalar", sV, var_done)
                    for tc in tcs:
                        p = tc % 2
                        b.op("scalar", "activation", stat[:, p * 32 + 26: p * 32 + 27],
                             stat[:, p * 32 + 25: p * 32 + 26], AF.Sqrt,
                             bias=cst[:, C_EPS_LN:C_EPS_LN + 1],
                             inc=(sS if p == 1 else None))
                    std_done = sS.n
                    b.w("vector", sS, std_done)
                    for tc in tcs:
                        p = tc % 2
                        b.op("vector", "reciprocal", stat[:, p * 32 + 27: p * 32 + 28],
                             stat[:, p * 32 + 26: p * 32 + 27],
                             inc=(sV if p == 1 else None))
                    b.w("vector", sV, sV.n)
                    for tc in tcs:
                        p = tc % 2
                        b.w("vector", sT, xn_free[p])
                        b.op("vector", "tensor_scalar", xntm[:, p * D:(p + 1) * D],
                             xA[:, p * D:(p + 1) * D], stat[:, p * 32 + 24: p * 32 + 25],
                             stat[:, p * 32 + 27: p * 32 + 28], ALU.subtract, ALU.mult,
                             inc=sV)
                        if tc == tcs[0]:
                            xn_done_a = sV.n
                        else:
                            xn_done_b = sV.n
                        x_free[p] = sV.n
                    for i, tc in enumerate(tcs):
                        p = tc % 2
                        b.w("tensor", sV, xn_done_a if i == 0 else xn_done_b)
                        xn1_r = xn1[:, :].rearrange("q (c t) -> q c t", c=FC)
                        for pj in range(FC // 2):
                            r = pj % 4
                            b.w("tensor", sV, tr4_free[r])
                            for half in range(2):
                                fc = 2 * pj + half
                                b.op("tensor", "matmul",
                                     tr4[r][:, half * 128:(half + 1) * 128],
                                     xntm[:, p * D + fc * 128: p * D + (fc + 1) * 128],
                                     ident_s[:, :], start=True, stop=True,
                                     skip_group_check=True,
                                     inc=(sT if half == 1 else None))
                            mm_done = sT.n
                            b.w("vector", sT, mm_done)
                            b.op("vector", "tensor_copy",
                                 xn1_r[:, 2 * pj:2 * pj + 2, tc * 128:(tc + 1) * 128],
                                 tr4[r][:, 0:256].rearrange("q (c t) -> q c t", c=2),
                                 inc=sV)
                            tr4_free[r] = sV.n
                        xn_free[p] = sT.n
                    if tcs[1] % 4 == 3:
                        sl = tcs[1] // 4
                        xn1_slab = sV.n
                        for part in range(3):
                            emit_proj_group(0, part, sl, xn1_wait=xn1_slab)
                xn1_done_v = sV.n
                xn1_done_s = sS.n
                qk_free0 = [tr4_free[0], tr4_free[1]]
                po_free0 = [tr4_free[2], tr4_free[3]]

                qk_free = list(qk_free0)
                po_free = list(po_free0)

                def emit_transposes(hh):
                    hp = hh % 2
                    if hh >= 2:
                        b.w("vector", sT, attn_mm_done[hh - 2])
                    b.w("tensor", sS, ev_k_done[hh])
                    for j in range(2 * NB):
                        r = j % 2
                        b.w("tensor", sV, po_free[r])
                        b.op("tensor", "matmul", pop[r][:, 0:128],
                             k_sb[:, hp * NT + j * 128: hp * NT + (j + 1) * 128],
                             ident_s[:, :], start=True, stop=True,
                             skip_group_check=True, inc=sT)
                        mm_done = sT.n
                        sub = j % 2
                        b.w("vector", sT, mm_done)
                        b.op("vector", "tensor_scalar",
                             kkT[:, hp * NT + j * 128: hp * NT + (j + 1) * 128],
                             pop[r][:, 0:128], kdec[:, 2 * hh + sub:2 * hh + sub + 1],
                             None, ALU.mult, inc=sV)
                        po_free[r] = sV.n
                    b.w("tensor", sS, ev_v_done[hh])
                    for pj in range(NB):
                        r = pj % 2
                        b.w("tensor", sV, po_free[r])
                        for half in range(2):
                            j = 2 * pj + half
                            b.op("tensor", "matmul",
                                 pop[r][:, half * 128:(half + 1) * 128],
                                 v_sb[:, hp * NT + j * 128: hp * NT + (j + 1) * 128],
                                 ident_s[:, :], start=True, stop=True,
                                 skip_group_check=True,
                                 inc=(sT if half == 1 else None))
                        mm_done = sT.n
                        b.w("vector", sT, mm_done)
                        b.op("vector", "tensor_copy",
                             vT[:, hp * NT + pj * 256: hp * NT + (pj + 1) * 256],
                             pop[r][:, 0:256], inc=sV)
                        po_free[r] = sV.n

                emit_transposes(0)

                for h in range(NH):
                    hp = h % 2
                    qkT_done = {}
                    o_done = {}
                    kvmm_done = {}
                    groups = ([(part, slab) for part in range(3) for slab in range(4)]
                              if h + 1 < NH else [])
                    gi = [0]

                    def drain_groups(target):
                        while gi[0] < min(target, len(groups)):
                            part, slab = groups[gi[0]]
                            emit_proj_group(h + 1, part, slab)
                            gi[0] += 1

                    def emit_qkT(bb, h=h, hp=hp, qkT_done=qkT_done):
                        pi = bb % 2
                        b.w("tensor", sV, qk_free[pi])
                        for sub in range(2):
                            b.op("tensor", "matmul",
                                 qkp[pi][:, sub * BLK:(sub + 1) * BLK],
                                 k_sb[:, hp * NT + bb * BLK + sub * 128:
                                      hp * NT + bb * BLK + (sub + 1) * 128],
                                 q_sb[:, hp * NT + bb * BLK: hp * NT + (bb + 1) * BLK],
                                 start=True, stop=True, skip_group_check=True,
                                 inc=(sT if sub == 1 else None))
                        qkT_done[bb] = sT.n

                    emit_qkT(0)
                    emit_qkT(1)
                    b.w("vector", sS, ev_all_done[h])
                    b.op("vector", "tensor_mul", qd_sb[:, 0:BLK],
                         q_sb[:, hp * NT: hp * NT + BLK],
                         qdec[:, h * BLK:(h + 1) * BLK], inc=sV)
                    qd_done[(h, 0)] = sV.n
                    b.w("vector", sT, qkT_done[0])
                    b.op("vector", "tensor_mul", qkTd[:, 0:512],
                         qkp[0][:, :], ddt[:, h * 2 * BLK:(h + 1) * 2 * BLK], inc=sV)
                    qkTd_done[(h, 0)] = sV.n
                    qk_free[0] = sV.n

                    for bb in range(NB):
                        pi = bb % 2
                        b.w("tensor", sV, po_free[pi])
                        b.w("tensor", sV, qkTd_done[(h, bb)])
                        if bb > 0:
                            b.w("tensor", sV, kv_bf_done[(h, bb - 1)])
                            b.w("tensor", sV, qd_done[(h, bb)])
                            b.op("tensor", "matmul", pop[pi][:, :], kv_bf[:, :],
                                 qd_sb[:, pi * BLK:(pi + 1) * BLK], start=True,
                                 stop=False, skip_group_check=True)
                        for sub in range(2):
                            b.op("tensor", "matmul", pop[pi][:, :],
                                 vT[:, hp * NT + (2 * bb + sub) * 128:
                                     hp * NT + (2 * bb + sub + 1) * 128],
                                 qkTd[:, pi * 512 + sub * BLK: pi * 512 + (sub + 1) * BLK],
                                 start=(bb == 0 and sub == 0), stop=(sub == 1),
                                 skip_group_check=True, inc=(sT if sub == 1 else None))
                        o_done[bb] = sT.n
                        b.w("tensor", sV, pkv_free[pi])
                        for sub in range(2):
                            b.op("tensor", "matmul", pkvp[pi][:, :],
                                 kkT[:, hp * NT + (2 * bb + sub) * 128:
                                     hp * NT + (2 * bb + sub + 1) * 128],
                                 vT[:, hp * NT + (2 * bb + sub) * 128:
                                     hp * NT + (2 * bb + sub + 1) * 128],
                                 start=(sub == 0), stop=(sub == 1),
                                 skip_group_check=True, inc=(sT if sub == 1 else None))
                        kvmm_done[bb] = sT.n
                        drain_groups(((bb + 1) * 12 + 7) // 8)
                        b.w("vector", sT, kvmm_done[bb])
                        if bb == 0:
                            b.op("vector", "tensor_copy", kv_bf[:, :], pkvp[pi][:, :],
                                 inc=sV)
                            kv_bf_done[(h, bb)] = sV.n
                            b.op("vector", "tensor_copy", kv_sb[:, :], pkvp[pi][:, :],
                                 inc=sV)
                        else:
                            b.op("vector", "scalar_tensor_tensor", kv_bf[:, :],
                                 kv_sb[:, :], bdall[:, h:h + 1], pkvp[pi][:, :],
                                 ALU.mult, ALU.add, inc=sV)
                            kv_bf_done[(h, bb)] = sV.n
                            b.op("vector", "scalar_tensor_tensor", kv_sb[:, :],
                                 kv_sb[:, :], bdall[:, h:h + 1], pkvp[pi][:, :],
                                 ALU.mult, ALU.add, inc=sV)
                        pkv_free[pi] = sV.n
                        if h >= 2 and bb == 0:
                            b.w("vector", sO, o_store[h - 2])
                        b.w("vector", sT, o_done[bb])
                        b.op("vector", "tensor_copy",
                             o_sb[:, hp * NT + bb * BLK: hp * NT + (bb + 1) * BLK],
                             pop[pi][:, :], inc=sV)
                        o_evac[(h, bb)] = sV.n
                        po_free[pi] = sV.n
                        if bb + 1 < NB:
                            b.w("vector", sT, o_done.get(bb - 1, 0))
                            b.op("vector", "tensor_mul",
                                 qd_sb[:, ((bb + 1) % 2) * BLK:((bb + 1) % 2 + 1) * BLK],
                                 q_sb[:, hp * NT + (bb + 1) * BLK:
                                      hp * NT + (bb + 2) * BLK],
                                 qdec[:, h * BLK:(h + 1) * BLK], inc=sV)
                            qd_done[(h, bb + 1)] = sV.n
                        if bb + 1 < NB:
                            pj = (bb + 1) % 2
                            b.w("vector", sT, qkT_done[bb + 1])
                            b.op("vector", "tensor_mul",
                                 qkTd[:, pj * 512:(pj + 1) * 512], qkp[pj][:, :],
                                 ddt[:, h * 2 * BLK:(h + 1) * 2 * BLK], inc=sV)
                            qkTd_done[(h, bb + 1)] = sV.n
                            qk_free[pj] = sV.n
                        if bb + 2 < NB:
                            emit_qkT(bb + 2)
                    attn_mm_done[h] = sT.n
                    qread_done[h] = qd_done[(h, NB - 1)]
                    drain_groups(len(groups))
                    b.w("sync", sV, o_evac[(h, NB - 1)])
                    b.op("sync", "dma_start", o_scr[h, :, :],
                         o_sb[:, hp * NT:(hp + 1) * NT], inc=sO, incn=16)
                    b.w("scalar", sV, o_evac[(h, NB - 1)])
                    if h >= 1:
                        b.w("scalar", sO, o2_store[h - 1])
                    b.op("scalar", "activation", o2st[:, :],
                         o_sb[:, hp * NT:(hp + 1) * NT], AF.Square, inc=sS)
                    o_sq_done[h] = sS.n
                    b.w("sync", sS, o_sq_done[h])
                    b.op("sync", "dma_start", o2_scr[h, :, :],
                         o2st[:, :], inc=sO, incn=16)
                    o2_store[h] = sO.n
                    o_store[h] = sO.n
                    if h + 1 < NH:
                        emit_transposes(h + 1)
                    if h + 2 < NH:
                        b.w("gpsimd", sT, proj_done[h])
                        b.op("gpsimd", "dma_start",
                             wsb[:, ((h + 2) % 2) * 6144:((h + 2) % 2 + 1) * 6144],
                             wqkv_d[h + 2, :, :], inc=sW, incn=16)
                        w_ld[h + 2] = sW.n
                    if h == NH - 2:
                        b.w("gpsimd", sT, proj_done[NH - 2])
                        for wo in range(2):
                            b.op("gpsimd", "dma_start",
                                 wsb[:, ((NH - 2) % 2) * 6144 + wo * 2048:
                                     ((NH - 2) % 2) * 6144 + (wo + 1) * 2048],
                                 wgate_d[wo, :, :], inc=sW, incn=16)
                        wg_pre = sW.n

            wsb_g.__exit__(None, None, None)
            if dbg:
                b.w("sync", sV, xn1_done_v)
                b.w("sync", sS, xn1_done_s)
                b.op("sync", "dma_start", xn1_dbg[:, :], xn1[:, :], inc=sO, incn=16)
            barrier()
            woutsb_g = nc.sbuf_tensor("woutsb", [128, FC * D], BF16, side="right")
            woutsb = woutsb_g.__enter__()
            go_t_g = nc.sbuf_tensor("go_t", [128, 2 * FC * 256], BF16, side="right")
            go_t = go_t_g.__enter__()

            with contextlib.ExitStack() as esB:
                ec = esB.enter_context
                o_in = ec(nc.sbuf_tensor("o_in", [128, 2 * NT], BF16))
                o2_sb = ec(nc.sbuf_tensor("o2_sb", [128, 2 * NT], BF16))
                wgsb = ec(nc.sbuf_tensor("wgsb", [128, 2 * D], BF16))
                gatef = ec(nc.sbuf_tensor("gatef", [128, 2 * 512], F32))
                gotmp = ec(nc.sbuf_tensor("gotmp", [128, 512], F32))
                go_sb = ec(nc.sbuf_tensor("go_sb", [128, 2 * NT], BF16))
                istd_b = ec(nc.sbuf_tensor("istd_b", [128, NT], F32))
                psR0 = ec(nc.psum_tensor("psR0", [128, 512], F32))
                psR1 = ec(nc.psum_tensor("psR1", [128, 512], F32))
                psR2 = ec(nc.psum_tensor("psR2", [128, 512], F32))
                psR3 = ec(nc.psum_tensor("psR3", [128, 512], F32))
                psG0 = ec(nc.psum_tensor("psG0", [128, 512], F32))
                psG1 = ec(nc.psum_tensor("psG1", [128, 512], F32))
                psR = (psR0, psR1, psR2, psR3)
                psG = (psG0, psG1)

                sq_free_s = [0, 0]
                sq_free_t = [0, 0]
                o_lds = [0] * NH
                for oc in range(min(2, NH)):
                    b.op("sync", "dma_start", o2_sb[:, (oc % 2) * NT:(oc % 2 + 1) * NT],
                         o2_scr[oc, :, :], inc=sA, incn=16)
                    o_lds[oc] = sA.n
                for oc in range(NH):
                    p = oc % 2
                    b.w("tensor", sA, o_lds[oc])
                    for slab in range(4):
                        b.op("tensor", "matmul", psR[slab][:, :], ones_s[:, :],
                             o2_sb[:, p * NT + slab * 512: p * NT + (slab + 1) * 512],
                             start=(oc == 0), stop=(oc == NH - 1),
                             skip_group_check=True, inc=(sT if slab == 3 else None))
                    sq_free_t[p] = sT.n
                    if oc + 2 < NH:
                        b.w("sync", sT, sq_free_t[p])
                        b.op("sync", "dma_start", o2_sb[:, p * NT:(p + 1) * NT],
                             o2_scr[oc + 2, :, :], inc=sA, incn=16)
                        o_lds[oc + 2] = sA.n
                rms_done = sT.n
                b.w("scalar", sT, rms_done)
                for slab in range(4):
                    b.op("scalar", "activation", istd_b[:, slab * 512:(slab + 1) * 512],
                         psR[slab][:, :], AF.Sqrt, scale=cst[:, C_INVD:C_INVD + 1],
                         bias=cst[:, C_EPS_RMS:C_EPS_RMS + 1],
                         inc=(sS if slab == 3 else None))
                sqrt_done = sS.n
                b.w("vector", sS, sqrt_done)
                b.op("vector", "reciprocal", istd_b[:, :], istd_b[:, :], inc=sV)
                istd_done = sV.n
                b.w("vector", sV, istd_done)

                wg_ld = [wg_pre, wg_pre]
                b.op("gpsimd", "dma_start", woutsb[:, 0:D], wout_d[0, :, :],
                     inc=sW, incn=16)
                go_done = [0] * FC
                go_store = [0] * FC
                gate_mm_done = [0] * FC
                sig_free = [0, 0]
                gate_read = [0, 0]
                o_lds2 = [0] * FC
                for oc in range(min(2, FC)):
                    b.w("sync", sS, sq_free_s[oc % 2])
                    b.op("sync", "dma_start", o_in[:, (oc % 2) * NT:(oc % 2 + 1) * NT],
                         o_scr[oc, :, :], inc=sA, incn=16)
                    o_lds2[oc] = sA.n
                wsb_base = ((NH - 2) % 2) * 6144
                for oc in range(FC):
                    p = oc % 2
                    o_ld2 = o_lds2[oc]
                    b.w("tensor", sW, wg_ld[p])
                    wg_src = (wsb[:, wsb_base + oc * 2048: wsb_base + (oc + 1) * 2048]
                              if oc < 2 else wgsb[:, p * D:(p + 1) * D])
                    for slab in range(4):
                        gp = slab % 2
                        b.w("tensor", sS, sig_free[gp])
                        for fc in range(FC):
                            b.op("tensor", "matmul", psG[gp][:, :],
                                 wg_src[:, fc * 128:(fc + 1) * 128],
                                 xn1[:, fc * NT + slab * 512: fc * NT + (slab + 1) * 512],
                                 start=(fc == 0), stop=(fc == FC - 1),
                                 skip_group_check=True, inc=(sT if fc == FC - 1 else None))
                        g_done = sT.n
                        b.w("scalar", sT, g_done)
                        b.w("scalar", sV, gate_read[gp])
                        b.op("scalar", "activation", gatef[:, gp * 512:(gp + 1) * 512],
                             psG[gp][:, :], AF.Sigmoid, inc=sS)
                        sig_done = sS.n
                        sig_free[gp] = sS.n
                        b.w("vector", sS, sig_done)
                        if slab == 0:
                            b.w("vector", sA, o_ld2)
                            if oc == 0:
                                b.w("vector", sV, istd_done)
                            if oc >= 2:
                                b.w("vector", sO, go_store[oc - 2])
                        b.op("vector", "scalar_tensor_tensor", gotmp[:, :],
                             gatef[:, gp * 512:(gp + 1) * 512], nw_s[:, oc:oc + 1],
                             o_in[:, p * NT + slab * 512: p * NT + (slab + 1) * 512],
                             ALU.mult, ALU.mult)
                        b.op("vector", "tensor_mul",
                             go_sb[:, p * NT + slab * 512: p * NT + (slab + 1) * 512],
                             gotmp[:, :], istd_b[:, slab * 512:(slab + 1) * 512], inc=sV)
                        gate_read[gp] = sV.n
                    gate_mm_done[oc] = sT.n
                    go_done[oc] = sV.n
                    if oc + 2 < FC:
                        b.w("sync", sV, go_done[oc])
                        b.op("sync", "dma_start", o_in[:, p * NT:(p + 1) * NT],
                             o_scr[oc + 2, :, :], inc=sA, incn=16)
                        o_lds2[oc + 2] = sA.n
                    if oc + 2 < FC:
                        b.w("gpsimd", sT, gate_mm_done[oc])
                        b.op("gpsimd", "dma_start", wgsb[:, p * D:(p + 1) * D],
                             wgate_d[oc + 2, :, :], inc=sW, incn=16)
                        wg_ld[p] = sW.n
                    for wo in (2 * oc + 1, 2 * oc + 2):
                        if wo < FC:
                            if wo == 10:
                                b.w("gpsimd", sT, gate_mm_done[1])
                            b.op("gpsimd", "dma_start",
                                 woutsb[:, wo * D:(wo + 1) * D],
                                 wout_d[wo, :, :], inc=sW, incn=16)
                    b.w("sync", sV, go_done[oc])
                    b.op("sync", "dma_start", go_scr[oc, :, :],
                         go_sb[:, p * NT:(p + 1) * NT], inc=sO, incn=16)
                    go_store[oc] = sO.n
                wout_ld = sW.n
                if dbg:
                    b.w("sync", sV, istd_done)
                    b.op("sync", "dma_start", istd_dbg[:, :], istd_b[:, :], inc=sO, incn=16)

            barrier(skips={"tensor": {"sO", "sA"}})

        with nc.sbuf_tensor("xn2", [128, FC * NT], BF16) as xn2:
            with contextlib.ExitStack() as esB2:
                ec = esB2.enter_context

                xB = ec(nc.sbuf_tensor("xB", [128, 2 * D], F32))
                x1sb = ec(nc.sbuf_tensor("x1sb", [128, 2 * D], F32))
                xn2tm = ec(nc.sbuf_tensor("xn2tm", [128, 2 * D], BF16))
                sqd2 = ec(nc.sbuf_tensor("sqd2", [128, D], BF16))
                stat2 = ec(nc.sbuf_tensor("stat2", [128, 64], F32))
                psO0 = ec(nc.psum_tensor("psO0", [128, 512], F32))
                psO1 = ec(nc.psum_tensor("psO1", [128, 512], F32))
                psO2 = ec(nc.psum_tensor("psO2", [128, 512], F32))
                psO3 = ec(nc.psum_tensor("psO3", [128, 512], F32))
                psT0 = ec(nc.psum_tensor("psT0", [128, 512], F32))
                psT1 = ec(nc.psum_tensor("psT1", [128, 512], F32))
                psT2 = ec(nc.psum_tensor("psT2", [128, 512], F32))
                psT3 = ec(nc.psum_tensor("psT3", [128, 512], F32))
                psO = (psO0, psO1, psO2, psO3)
                psT = (psT0, psT1, psT2, psT3)

                outproj_done = [0] * TC
                x1_evac = [0] * TC
                x1_store = [0] * TC
                xn2tm_done = [0] * TC
                ln2_tr_done = [0] * TC
                go_ld = [0] * TC
                x1_half_evac = [[0] * TC, [0] * TC]
                for pr0 in range(2):
                    b.w("gpsimd", sO, go_store[2 * pr0 + 1])
                    b.op("gpsimd", "dma_start",
                         go_t[:, pr0 * FC * 256:(pr0 + 1) * FC * 256].rearrange(
                             "q (c j) -> q c j", c=FC),
                         go_scr[:, :, pr0 * 256:(pr0 + 1) * 256].rearrange(
                             "c q j -> q c j"),
                         inc=sW, incn=16)
                    go_ld[2 * pr0] = go_ld[2 * pr0 + 1] = sW.n
                sq_done_l = [0] * TC
                psT_free = [0, 0, 0, 0]
                xn2_r = xn2[:, :].rearrange("q (c t) -> q c t", c=FC)

                def emit_ln2_chain(tc):
                    p = tc % 2
                    for i in range(4):
                        b.op("vector", "bn_stats",
                             stat2[:, p * 32 + i * 6: p * 32 + (i + 1) * 6],
                             x1sb[:, p * D + i * 512: p * D + (i + 1) * 512],
                             inc=(sV if i == 3 else None))
                    b.w("vector", sV, sV.n)
                    b.op("vector", "bn_aggr", stat2[:, p * 32 + 24: p * 32 + 26],
                         stat2[:, p * 32: p * 32 + 24], inc=sV)
                    var_done = sV.n
                    sq_done_l[tc] = 0
                    b.w("scalar", sV, var_done)
                    b.op("scalar", "activation", stat2[:, p * 32 + 26: p * 32 + 27],
                         stat2[:, p * 32 + 25: p * 32 + 26], AF.Sqrt,
                         bias=cst[:, C_EPS_LN:C_EPS_LN + 1], inc=sS)
                    std_done = sS.n
                    b.w("vector", sS, std_done)
                    b.op("vector", "reciprocal", stat2[:, p * 32 + 27: p * 32 + 28],
                         stat2[:, p * 32 + 26: p * 32 + 27], inc=sV)
                    b.w("vector", sV, sV.n)
                    if tc >= 2:
                        b.w("vector", sT, ln2_tr_done[tc - 2])
                    b.op("vector", "tensor_scalar", xn2tm[:, p * D:(p + 1) * D],
                         x1sb[:, p * D:(p + 1) * D], stat2[:, p * 32 + 24: p * 32 + 25],
                         stat2[:, p * 32 + 27: p * 32 + 28], ALU.subtract, ALU.mult, inc=sV)
                    xn2tm_done[tc] = sV.n

                def emit_ln2_transposes(tc):
                    p = tc % 2
                    b.w("tensor", sV, xn2tm_done[tc])
                    for pj in range(FC // 4):
                        r = pj % 4
                        b.w("tensor", sV, psT_free[r])
                        for half in range(4):
                            fc = 4 * pj + half
                            b.op("tensor", "matmul", psT[r][:, half * 128:(half + 1) * 128],
                                 xn2tm[:, p * D + fc * 128: p * D + (fc + 1) * 128],
                                 ident_s[:, :], start=True, stop=True,
                                 skip_group_check=True, inc=(sT if half == 3 else None))
                        mm_done = sT.n
                        b.w("vector", sT, mm_done)
                        b.op("vector", "tensor_copy",
                             xn2_r[:, 4 * pj:4 * pj + 4, tc * 128:(tc + 1) * 128],
                             psT[r][:, 0:512].rearrange("q (c t) -> q c t", c=4), inc=sV)
                        psT_free[r] = sV.n
                    ln2_tr_done[tc] = sT.n

                for tc in range(TC):
                    p = tc % 2
                    pr = tc // 2
                    if tc % 2 == 0 and pr >= 2:
                        b.w("gpsimd", sT, outproj_done[2 * (pr - 2) + 1])
                        b.op("gpsimd", "dma_start",
                             go_t[:, (pr % 2) * FC * 256:(pr % 2 + 1) * FC * 256].rearrange(
                                 "q (c j) -> q c j", c=FC),
                             go_scr[:, :, tc * 128:(tc + 2) * 128].rearrange(
                                 "c q j -> q c j"),
                             inc=sW, incn=16)
                        go_ld[tc] = go_ld[tc + 1] = sW.n
                    b.w("sync", sV, x1_evac[tc - 2] if tc >= 2 else 0)
                    b.op("sync", "dma_start", xB[:, p * D:(p + 1) * D],
                         x_d[tc * 128:(tc + 1) * 128, :], inc=sA, incn=16)
                    x_ld = sA.n
                    b.w("tensor", sW, go_ld[tc])
                    if tc == 0:
                        b.w("tensor", sW, wout_ld)
                    gbase = (pr % 2) * FC * 256 + (tc % 2) * 128
                    half_done = [0, 0]
                    for sp in range(2):
                        b.w("tensor", sV, x1_half_evac[sp][tc - 1] if tc >= 1 else 0)
                        for oc in range(FC):
                            for slab in (2 * sp, 2 * sp + 1):
                                b.op("tensor", "matmul", psO[slab][:, :],
                                     go_t[:, gbase + oc * 256: gbase + oc * 256 + 128],
                                     woutsb[:, oc * D + slab * 512: oc * D + (slab + 1) * 512],
                                     start=(oc == 0), stop=(oc == FC - 1),
                                     skip_group_check=True,
                                     inc=(sT if (oc == FC - 1 and slab == 2 * sp + 1) else None))
                        half_done[sp] = sT.n
                        b.w("vector", sT, half_done[sp])
                        if sp == 0:
                            b.w("vector", sA, x_ld)
                            if tc >= 2:
                                b.w("vector", sO, x1_store[tc - 2])
                        for slab in (2 * sp, 2 * sp + 1):
                            b.op("vector", "tensor_add",
                                 x1sb[:, p * D + slab * 512: p * D + (slab + 1) * 512],
                                 psO[slab][:, :],
                                 xB[:, p * D + slab * 512: p * D + (slab + 1) * 512],
                                 inc=(sV if slab == 2 * sp + 1 else None))
                        x1_half_evac[sp][tc] = sV.n
                    outproj_done[tc] = sT.n
                    x1_evac[tc] = sV.n
                    b.w("sync", sV, x1_evac[tc])
                    b.op("sync", "dma_start", x1_scr[tc * 128:(tc + 1) * 128, :],
                         x1sb[:, p * D:(p + 1) * D], inc=sO, incn=16)
                    x1_store[tc] = sO.n
                    if tc >= 2:
                        emit_ln2_transposes(tc - 2)
                    if tc >= 1:
                        emit_ln2_chain(tc - 1)
                emit_ln2_chain(TC - 1)
                emit_ln2_transposes(TC - 2)
                emit_ln2_transposes(TC - 1)
                xn2_done_v = sV.n
                if dbg:
                    b.w("sync", sV, xn2_done_v)
                    b.op("sync", "dma_start", xn2_dbg[:, :], xn2[:, :], inc=sO, incn=16)

            go_t_g.__exit__(None, None, None)
            woutsb_g.__exit__(None, None, None)
            barrier()
            wdsb_g = nc.sbuf_tensor("wdsb", [128, 2 * ICN * 512], BF16, side="right")
            wdsb = wdsb_g.__enter__()

            with contextlib.ExitStack() as esC:
                ec = esC.enter_context
                wgt = ec(nc.sbuf_tensor("wgt", [128, 2 * D], BF16))
                wut = ec(nc.sbuf_tensor("wut", [128, 2 * D], BF16))
                tg = ec(nc.sbuf_tensor("tg", [128, 2 * 512], F32))
                h_sb = ec(nc.sbuf_tensor("h_sb", [128, 2 * NT], BF16))
                psg0 = ec(nc.psum_tensor("psg0", [128, 512], F32))
                psg1 = ec(nc.psum_tensor("psg1", [128, 512], F32))
                psu0 = ec(nc.psum_tensor("psu0", [128, 512], F32))
                psu1 = ec(nc.psum_tensor("psu1", [128, 512], F32))
                psg = (psg0, psg1)
                psu = (psu0, psu1)
                wld = [0, 0]
                for ic in range(min(2, ICN)):
                    b.op("gpsimd", "dma_start", wgt[:, (ic % 2) * D:(ic % 2 + 1) * D],
                         wg_d[ic, :, :], inc=sW, incn=16)
                    b.op("gpsimd", "dma_start", wut[:, (ic % 2) * D:(ic % 2 + 1) * D],
                         wu_d[ic, :, :], inc=sW, incn=16)
                    wld[ic % 2] = sW.n
                gu_mm_done = [0] * ICN
                h_done = [0] * ICN
                h_store = [0] * ICN
                tg_free = [0, 0]
                psg_free = [0, 0]
                psu_free = [0, 0]
                for ic in range(ICN):
                    p = ic % 2
                    b.w("tensor", sW, wld[p])
                    if ic == 0:
                        b.w("tensor", sV, xn2_done_v)
                    for slab in range(4):
                        gp = slab % 2
                        b.w("tensor", sS, psg_free[gp])
                        for fc in range(FC):
                            b.op("tensor", "matmul", psg[gp][:, :],
                                 wgt[:, p * D + fc * 128: p * D + (fc + 1) * 128],
                                 xn2[:, fc * NT + slab * 512: fc * NT + (slab + 1) * 512],
                                 start=(fc == 0), stop=(fc == FC - 1),
                                 skip_group_check=True, inc=(sT if fc == FC - 1 else None))
                        g_mm = sT.n
                        b.w("tensor", sV, psu_free[gp])
                        for fc in range(FC):
                            b.op("tensor", "matmul", psu[gp][:, :],
                                 wut[:, p * D + fc * 128: p * D + (fc + 1) * 128],
                                 xn2[:, fc * NT + slab * 512: fc * NT + (slab + 1) * 512],
                                 start=(fc == 0), stop=(fc == FC - 1),
                                 skip_group_check=True, inc=(sT if fc == FC - 1 else None))
                        u_mm = sT.n
                        b.w("scalar", sT, g_mm)
                        b.w("scalar", sV, tg_free[gp])
                        b.op("scalar", "activation", tg[:, gp * 512:(gp + 1) * 512],
                             psg[gp][:, :], AF.Silu, scale=gs_s[:, ic:ic + 1], inc=sS)
                        t_done = sS.n
                        psg_free[gp] = sS.n
                        b.w("vector", sT, u_mm)
                        b.w("vector", sS, t_done)
                        if slab == 0 and ic >= 2:
                            b.w("vector", sO, h_store[ic - 2])
                        b.op("vector", "scalar_tensor_tensor",
                             h_sb[:, p * NT + slab * 512: p * NT + (slab + 1) * 512],
                             psu[gp][:, :], us_s[:, ic:ic + 1],
                             tg[:, gp * 512:(gp + 1) * 512], ALU.mult, ALU.mult, inc=sV)
                        tg_free[gp] = sV.n
                        psu_free[gp] = sV.n
                    gu_mm_done[ic] = sT.n
                    h_done[ic] = sV.n
                    if ic + 2 < ICN:
                        b.w("gpsimd", sT, gu_mm_done[ic])
                        b.op("gpsimd", "dma_start", wgt[:, p * D:(p + 1) * D],
                             wg_d[ic + 2, :, :], inc=sW, incn=16)
                        b.op("gpsimd", "dma_start", wut[:, p * D:(p + 1) * D],
                             wu_d[ic + 2, :, :], inc=sW, incn=16)
                        wld[p] = sW.n
                    if ic in (2, 4, 6, 8):
                        jj = (ic - 2) // 2
                        qq, piece = jj // 2, jj % 2
                        ic0, ic1 = piece * 22, (piece + 1) * 22
                        base = (qq % 2) * ICN * 512
                        b.op("gpsimd", "dma_start",
                             wdsb[:, base + ic0 * 512: base + ic1 * 512].rearrange(
                                 "q (c j) -> q c j", j=512),
                             wd_d[ic0:ic1, :, qq * 512:(qq + 1) * 512].rearrange(
                                 "c q j -> q c j"),
                             inc=sW, incn=16)
                    b.w("sync", sV, h_done[ic])
                    b.op("sync", "dma_start", h_scr[ic, :, :],
                         h_sb[:, p * NT:(p + 1) * NT], inc=sO, incn=16)
                    h_store[ic] = sO.n

            barrier()

        with contextlib.ExitStack() as esD:
            ec = esD.enter_context
            HTP = ICN * 256
            h_t = ec(nc.sbuf_tensor("h_t", [128, 4 * HTP], BF16))
            x1t = ec(nc.sbuf_tensor("x1t", [128, 2 * 512], F32))
            outsb = ec(nc.sbuf_tensor("outsb", [128, 2 * 512], F32))
            dtmp = ec(nc.sbuf_tensor("dtmp", [128, 512], F32))
            psD0 = ec(nc.psum_tensor("psD0", [128, 512], F32))
            psD1 = ec(nc.psum_tensor("psD1", [128, 512], F32))
            psD2 = ec(nc.psum_tensor("psD2", [128, 512], F32))
            psD3 = ec(nc.psum_tensor("psD3", [128, 512], F32))
            psD = (psD0, psD1, psD2, psD3)
            out_evac = {}
            out_store = {}
            d_mm_done = {}
            h_ld = {}
            wdq_ld = [0, 0]
            NQ = 4
            for q in range(NQ):
                if q >= 2:
                    b.w("gpsimd", sT, d_mm_done[(q - 2, TC - 1)])
                    base = (q % 2) * ICN * 512
                    for piece in range(2):
                        ic0, ic1 = piece * 22, (piece + 1) * 22
                        b.op("gpsimd", "dma_start",
                             wdsb[:, base + ic0 * 512: base + ic1 * 512].rearrange(
                                 "q (c j) -> q c j", j=512),
                             wd_d[ic0:ic1, :, q * 512:(q + 1) * 512].rearrange(
                                 "c q j -> q c j"),
                             inc=sW, incn=16)
                    wdq_ld[q % 2] = sW.n
                for tc in range(TC):
                    p = tc % 2
                    pr = tc // 2
                    pd = psD[tc % 4]
                    if tc >= 4:
                        key_prev = (q, tc - 4)
                    elif q >= 1:
                        key_prev = (q - 1, TC + tc - 4)
                    else:
                        key_prev = None
                    if tc >= 2:
                        key_prev2 = (q, tc - 2)
                    elif q >= 1:
                        key_prev2 = (q - 1, TC + tc - 2)
                    else:
                        key_prev2 = None
                    if tc % 2 == 0:
                        g = q * 8 + pr
                        if g >= 4:
                            qq, pp2 = divmod(g - 4, 8)
                            b.w("gpsimd", sT, d_mm_done[(qq, 2 * pp2 + 1)])
                        slot = g % 4
                        for jj in range(2):
                            ic0, ic1 = jj * 22, (jj + 1) * 22
                            b.op("gpsimd", "dma_start",
                                 h_t[:, slot * HTP + ic0 * 256:
                                     slot * HTP + ic1 * 256].rearrange(
                                     "q (c j) -> q c j", j=256),
                                 h_scr[ic0:ic1, :, tc * 128:(tc + 2) * 128].rearrange(
                                     "c q j -> q c j"),
                                 inc=sW, incn=16)
                        h_ld[(q, pr)] = sW.n
                    b.w("sync", sV, out_evac.get(key_prev2, 0) if key_prev2 else 0)
                    b.op("sync", "dma_start", x1t[:, p * 512:(p + 1) * 512],
                         x1_scr[tc * 128:(tc + 1) * 128, q * 512:(q + 1) * 512],
                         inc=sA, incn=16)
                    x1_ld = sA.n
                    b.w("tensor", sW, h_ld[(q, pr)])
                    if q >= 2 and tc == 0:
                        b.w("tensor", sW, wdq_ld[q % 2])
                    b.w("tensor", sV, out_evac.get(key_prev, 0) if key_prev else 0)
                    hbase = ((q * 8 + pr) % 4) * HTP + (tc % 2) * 128
                    wbase = (q % 2) * ICN * 512
                    for ic in range(ICN):
                        b.op("tensor", "matmul", pd[:, :],
                             h_t[:, hbase + ic * 256: hbase + ic * 256 + 128],
                             wdsb[:, wbase + ic * 512: wbase + (ic + 1) * 512],
                             start=(ic == 0), stop=(ic == ICN - 1),
                             skip_group_check=True,
                             inc=(sT if ic == ICN - 1 else None))
                    d_mm_done[(q, tc)] = sT.n
                    b.w("vector", sT, d_mm_done[(q, tc)])
                    b.w("vector", sA, x1_ld)
                    prev_store = out_store.get((q, tc - 2) if tc >= 2 else
                                               ((q - 1, TC + tc - 2) if q >= 1 else None), 0)
                    b.w("vector", sO, prev_store)
                    b.op("vector", "tensor_mul", dtmp[:, :], pd[:, :],
                         dsb_s[:, q * 512:(q + 1) * 512])
                    b.op("vector", "tensor_add",
                         outsb[:, p * 512:(p + 1) * 512], dtmp[:, :],
                         x1t[:, p * 512:(p + 1) * 512], inc=sV)
                    out_evac[(q, tc)] = sV.n
                    b.w("sync", sV, out_evac[(q, tc)])
                    b.op("sync", "dma_start",
                         out_d[tc * 128:(tc + 1) * 128, q * 512:(q + 1) * 512],
                         outsb[:, p * 512:(p + 1) * 512], inc=sO, incn=16)
                    out_store[(q, tc)] = sO.n
        wdsb_g.__exit__(None, None, None)

        b.w("sync", sO, sO.n)

        with nc.Block() as block:
            @block.tensor
            def _(e):
                for f in b.streams["tensor"]:
                    f(e)

            @block.vector
            def _(e):
                for f in b.streams["vector"]:
                    f(e)

            @block.scalar
            def _(e):
                for f in b.streams["scalar"]:
                    f(e)

            @block.sync
            def _(e):
                for f in b.streams["sync"]:
                    f(e)

            @block.gpsimd
            def _(e):
                for f in b.streams["gpsimd"]:
                    f(e)

    return nc


def prep_inputs(L0_attn_hidden_states, L0_attn_slope_rate, L0_attn_w_qkv,
                L0_attn_w_gate, L0_attn_w_out, L0_attn_norm_w, L0_ffn_gate_w,
                L0_ffn_up_w, L0_ffn_down_w, L0_ffn_gate_s, L0_ffn_up_s, L0_ffn_down_s):
    f32 = lambda a: np.ascontiguousarray(np.asarray(a), dtype=np.float32)
    hs = f32(L0_attn_hidden_states)
    wqkv = f32(L0_attn_w_qkv)
    wgate = f32(L0_attn_w_gate)
    wout = f32(L0_attn_w_out)
    wg = f32(L0_ffn_gate_w)
    wu = f32(L0_ffn_up_w)
    wd = f32(L0_ffn_down_w)
    slope = f32(L0_attn_slope_rate).reshape(1, NH)
    nw = f32(L0_attn_norm_w)
    gsv = f32(L0_ffn_gate_s)
    usv = f32(L0_ffn_up_s)
    dsv = f32(L0_ffn_down_s)

    wqkv_img = np.ascontiguousarray(
        wqkv.reshape(NH, 3, HD, FC, 128).transpose(0, 4, 1, 3, 2).reshape(NH, 128, 3 * FC * 128))
    wgate_img = np.ascontiguousarray(
        wgate.reshape(FC, 128, FC, 128).transpose(0, 3, 2, 1).reshape(FC, 128, FC * 128))
    wout_img = np.ascontiguousarray(wout.T).reshape(FC, 128, D)
    wg_img = np.ascontiguousarray(
        wg.reshape(ICN, 128, FC, 128).transpose(0, 3, 2, 1).reshape(ICN, 128, FC * 128))
    wu_img = np.ascontiguousarray(
        wu.reshape(ICN, 128, FC, 128).transpose(0, 3, 2, 1).reshape(ICN, 128, FC * 128))
    wd_img = np.ascontiguousarray(wd.T).reshape(ICN, 128, D)

    common = {
        "wqkv": wqkv_img, "wgate": wgate_img, "wout": wout_img,
        "wgff": wg_img, "wuff": wu_img, "wdff": wd_img,
        "slopes": np.ascontiguousarray(np.broadcast_to(slope, (128, NH))),
        "nw": np.ascontiguousarray(nw.reshape(FC, 128).T),
        "gs": np.ascontiguousarray(gsv.reshape(ICN, 128).T),
        "us": np.ascontiguousarray(usv.reshape(ICN, 128).T),
        "dsb": np.ascontiguousarray(np.broadcast_to(dsv.reshape(1, D), (128, D))),
    }
    in_maps = []
    for c in range(8):
        m = dict(common)
        m["x"] = np.ascontiguousarray(hs[c])
        in_maps.append(m)
    return in_maps


def kernel(**inputs):
    in_maps = prep_inputs(**inputs)
    nc = build_core_program()
    res = run_bass_kernel_spmd(nc, in_maps, core_ids=list(range(8)))
    return np.stack([res.results[c]["out"] for c in range(8)], axis=0)


# version 21
# speedup vs baseline: 1.0649x; 1.0015x over previous
import contextlib
import numpy as np
import ml_dtypes
import concourse.bass as bass
import concourse.mybir as mybir
from concourse.bass_utils import run_bass_kernel_spmd

F32 = mybir.dt.float32
BF16 = mybir.dt.bfloat16
AF = mybir.ActivationFunctionType
ALU = mybir.AluOpType
AX = mybir.AxisListType

NT = 2048
D = 2048
FC = 16
TC = 16
NH = 16
HD = 128
BLK = 256
NB = 8
ICN = 44
INTER = 5632
BIGNEG = 40000.0

ENGS = ("tensor", "vector", "scalar", "sync", "gpsimd")

C_NEG1, C_EPS_LN, C_EPS_RMS, C_INVD, C_256 = 0, 1, 2, 3, 4


class Sem:
    def __init__(self, h):
        self.h = h
        self.n = 0


class Builder:
    def __init__(self):
        self.streams = {e: [] for e in ENGS}

    def w(self, eng, sem, val):
        if val is None or val <= 0:
            return
        self.streams[eng].append(lambda e, h=sem.h, v=val: e.wait_ge(h, v))

    def op(self, eng, method, *args, inc=None, incn=1, **kwargs):
        def f(e, m=method, a=args, k=kwargs, inc=inc, incn=incn):
            r = getattr(e, m)(*a, **k)
            if inc is not None:
                r.then_inc(inc.h, incn)
        self.streams[eng].append(f)
        if inc is not None:
            inc.n += incn
            return inc.n
        return None


def build_core_program(dbg=False):
    nc = bass.Bass("TRN2", target_bir_lowering=False)

    x_d = nc.dram_tensor("x", [NT, D], F32, kind="ExternalInput").ap()
    wqkv_d = nc.dram_tensor("wqkv", [NH, 128, 3 * FC * 128], F32, kind="ExternalInput").ap()
    wgate_d = nc.dram_tensor("wgate", [FC, 128, FC * 128], F32, kind="ExternalInput").ap()
    wout_d = nc.dram_tensor("wout", [FC, 128, D], F32, kind="ExternalInput").ap()
    wg_d = nc.dram_tensor("wgff", [ICN, 128, FC * 128], F32, kind="ExternalInput").ap()
    wu_d = nc.dram_tensor("wuff", [ICN, 128, FC * 128], F32, kind="ExternalInput").ap()
    wd_d = nc.dram_tensor("wdff", [ICN, 128, D], F32, kind="ExternalInput").ap()
    slopes_d = nc.dram_tensor("slopes", [128, NH], F32, kind="ExternalInput").ap()
    nw_d = nc.dram_tensor("nw", [128, FC], F32, kind="ExternalInput").ap()
    gs_d = nc.dram_tensor("gs", [128, ICN], F32, kind="ExternalInput").ap()
    us_d = nc.dram_tensor("us", [128, ICN], F32, kind="ExternalInput").ap()
    dsb_d = nc.dram_tensor("dsb", [128, D], F32, kind="ExternalInput").ap()

    out_d = nc.dram_tensor("out", [NT, D], F32, kind="ExternalOutput").ap()
    scratch_kind = "ExternalOutput" if dbg else "Internal"
    if dbg:
        xn1_dbg = nc.dram_tensor("xn1_dbg", [128, FC * NT], BF16, kind="ExternalOutput").ap()
        xn2_dbg = nc.dram_tensor("xn2_dbg", [128, FC * NT], BF16, kind="ExternalOutput").ap()
        istd_dbg = nc.dram_tensor("istd_dbg", [128, NT], F32, kind="ExternalOutput").ap()
    o_scr = nc.dram_tensor("o_scr", [NH, 128, NT], BF16, kind=scratch_kind).ap()
    go_scr = nc.dram_tensor("go_scr", [FC, 128, NT], BF16, kind=scratch_kind).ap()
    x1_scr = nc.dram_tensor("x1_scr", [NT, D], F32, kind=scratch_kind).ap()
    o2_scr = nc.dram_tensor("o2_scr", [NH, 128, NT], BF16, kind="Internal").ap()
    h_scr = nc.dram_tensor("h_scr", [ICN, 128, NT], BF16, kind=scratch_kind).ap()

    ident_np = np.eye(128, dtype=ml_dtypes.bfloat16)
    ones_np = np.ones((128, 128), dtype=ml_dtypes.bfloat16)
    consts_np = np.broadcast_to(
        np.array([-1.0, 1e-5, 1e-6, 1.0 / 2048.0, 256.0, 0.0, 1.0, 0.0], dtype=np.float32),
        (128, 8)).copy()
    arr_np = np.broadcast_to(np.arange(1, BLK + 1, dtype=np.float32), (128, BLK)).copy()
    karr_np = np.empty((128, 2), dtype=np.float32)
    for sub in range(2):
        karr_np[:, sub] = 255.0 - (sub * 128 + np.arange(128))
    idxt_np = np.empty((128, 2 * BLK), dtype=np.float32)
    for sub in range(2):
        tk = sub * 128 + np.arange(128)[:, None]
        tq = np.arange(BLK)[None, :]
        dif = tq - tk
        idxt_np[:, sub * BLK:(sub + 1) * BLK] = np.where(dif >= 0, dif, BIGNEG)
    ident_dr = nc.inline_tensor(ident_np, name="identc").ap()
    ones_dr = nc.inline_tensor(ones_np, name="onesc").ap()
    consts_dr = nc.inline_tensor(consts_np, name="constsc").ap()
    arr_dr = nc.inline_tensor(arr_np, name="arrc").ap()
    karr_dr = nc.inline_tensor(karr_np, name="karrc").ap()
    idxt_dr = nc.inline_tensor(idxt_np, name="idxtc").ap()

    b = Builder()

    with contextlib.ExitStack() as es0:
        ec = es0.enter_context
        hT = ec(nc.semaphore("sT")); hV = ec(nc.semaphore("sV")); hS = ec(nc.semaphore("sS"))
        hW = ec(nc.semaphore("sW")); hA = ec(nc.semaphore("sA")); hO = ec(nc.semaphore("sO"))
        cst = ec(nc.sbuf_tensor("cst", [128, 8], F32))
        ident_s = ec(nc.sbuf_tensor("ident_s", [128, 128], BF16))
        ones_s = ec(nc.sbuf_tensor("ones_s", [128, 128], BF16))
        slopes_s = ec(nc.sbuf_tensor("slopes_s", [128, NH], F32))
        nw_s = ec(nc.sbuf_tensor("nw_s", [128, FC], F32))
        gs_s = ec(nc.sbuf_tensor("gs_s", [128, ICN], F32))
        us_s = ec(nc.sbuf_tensor("us_s", [128, ICN], F32))
        dsb_s = ec(nc.sbuf_tensor("dsb_s", [128, D], F32))
        sT, sV, sS = Sem(hT), Sem(hV), Sem(hS)
        sW, sA, sO = Sem(hW), Sem(hA), Sem(hO)
        sems = {"sT": sT, "sV": sV, "sS": sS, "sW": sW, "sA": sA, "sO": sO}
        own = {"tensor": {"sT"}, "vector": {"sV"}, "scalar": {"sS"},
               "sync": set(), "gpsimd": set()}

        def barrier(skips=None):
            snap = {k: s.n for k, s in sems.items()}
            for eng in ENGS:
                for k, s in sems.items():
                    if k in own[eng]:
                        continue
                    if skips and k in skips.get(eng, ()):
                        continue
                    b.w(eng, s, snap[k])

        for dst, src in ((cst, consts_dr), (ident_s, ident_dr), (ones_s, ones_dr),
                         (slopes_s, slopes_d), (nw_s, nw_d), (gs_s, gs_d),
                         (us_s, us_d), (dsb_s, dsb_d)):
            b.op("gpsimd", "dma_start", dst[:, :], src[:, :], inc=sW, incn=16)
        consts_ld0 = sW.n

        with nc.sbuf_tensor("xn1", [128, FC * NT], BF16) as xn1:
            with contextlib.ExitStack() as esA:
                ec = esA.enter_context
                arr_s = ec(nc.sbuf_tensor("arr_s", [128, BLK], F32))
                karr_s = ec(nc.sbuf_tensor("karr_s", [128, 2], F32))
                idxt_s = ec(nc.sbuf_tensor("idxt_s", [128, 2 * BLK], F32))
                qdec = ec(nc.sbuf_tensor("qdec", [128, NH * BLK], BF16))
                ddt = ec(nc.sbuf_tensor("ddt", [128, NH * 2 * BLK], BF16))
                kdec = ec(nc.sbuf_tensor("kdec", [128, NH * 2], F32))
                bdall = ec(nc.sbuf_tensor("bdall", [128, NH], F32))
                dtmpA = ec(nc.sbuf_tensor("dtmpA", [128, 2 * BLK], F32))
                wsb_g = nc.sbuf_tensor("wsb", [128, 2 * 3 * FC * 128], BF16,
                                       side="right")
                wsb = wsb_g.__enter__()
                xA = ec(nc.sbuf_tensor("xA", [128, 2 * D], F32))
                xntm = ec(nc.sbuf_tensor("xntm", [128, 2 * D], BF16))
                stat = ec(nc.sbuf_tensor("stat", [128, 64], F32))
                q_sb = ec(nc.sbuf_tensor("q_sb", [128, 2 * NT], BF16))
                k_sb = ec(nc.sbuf_tensor("k_sb", [128, 2 * NT], BF16))
                v_sb = ec(nc.sbuf_tensor("v_sb", [128, 2 * NT], BF16))
                kkT = ec(nc.sbuf_tensor("kkT", [128, 2 * NT], BF16))
                vT = ec(nc.sbuf_tensor("vT", [128, 2 * NT], BF16))
                qd_sb = ec(nc.sbuf_tensor("qd_sb", [128, 2 * BLK], BF16))
                qkTd = ec(nc.sbuf_tensor("qkTd", [128, 2 * 2 * BLK], BF16))
                kv_sb = ec(nc.sbuf_tensor("kv_sb", [128, HD], F32))
                kv_bf = ec(nc.sbuf_tensor("kv_bf", [128, HD], BF16))
                o_sb = ec(nc.sbuf_tensor("o_sb", [128, 2 * NT], BF16))
                o2st = ec(nc.sbuf_tensor("o2st", [128, NT], BF16))
                pp0 = ec(nc.psum_tensor("pp0", [128, 512], F32))
                pp1 = ec(nc.psum_tensor("pp1", [128, 512], F32))
                qk0 = ec(nc.psum_tensor("qk0", [128, 512], F32))
                qk1 = ec(nc.psum_tensor("qk1", [128, 512], F32))
                po0 = ec(nc.psum_tensor("po0", [128, BLK], F32))
                po1 = ec(nc.psum_tensor("po1", [128, BLK], F32))
                pkv0 = ec(nc.psum_tensor("pkv0", [128, HD], F32))
                pkv1 = ec(nc.psum_tensor("pkv1", [128, HD], F32))
                pp = (pp0, pp1)
                qkp = (qk0, qk1)
                pop = (po0, po1)
                pkvp = (pkv0, pkv1)

                for dst, src in ((arr_s, arr_dr), (karr_s, karr_dr), (idxt_s, idxt_dr)):
                    b.op("gpsimd", "dma_start", dst[:, :], src[:, :], inc=sW, incn=16)
                consts_ld = sW.n
                b.w("vector", sW, consts_ld)
                b.w("scalar", sW, consts_ld)
                b.op("vector", "tensor_scalar", dtmpA[:, 0:NH], slopes_s[:, :],
                     cst[:, C_NEG1:C_NEG1 + 1], None, ALU.mult)
                b.op("vector", "tensor_scalar", dtmpA[:, NH:2 * NH], slopes_s[:, :],
                     cst[:, C_256:C_256 + 1], None, ALU.mult, inc=sV)
                negs_done = sV.n
                b.w("scalar", sV, negs_done)
                b.op("scalar", "activation", bdall[:, :], dtmpA[:, NH:2 * NH], AF.Exp,
                     scale=cst[:, C_NEG1:C_NEG1 + 1], inc=sS)
                for h in range(NH):
                    nscol = dtmpA[:, h:h + 1]
                    b.op("scalar", "activation", qdec[:, h * BLK:(h + 1) * BLK],
                         arr_s[:, :], AF.Exp, scale=nscol)
                    b.op("scalar", "activation", kdec[:, 2 * h:2 * h + 2],
                         karr_s[:, :], AF.Exp, scale=nscol)
                    b.op("scalar", "activation", ddt[:, h * 2 * BLK:(h + 1) * 2 * BLK],
                         idxt_s[:, :], AF.Exp, scale=nscol,
                         inc=(sS if h == NH - 1 else None))
                decay_done_s = sS.n

                pp_free_v = [0, 0]
                pp_free_s = [0, 0]
                tr4 = (qk0, qk1, po0, po1)
                tr4_free = [0, 0, 0, 0]
                w_ld = [0] * NH
                proj_done = [0] * NH
                attn_mm_done = [0] * NH
                qread_done = [0] * NH
                o_store = [0] * NH
                o2_store = [0] * NH
                o_sq_done = [0] * NH
                ev_k_done = [0] * NH
                ev_v_done = [0] * NH
                ev_all_done = [0] * NH
                for hh in range(min(2, NH)):
                    b.op("gpsimd", "dma_start",
                         wsb[:, (hh % 2) * 6144:(hh % 2 + 1) * 6144],
                         wqkv_d[hh, :, :], inc=sW, incn=16)
                    w_ld[hh] = sW.n

                grp = [0]
                pkv_free = [0, 0]
                o_evac = {}
                qkTd_done = {}
                qd_done = {}
                kv_bf_done = {}
                scalar_head_gate = [False] * NH

                def emit_proj_group(hh, part, slab, xn1_wait=None):
                    hp = hh % 2
                    pi = grp[0] % 2
                    grp[0] += 1
                    if part == 0 and slab == 0:
                        b.w("tensor", sW, w_ld[hh])
                    if xn1_wait is not None:
                        b.w("tensor", sV, xn1_wait)
                    b.w("tensor", sS, pp_free_s[pi])
                    b.w("tensor", sV, pp_free_v[pi])
                    for fc in range(FC):
                        b.op("tensor", "matmul", pp[pi][:, :],
                             wsb[:, hp * 6144 + (part * FC + fc) * 128:
                                 hp * 6144 + (part * FC + fc + 1) * 128],
                             xn1[:, fc * NT + slab * 512: fc * NT + (slab + 1) * 512],
                             start=(fc == 0), stop=(fc == FC - 1),
                             skip_group_check=True,
                             inc=(sT if fc == FC - 1 else None))
                    mm_done = sT.n
                    if not scalar_head_gate[hh]:
                        scalar_head_gate[hh] = True
                        if hh >= 2:
                            b.w("scalar", sT, attn_mm_done[hh - 2])
                            b.w("scalar", sV, qread_done[hh - 2])
                    b.w("scalar", sT, mm_done)
                    dst = (q_sb, k_sb, v_sb)[part]
                    b.op("scalar", "activation",
                         dst[:, hp * NT + slab * 512: hp * NT + (slab + 1) * 512],
                         pp[pi][:, :], AF.Silu, inc=sS)
                    pp_free_s[pi] = sS.n
                    if part == 1 and slab == 3:
                        ev_k_done[hh] = sS.n
                    if part == 2 and slab == 3:
                        ev_v_done[hh] = sS.n
                        ev_all_done[hh] = sS.n
                    if part == 2 and slab == 3:
                        proj_done[hh] = sT.n

                x_free = [0, 0]
                xn_free = [0, 0]
                for tcp in range(TC // 2):
                    tcs = (2 * tcp, 2 * tcp + 1)
                    lds = [[0, 0], [0, 0]]
                    for i, tc in enumerate(tcs):
                        p = tc % 2
                        b.w("sync", sV, x_free[p])
                        for hx in range(2):
                            b.op("sync", "dma_start",
                                 xA[:, p * D + hx * 1024: p * D + (hx + 1) * 1024],
                                 x_d[tc * 128:(tc + 1) * 128,
                                     hx * 1024:(hx + 1) * 1024],
                                 inc=sA, incn=16)
                            lds[i][hx] = sA.n
                    for i, tc in enumerate(tcs):
                        p = tc % 2
                        for j in range(4):
                            if j % 2 == 0:
                                b.w("vector", sA, lds[i][j // 2])
                            b.op("vector", "bn_stats",
                                 stat[:, p * 32 + j * 6: p * 32 + (j + 1) * 6],
                                 xA[:, p * D + j * 512: p * D + (j + 1) * 512],
                                 inc=(sV if j == 3 else None))
                    b.w("vector", sV, sV.n)
                    for tc in tcs:
                        p = tc % 2
                        b.op("vector", "bn_aggr", stat[:, p * 32 + 24: p * 32 + 26],
                             stat[:, p * 32: p * 32 + 24], inc=(sV if p == 1 else None))
                    var_done = sV.n
                    b.w("scalar", sV, var_done)
                    for tc in tcs:
                        p = tc % 2
                        b.op("scalar", "activation", stat[:, p * 32 + 26: p * 32 + 27],
                             stat[:, p * 32 + 25: p * 32 + 26], AF.Sqrt,
                             bias=cst[:, C_EPS_LN:C_EPS_LN + 1],
                             inc=(sS if p == 1 else None))
                    std_done = sS.n
                    b.w("vector", sS, std_done)
                    for tc in tcs:
                        p = tc % 2
                        b.op("vector", "reciprocal", stat[:, p * 32 + 27: p * 32 + 28],
                             stat[:, p * 32 + 26: p * 32 + 27],
                             inc=(sV if p == 1 else None))
                    b.w("vector", sV, sV.n)
                    for tc in tcs:
                        p = tc % 2
                        b.w("vector", sT, xn_free[p])
                        b.op("vector", "tensor_scalar", xntm[:, p * D:(p + 1) * D],
                             xA[:, p * D:(p + 1) * D], stat[:, p * 32 + 24: p * 32 + 25],
                             stat[:, p * 32 + 27: p * 32 + 28], ALU.subtract, ALU.mult,
                             inc=sV)
                        if tc == tcs[0]:
                            xn_done_a = sV.n
                        else:
                            xn_done_b = sV.n
                        x_free[p] = sV.n
                    for i, tc in enumerate(tcs):
                        p = tc % 2
                        b.w("tensor", sV, xn_done_a if i == 0 else xn_done_b)
                        xn1_r = xn1[:, :].rearrange("q (c t) -> q c t", c=FC)
                        for pj in range(FC // 2):
                            r = pj % 4
                            b.w("tensor", sV, tr4_free[r])
                            for half in range(2):
                                fc = 2 * pj + half
                                b.op("tensor", "matmul",
                                     tr4[r][:, half * 128:(half + 1) * 128],
                                     xntm[:, p * D + fc * 128: p * D + (fc + 1) * 128],
                                     ident_s[:, :], start=True, stop=True,
                                     skip_group_check=True,
                                     inc=(sT if half == 1 else None))
                            mm_done = sT.n
                            b.w("vector", sT, mm_done)
                            b.op("vector", "tensor_copy",
                                 xn1_r[:, 2 * pj:2 * pj + 2, tc * 128:(tc + 1) * 128],
                                 tr4[r][:, 0:256].rearrange("q (c t) -> q c t", c=2),
                                 inc=sV)
                            tr4_free[r] = sV.n
                        xn_free[p] = sT.n
                    if tcs[1] % 4 == 3:
                        sl = tcs[1] // 4
                        xn1_slab = sV.n
                        for part in range(3):
                            emit_proj_group(0, part, sl, xn1_wait=xn1_slab)
                xn1_done_v = sV.n
                xn1_done_s = sS.n
                qk_free0 = [tr4_free[0], tr4_free[1]]
                po_free0 = [tr4_free[2], tr4_free[3]]

                qk_free = list(qk_free0)
                po_free = list(po_free0)

                def emit_transposes(hh):
                    hp = hh % 2
                    if hh >= 2:
                        b.w("vector", sT, attn_mm_done[hh - 2])
                    b.w("tensor", sS, ev_k_done[hh])
                    for j in range(2 * NB):
                        r = j % 2
                        b.w("tensor", sV, po_free[r])
                        b.op("tensor", "matmul", pop[r][:, 0:128],
                             k_sb[:, hp * NT + j * 128: hp * NT + (j + 1) * 128],
                             ident_s[:, :], start=True, stop=True,
                             skip_group_check=True, inc=sT)
                        mm_done = sT.n
                        sub = j % 2
                        b.w("vector", sT, mm_done)
                        b.op("vector", "tensor_scalar",
                             kkT[:, hp * NT + j * 128: hp * NT + (j + 1) * 128],
                             pop[r][:, 0:128], kdec[:, 2 * hh + sub:2 * hh + sub + 1],
                             None, ALU.mult, inc=sV)
                        po_free[r] = sV.n
                    b.w("tensor", sS, ev_v_done[hh])
                    for pj in range(NB):
                        r = pj % 2
                        b.w("tensor", sV, po_free[r])
                        for half in range(2):
                            j = 2 * pj + half
                            b.op("tensor", "matmul",
                                 pop[r][:, half * 128:(half + 1) * 128],
                                 v_sb[:, hp * NT + j * 128: hp * NT + (j + 1) * 128],
                                 ident_s[:, :], start=True, stop=True,
                                 skip_group_check=True,
                                 inc=(sT if half == 1 else None))
                        mm_done = sT.n
                        b.w("vector", sT, mm_done)
                        b.op("vector", "tensor_copy",
                             vT[:, hp * NT + pj * 256: hp * NT + (pj + 1) * 256],
                             pop[r][:, 0:256], inc=sV)
                        po_free[r] = sV.n

                emit_transposes(0)

                for h in range(NH):
                    hp = h % 2
                    qkT_done = {}
                    o_done = {}
                    kvmm_done = {}
                    groups = ([(part, slab) for part in range(3) for slab in range(4)]
                              if h + 1 < NH else [])
                    gi = [0]

                    def drain_groups(target):
                        while gi[0] < min(target, len(groups)):
                            part, slab = groups[gi[0]]
                            emit_proj_group(h + 1, part, slab)
                            gi[0] += 1

                    def emit_qkT(bb, h=h, hp=hp, qkT_done=qkT_done):
                        pi = bb % 2
                        b.w("tensor", sV, qk_free[pi])
                        for sub in range(2):
                            b.op("tensor", "matmul",
                                 qkp[pi][:, sub * BLK:(sub + 1) * BLK],
                                 k_sb[:, hp * NT + bb * BLK + sub * 128:
                                      hp * NT + bb * BLK + (sub + 1) * 128],
                                 q_sb[:, hp * NT + bb * BLK: hp * NT + (bb + 1) * BLK],
                                 start=True, stop=True, skip_group_check=True,
                                 inc=(sT if sub == 1 else None))
                        qkT_done[bb] = sT.n

                    emit_qkT(0)
                    emit_qkT(1)
                    b.w("vector", sS, ev_all_done[h])
                    b.op("vector", "tensor_mul", qd_sb[:, 0:BLK],
                         q_sb[:, hp * NT: hp * NT + BLK],
                         qdec[:, h * BLK:(h + 1) * BLK], inc=sV)
                    qd_done[(h, 0)] = sV.n
                    b.w("vector", sT, qkT_done[0])
                    b.op("vector", "tensor_mul", qkTd[:, 0:512],
                         qkp[0][:, :], ddt[:, h * 2 * BLK:(h + 1) * 2 * BLK], inc=sV)
                    qkTd_done[(h, 0)] = sV.n
                    qk_free[0] = sV.n

                    for bb in range(NB):
                        pi = bb % 2
                        b.w("tensor", sV, po_free[pi])
                        b.w("tensor", sV, qkTd_done[(h, bb)])
                        if bb > 0:
                            b.w("tensor", sV, kv_bf_done[(h, bb - 1)])
                            b.w("tensor", sV, qd_done[(h, bb)])
                            b.op("tensor", "matmul", pop[pi][:, :], kv_bf[:, :],
                                 qd_sb[:, pi * BLK:(pi + 1) * BLK], start=True,
                                 stop=False, skip_group_check=True)
                        for sub in range(2):
                            b.op("tensor", "matmul", pop[pi][:, :],
                                 vT[:, hp * NT + (2 * bb + sub) * 128:
                                     hp * NT + (2 * bb + sub + 1) * 128],
                                 qkTd[:, pi * 512 + sub * BLK: pi * 512 + (sub + 1) * BLK],
                                 start=(bb == 0 and sub == 0), stop=(sub == 1),
                                 skip_group_check=True, inc=(sT if sub == 1 else None))
                        o_done[bb] = sT.n
                        b.w("tensor", sV, pkv_free[pi])
                        for sub in range(2):
                            b.op("tensor", "matmul", pkvp[pi][:, :],
                                 kkT[:, hp * NT + (2 * bb + sub) * 128:
                                     hp * NT + (2 * bb + sub + 1) * 128],
                                 vT[:, hp * NT + (2 * bb + sub) * 128:
                                     hp * NT + (2 * bb + sub + 1) * 128],
                                 start=(sub == 0), stop=(sub == 1),
                                 skip_group_check=True, inc=(sT if sub == 1 else None))
                        kvmm_done[bb] = sT.n
                        drain_groups(((bb + 1) * 12 + 7) // 8)
                        b.w("vector", sT, kvmm_done[bb])
                        if bb == 0:
                            b.op("vector", "tensor_copy", kv_bf[:, :], pkvp[pi][:, :],
                                 inc=sV)
                            kv_bf_done[(h, bb)] = sV.n
                            b.op("vector", "tensor_copy", kv_sb[:, :], pkvp[pi][:, :],
                                 inc=sV)
                        else:
                            b.op("vector", "scalar_tensor_tensor", kv_bf[:, :],
                                 kv_sb[:, :], bdall[:, h:h + 1], pkvp[pi][:, :],
                                 ALU.mult, ALU.add, inc=sV)
                            kv_bf_done[(h, bb)] = sV.n
                            b.op("vector", "scalar_tensor_tensor", kv_sb[:, :],
                                 kv_sb[:, :], bdall[:, h:h + 1], pkvp[pi][:, :],
                                 ALU.mult, ALU.add, inc=sV)
                        pkv_free[pi] = sV.n
                        if h >= 2 and bb == 0:
                            b.w("vector", sO, o_store[h - 2])
                        b.w("vector", sT, o_done[bb])
                        b.op("vector", "tensor_copy",
                             o_sb[:, hp * NT + bb * BLK: hp * NT + (bb + 1) * BLK],
                             pop[pi][:, :], inc=sV)
                        o_evac[(h, bb)] = sV.n
                        po_free[pi] = sV.n
                        if bb + 1 < NB:
                            b.w("vector", sT, o_done.get(bb - 1, 0))
                            b.op("vector", "tensor_mul",
                                 qd_sb[:, ((bb + 1) % 2) * BLK:((bb + 1) % 2 + 1) * BLK],
                                 q_sb[:, hp * NT + (bb + 1) * BLK:
                                      hp * NT + (bb + 2) * BLK],
                                 qdec[:, h * BLK:(h + 1) * BLK], inc=sV)
                            qd_done[(h, bb + 1)] = sV.n
                        if bb + 1 < NB:
                            pj = (bb + 1) % 2
                            b.w("vector", sT, qkT_done[bb + 1])
                            b.op("vector", "tensor_mul",
                                 qkTd[:, pj * 512:(pj + 1) * 512], qkp[pj][:, :],
                                 ddt[:, h * 2 * BLK:(h + 1) * 2 * BLK], inc=sV)
                            qkTd_done[(h, bb + 1)] = sV.n
                            qk_free[pj] = sV.n
                        if bb + 2 < NB:
                            emit_qkT(bb + 2)
                    attn_mm_done[h] = sT.n
                    qread_done[h] = qd_done[(h, NB - 1)]
                    drain_groups(len(groups))
                    b.w("sync", sV, o_evac[(h, NB - 1)])
                    b.op("sync", "dma_start", o_scr[h, :, :],
                         o_sb[:, hp * NT:(hp + 1) * NT], inc=sO, incn=16)
                    b.w("scalar", sV, o_evac[(h, NB - 1)])
                    if h >= 1:
                        b.w("scalar", sO, o2_store[h - 1])
                    b.op("scalar", "activation", o2st[:, :],
                         o_sb[:, hp * NT:(hp + 1) * NT], AF.Square, inc=sS)
                    o_sq_done[h] = sS.n
                    b.w("sync", sS, o_sq_done[h])
                    b.op("sync", "dma_start", o2_scr[h, :, :],
                         o2st[:, :], inc=sO, incn=16)
                    o2_store[h] = sO.n
                    o_store[h] = sO.n
                    if h + 1 < NH:
                        emit_transposes(h + 1)
                    if h + 2 < NH:
                        b.w("gpsimd", sT, proj_done[h])
                        b.op("gpsimd", "dma_start",
                             wsb[:, ((h + 2) % 2) * 6144:((h + 2) % 2 + 1) * 6144],
                             wqkv_d[h + 2, :, :], inc=sW, incn=16)
                        w_ld[h + 2] = sW.n
                    if h == NH - 2:
                        b.w("gpsimd", sT, proj_done[NH - 2])
                        for wo in range(2):
                            b.op("gpsimd", "dma_start",
                                 wsb[:, ((NH - 2) % 2) * 6144 + wo * 2048:
                                     ((NH - 2) % 2) * 6144 + (wo + 1) * 2048],
                                 wgate_d[wo, :, :], inc=sW, incn=16)
                        wg_pre = sW.n

            wsb_g.__exit__(None, None, None)
            if dbg:
                b.w("sync", sV, xn1_done_v)
                b.w("sync", sS, xn1_done_s)
                b.op("sync", "dma_start", xn1_dbg[:, :], xn1[:, :], inc=sO, incn=16)
            barrier()
            woutsb_g = nc.sbuf_tensor("woutsb", [128, FC * D], BF16, side="right")
            woutsb = woutsb_g.__enter__()
            go_t_g = nc.sbuf_tensor("go_t", [128, 2 * FC * 256], BF16, side="right")
            go_t = go_t_g.__enter__()

            with contextlib.ExitStack() as esB:
                ec = esB.enter_context
                o_in = ec(nc.sbuf_tensor("o_in", [128, 2 * NT], BF16))
                o2_sb = ec(nc.sbuf_tensor("o2_sb", [128, 2 * NT], BF16))
                wgsb = ec(nc.sbuf_tensor("wgsb", [128, 2 * D], BF16))
                gatef = ec(nc.sbuf_tensor("gatef", [128, 2 * 512], F32))
                gotmp = ec(nc.sbuf_tensor("gotmp", [128, 512], F32))
                go_sb = ec(nc.sbuf_tensor("go_sb", [128, 2 * NT], BF16))
                istd_b = ec(nc.sbuf_tensor("istd_b", [128, NT], F32))
                psR0 = ec(nc.psum_tensor("psR0", [128, 512], F32))
                psR1 = ec(nc.psum_tensor("psR1", [128, 512], F32))
                psR2 = ec(nc.psum_tensor("psR2", [128, 512], F32))
                psR3 = ec(nc.psum_tensor("psR3", [128, 512], F32))
                psG0 = ec(nc.psum_tensor("psG0", [128, 512], F32))
                psG1 = ec(nc.psum_tensor("psG1", [128, 512], F32))
                psR = (psR0, psR1, psR2, psR3)
                psG = (psG0, psG1)

                sq_free_s = [0, 0]
                sq_free_t = [0, 0]
                o_lds = [0] * NH
                for oc in range(min(2, NH)):
                    b.op("sync", "dma_start", o2_sb[:, (oc % 2) * NT:(oc % 2 + 1) * NT],
                         o2_scr[oc, :, :], inc=sA, incn=16)
                    o_lds[oc] = sA.n
                for oc in range(NH):
                    p = oc % 2
                    b.w("tensor", sA, o_lds[oc])
                    for slab in range(4):
                        b.op("tensor", "matmul", psR[slab][:, :], ones_s[:, :],
                             o2_sb[:, p * NT + slab * 512: p * NT + (slab + 1) * 512],
                             start=(oc == 0), stop=(oc == NH - 1),
                             skip_group_check=True, inc=(sT if slab == 3 else None))
                    sq_free_t[p] = sT.n
                    if oc + 2 < NH:
                        b.w("sync", sT, sq_free_t[p])
                        b.op("sync", "dma_start", o2_sb[:, p * NT:(p + 1) * NT],
                             o2_scr[oc + 2, :, :], inc=sA, incn=16)
                        o_lds[oc + 2] = sA.n
                rms_done = sT.n
                b.w("scalar", sT, rms_done)
                for slab in range(4):
                    b.op("scalar", "activation", istd_b[:, slab * 512:(slab + 1) * 512],
                         psR[slab][:, :], AF.Sqrt, scale=cst[:, C_INVD:C_INVD + 1],
                         bias=cst[:, C_EPS_RMS:C_EPS_RMS + 1],
                         inc=(sS if slab == 3 else None))
                sqrt_done = sS.n
                b.w("vector", sS, sqrt_done)
                b.op("vector", "reciprocal", istd_b[:, :], istd_b[:, :], inc=sV)
                istd_done = sV.n
                b.w("vector", sV, istd_done)

                wg_ld = [wg_pre, wg_pre]
                b.op("gpsimd", "dma_start", woutsb[:, 0:D], wout_d[0, :, :],
                     inc=sW, incn=16)
                go_done = [0] * FC
                go_store = [0] * FC
                gate_mm_done = [0] * FC
                sig_free = [0, 0]
                gate_read = [0, 0]
                o_lds2 = [0] * FC
                for oc in range(min(2, FC)):
                    b.w("sync", sS, sq_free_s[oc % 2])
                    b.op("sync", "dma_start", o_in[:, (oc % 2) * NT:(oc % 2 + 1) * NT],
                         o_scr[oc, :, :], inc=sA, incn=16)
                    o_lds2[oc] = sA.n
                wsb_base = ((NH - 2) % 2) * 6144
                for oc in range(FC):
                    p = oc % 2
                    o_ld2 = o_lds2[oc]
                    b.w("tensor", sW, wg_ld[p])
                    wg_src = (wsb[:, wsb_base + oc * 2048: wsb_base + (oc + 1) * 2048]
                              if oc < 2 else wgsb[:, p * D:(p + 1) * D])
                    for slab in range(4):
                        gp = slab % 2
                        b.w("tensor", sS, sig_free[gp])
                        for fc in range(FC):
                            b.op("tensor", "matmul", psG[gp][:, :],
                                 wg_src[:, fc * 128:(fc + 1) * 128],
                                 xn1[:, fc * NT + slab * 512: fc * NT + (slab + 1) * 512],
                                 start=(fc == 0), stop=(fc == FC - 1),
                                 skip_group_check=True, inc=(sT if fc == FC - 1 else None))
                        g_done = sT.n
                        b.w("scalar", sT, g_done)
                        b.w("scalar", sV, gate_read[gp])
                        b.op("scalar", "activation", gatef[:, gp * 512:(gp + 1) * 512],
                             psG[gp][:, :], AF.Sigmoid, inc=sS)
                        sig_done = sS.n
                        sig_free[gp] = sS.n
                        b.w("vector", sS, sig_done)
                        if slab == 0:
                            b.w("vector", sA, o_ld2)
                            if oc == 0:
                                b.w("vector", sV, istd_done)
                            if oc >= 2:
                                b.w("vector", sO, go_store[oc - 2])
                        b.op("vector", "scalar_tensor_tensor", gotmp[:, :],
                             gatef[:, gp * 512:(gp + 1) * 512], nw_s[:, oc:oc + 1],
                             o_in[:, p * NT + slab * 512: p * NT + (slab + 1) * 512],
                             ALU.mult, ALU.mult)
                        b.op("vector", "tensor_mul",
                             go_sb[:, p * NT + slab * 512: p * NT + (slab + 1) * 512],
                             gotmp[:, :], istd_b[:, slab * 512:(slab + 1) * 512], inc=sV)
                        gate_read[gp] = sV.n
                    gate_mm_done[oc] = sT.n
                    go_done[oc] = sV.n
                    if oc + 2 < FC:
                        b.w("sync", sV, go_done[oc])
                        b.op("sync", "dma_start", o_in[:, p * NT:(p + 1) * NT],
                             o_scr[oc + 2, :, :], inc=sA, incn=16)
                        o_lds2[oc + 2] = sA.n
                    if oc + 2 < FC:
                        b.w("gpsimd", sT, gate_mm_done[oc])
                        b.op("gpsimd", "dma_start", wgsb[:, p * D:(p + 1) * D],
                             wgate_d[oc + 2, :, :], inc=sW, incn=16)
                        wg_ld[p] = sW.n
                    for wo in (2 * oc + 1, 2 * oc + 2):
                        if wo < FC:
                            if wo == 10:
                                b.w("gpsimd", sT, gate_mm_done[1])
                            b.op("gpsimd", "dma_start",
                                 woutsb[:, wo * D:(wo + 1) * D],
                                 wout_d[wo, :, :], inc=sW, incn=16)
                    b.w("sync", sV, go_done[oc])
                    b.op("sync", "dma_start", go_scr[oc, :, :],
                         go_sb[:, p * NT:(p + 1) * NT], inc=sO, incn=16)
                    go_store[oc] = sO.n
                wout_ld = sW.n
                if dbg:
                    b.w("sync", sV, istd_done)
                    b.op("sync", "dma_start", istd_dbg[:, :], istd_b[:, :], inc=sO, incn=16)

            barrier(skips={"tensor": {"sO", "sA"}})

        with nc.sbuf_tensor("xn2", [128, FC * NT], BF16) as xn2:
            with contextlib.ExitStack() as esB2:
                ec = esB2.enter_context

                xB = ec(nc.sbuf_tensor("xB", [128, 2 * D], F32))
                x1sb = ec(nc.sbuf_tensor("x1sb", [128, 2 * D], F32))
                xn2tm = ec(nc.sbuf_tensor("xn2tm", [128, 2 * D], BF16))
                sqd2 = ec(nc.sbuf_tensor("sqd2", [128, D], BF16))
                stat2 = ec(nc.sbuf_tensor("stat2", [128, 64], F32))
                psO0 = ec(nc.psum_tensor("psO0", [128, 512], F32))
                psO1 = ec(nc.psum_tensor("psO1", [128, 512], F32))
                psO2 = ec(nc.psum_tensor("psO2", [128, 512], F32))
                psO3 = ec(nc.psum_tensor("psO3", [128, 512], F32))
                psT0 = ec(nc.psum_tensor("psT0", [128, 512], F32))
                psT1 = ec(nc.psum_tensor("psT1", [128, 512], F32))
                psT2 = ec(nc.psum_tensor("psT2", [128, 512], F32))
                psT3 = ec(nc.psum_tensor("psT3", [128, 512], F32))
                psO = (psO0, psO1, psO2, psO3)
                psT = (psT0, psT1, psT2, psT3)

                outproj_done = [0] * TC
                x1_evac = [0] * TC
                x1_store = [0] * TC
                xn2tm_done = [0] * TC
                ln2_tr_done = [0] * TC
                go_ld = [0] * TC
                x1_half_evac = [[0] * TC, [0] * TC]
                for pr0 in range(2):
                    b.w("gpsimd", sO, go_store[2 * pr0 + 1])
                    b.op("gpsimd", "dma_start",
                         go_t[:, pr0 * FC * 256:(pr0 + 1) * FC * 256].rearrange(
                             "q (c j) -> q c j", c=FC),
                         go_scr[:, :, pr0 * 256:(pr0 + 1) * 256].rearrange(
                             "c q j -> q c j"),
                         inc=sW, incn=16)
                    go_ld[2 * pr0] = go_ld[2 * pr0 + 1] = sW.n
                sq_done_l = [0] * TC
                psT_free = [0, 0, 0, 0]
                xn2_r = xn2[:, :].rearrange("q (c t) -> q c t", c=FC)

                def emit_ln2_chain(tc):
                    p = tc % 2
                    for i in range(4):
                        b.op("vector", "bn_stats",
                             stat2[:, p * 32 + i * 6: p * 32 + (i + 1) * 6],
                             x1sb[:, p * D + i * 512: p * D + (i + 1) * 512],
                             inc=(sV if i == 3 else None))
                    b.w("vector", sV, sV.n)
                    b.op("vector", "bn_aggr", stat2[:, p * 32 + 24: p * 32 + 26],
                         stat2[:, p * 32: p * 32 + 24], inc=sV)
                    var_done = sV.n
                    sq_done_l[tc] = 0
                    b.w("scalar", sV, var_done)
                    b.op("scalar", "activation", stat2[:, p * 32 + 26: p * 32 + 27],
                         stat2[:, p * 32 + 25: p * 32 + 26], AF.Sqrt,
                         bias=cst[:, C_EPS_LN:C_EPS_LN + 1], inc=sS)
                    std_done = sS.n
                    b.w("vector", sS, std_done)
                    b.op("vector", "reciprocal", stat2[:, p * 32 + 27: p * 32 + 28],
                         stat2[:, p * 32 + 26: p * 32 + 27], inc=sV)
                    b.w("vector", sV, sV.n)
                    if tc >= 2:
                        b.w("vector", sT, ln2_tr_done[tc - 2])
                    b.op("vector", "tensor_scalar", xn2tm[:, p * D:(p + 1) * D],
                         x1sb[:, p * D:(p + 1) * D], stat2[:, p * 32 + 24: p * 32 + 25],
                         stat2[:, p * 32 + 27: p * 32 + 28], ALU.subtract, ALU.mult, inc=sV)
                    xn2tm_done[tc] = sV.n

                def emit_ln2_transposes(tc):
                    p = tc % 2
                    b.w("tensor", sV, xn2tm_done[tc])
                    for pj in range(FC // 4):
                        r = pj % 4
                        b.w("tensor", sV, psT_free[r])
                        for half in range(4):
                            fc = 4 * pj + half
                            b.op("tensor", "matmul", psT[r][:, half * 128:(half + 1) * 128],
                                 xn2tm[:, p * D + fc * 128: p * D + (fc + 1) * 128],
                                 ident_s[:, :], start=True, stop=True,
                                 skip_group_check=True, inc=(sT if half == 3 else None))
                        mm_done = sT.n
                        b.w("vector", sT, mm_done)
                        b.op("vector", "tensor_copy",
                             xn2_r[:, 4 * pj:4 * pj + 4, tc * 128:(tc + 1) * 128],
                             psT[r][:, 0:512].rearrange("q (c t) -> q c t", c=4), inc=sV)
                        psT_free[r] = sV.n
                    ln2_tr_done[tc] = sT.n

                for tc in range(TC):
                    p = tc % 2
                    pr = tc // 2
                    if tc % 2 == 0 and pr >= 2:
                        b.w("gpsimd", sT, outproj_done[2 * (pr - 2) + 1])
                        b.op("gpsimd", "dma_start",
                             go_t[:, (pr % 2) * FC * 256:(pr % 2 + 1) * FC * 256].rearrange(
                                 "q (c j) -> q c j", c=FC),
                             go_scr[:, :, tc * 128:(tc + 2) * 128].rearrange(
                                 "c q j -> q c j"),
                             inc=sW, incn=16)
                        go_ld[tc] = go_ld[tc + 1] = sW.n
                    b.w("sync", sV, x1_evac[tc - 2] if tc >= 2 else 0)
                    b.op("sync", "dma_start", xB[:, p * D:(p + 1) * D],
                         x_d[tc * 128:(tc + 1) * 128, :], inc=sA, incn=16)
                    x_ld = sA.n
                    b.w("tensor", sW, go_ld[tc])
                    if tc == 0:
                        b.w("tensor", sW, wout_ld)
                    gbase = (pr % 2) * FC * 256 + (tc % 2) * 128
                    half_done = [0, 0]
                    for sp in range(2):
                        b.w("tensor", sV, x1_half_evac[sp][tc - 1] if tc >= 1 else 0)
                        for oc in range(FC):
                            for slab in (2 * sp, 2 * sp + 1):
                                b.op("tensor", "matmul", psO[slab][:, :],
                                     go_t[:, gbase + oc * 256: gbase + oc * 256 + 128],
                                     woutsb[:, oc * D + slab * 512: oc * D + (slab + 1) * 512],
                                     start=(oc == 0), stop=(oc == FC - 1),
                                     skip_group_check=True,
                                     inc=(sT if (oc == FC - 1 and slab == 2 * sp + 1) else None))
                        half_done[sp] = sT.n
                        b.w("vector", sT, half_done[sp])
                        if sp == 0:
                            b.w("vector", sA, x_ld)
                            if tc >= 2:
                                b.w("vector", sO, x1_store[tc - 2])
                        for slab in (2 * sp, 2 * sp + 1):
                            b.op("vector", "tensor_add",
                                 x1sb[:, p * D + slab * 512: p * D + (slab + 1) * 512],
                                 psO[slab][:, :],
                                 xB[:, p * D + slab * 512: p * D + (slab + 1) * 512],
                                 inc=(sV if slab == 2 * sp + 1 else None))
                        x1_half_evac[sp][tc] = sV.n
                    outproj_done[tc] = sT.n
                    x1_evac[tc] = sV.n
                    b.w("sync", sV, x1_evac[tc])
                    b.op("sync", "dma_start", x1_scr[tc * 128:(tc + 1) * 128, :],
                         x1sb[:, p * D:(p + 1) * D], inc=sO, incn=16)
                    x1_store[tc] = sO.n
                    if tc >= 2:
                        emit_ln2_transposes(tc - 2)
                    if tc >= 1:
                        emit_ln2_chain(tc - 1)
                emit_ln2_chain(TC - 1)
                emit_ln2_transposes(TC - 2)
                emit_ln2_transposes(TC - 1)
                xn2_done_v = sV.n
                if dbg:
                    b.w("sync", sV, xn2_done_v)
                    b.op("sync", "dma_start", xn2_dbg[:, :], xn2[:, :], inc=sO, incn=16)

            go_t_g.__exit__(None, None, None)
            woutsb_g.__exit__(None, None, None)
            barrier()
            wdsb_g = nc.sbuf_tensor("wdsb", [128, 2 * ICN * 512], BF16, side="right")
            wdsb = wdsb_g.__enter__()

            with contextlib.ExitStack() as esC:
                ec = esC.enter_context
                wgt = ec(nc.sbuf_tensor("wgt", [128, 2 * D], BF16))
                wut = ec(nc.sbuf_tensor("wut", [128, 2 * D], BF16))
                tg = ec(nc.sbuf_tensor("tg", [128, 2 * 512], F32))
                h_sb = ec(nc.sbuf_tensor("h_sb", [128, 2 * NT], BF16))
                psg0 = ec(nc.psum_tensor("psg0", [128, 512], F32))
                psg1 = ec(nc.psum_tensor("psg1", [128, 512], F32))
                psu0 = ec(nc.psum_tensor("psu0", [128, 512], F32))
                psu1 = ec(nc.psum_tensor("psu1", [128, 512], F32))
                psg = (psg0, psg1)
                psu = (psu0, psu1)
                wld = [0, 0]
                for ic in range(min(2, ICN)):
                    b.op("gpsimd", "dma_start", wgt[:, (ic % 2) * D:(ic % 2 + 1) * D],
                         wg_d[ic, :, :], inc=sW, incn=16)
                    b.op("gpsimd", "dma_start", wut[:, (ic % 2) * D:(ic % 2 + 1) * D],
                         wu_d[ic, :, :], inc=sW, incn=16)
                    wld[ic % 2] = sW.n
                gu_mm_done = [0] * ICN
                h_done = [0] * ICN
                h_store = [0] * ICN
                tg_free = [0, 0]
                psg_free = [0, 0]
                psu_free = [0, 0]
                for ic in range(ICN):
                    p = ic % 2
                    b.w("tensor", sW, wld[p])
                    if ic == 0:
                        b.w("tensor", sV, xn2_done_v)
                    for slab in range(4):
                        gp = slab % 2
                        b.w("tensor", sS, psg_free[gp])
                        for fc in range(FC):
                            b.op("tensor", "matmul", psg[gp][:, :],
                                 wgt[:, p * D + fc * 128: p * D + (fc + 1) * 128],
                                 xn2[:, fc * NT + slab * 512: fc * NT + (slab + 1) * 512],
                                 start=(fc == 0), stop=(fc == FC - 1),
                                 skip_group_check=True, inc=(sT if fc == FC - 1 else None))
                        g_mm = sT.n
                        b.w("tensor", sV, psu_free[gp])
                        for fc in range(FC):
                            b.op("tensor", "matmul", psu[gp][:, :],
                                 wut[:, p * D + fc * 128: p * D + (fc + 1) * 128],
                                 xn2[:, fc * NT + slab * 512: fc * NT + (slab + 1) * 512],
                                 start=(fc == 0), stop=(fc == FC - 1),
                                 skip_group_check=True, inc=(sT if fc == FC - 1 else None))
                        u_mm = sT.n
                        b.w("scalar", sT, g_mm)
                        b.w("scalar", sV, tg_free[gp])
                        b.op("scalar", "activation", tg[:, gp * 512:(gp + 1) * 512],
                             psg[gp][:, :], AF.Silu, scale=gs_s[:, ic:ic + 1], inc=sS)
                        t_done = sS.n
                        psg_free[gp] = sS.n
                        b.w("vector", sT, u_mm)
                        b.w("vector", sS, t_done)
                        if slab == 0 and ic >= 2:
                            b.w("vector", sO, h_store[ic - 2])
                        b.op("vector", "scalar_tensor_tensor",
                             h_sb[:, p * NT + slab * 512: p * NT + (slab + 1) * 512],
                             psu[gp][:, :], us_s[:, ic:ic + 1],
                             tg[:, gp * 512:(gp + 1) * 512], ALU.mult, ALU.mult, inc=sV)
                        tg_free[gp] = sV.n
                        psu_free[gp] = sV.n
                    gu_mm_done[ic] = sT.n
                    h_done[ic] = sV.n
                    if ic + 2 < ICN:
                        b.w("gpsimd", sT, gu_mm_done[ic])
                        b.op("gpsimd", "dma_start", wgt[:, p * D:(p + 1) * D],
                             wg_d[ic + 2, :, :], inc=sW, incn=16)
                        b.op("gpsimd", "dma_start", wut[:, p * D:(p + 1) * D],
                             wu_d[ic + 2, :, :], inc=sW, incn=16)
                        wld[p] = sW.n
                    if ic in (2, 4, 6, 8):
                        jj = (ic - 2) // 2
                        qq, piece = jj // 2, jj % 2
                        ic0, ic1 = piece * 22, (piece + 1) * 22
                        base = (qq % 2) * ICN * 512
                        b.op("gpsimd", "dma_start",
                             wdsb[:, base + ic0 * 512: base + ic1 * 512].rearrange(
                                 "q (c j) -> q c j", j=512),
                             wd_d[ic0:ic1, :, qq * 512:(qq + 1) * 512].rearrange(
                                 "c q j -> q c j"),
                             inc=sW, incn=16)
                    b.w("sync", sV, h_done[ic])
                    b.op("sync", "dma_start", h_scr[ic, :, :],
                         h_sb[:, p * NT:(p + 1) * NT], inc=sO, incn=16)
                    h_store[ic] = sO.n

            barrier()

        with contextlib.ExitStack() as esD:
            ec = esD.enter_context
            HTP = ICN * 256
            h_t = ec(nc.sbuf_tensor("h_t", [128, 4 * HTP], BF16))
            x1t = ec(nc.sbuf_tensor("x1t", [128, 2 * 512], F32))
            outsb = ec(nc.sbuf_tensor("outsb", [128, 2 * 512], F32))
            dtmp = ec(nc.sbuf_tensor("dtmp", [128, 512], F32))
            psD0 = ec(nc.psum_tensor("psD0", [128, 512], F32))
            psD1 = ec(nc.psum_tensor("psD1", [128, 512], F32))
            psD = (psD0, psD1)
            out_evac = {}
            out_store = {}
            d_mm_done = {}
            h_ld = {}
            wdq_ld = [0, 0]
            NQ = 4
            for q in range(NQ):
                if q >= 2:
                    b.w("gpsimd", sT, d_mm_done[(q - 2, TC - 1)])
                    base = (q % 2) * ICN * 512
                    for piece in range(2):
                        ic0, ic1 = piece * 22, (piece + 1) * 22
                        b.op("gpsimd", "dma_start",
                             wdsb[:, base + ic0 * 512: base + ic1 * 512].rearrange(
                                 "q (c j) -> q c j", j=512),
                             wd_d[ic0:ic1, :, q * 512:(q + 1) * 512].rearrange(
                                 "c q j -> q c j"),
                             inc=sW, incn=16)
                    wdq_ld[q % 2] = sW.n
                for tc in range(TC):
                    p = tc % 2
                    pr = tc // 2
                    pd = psD[p]
                    if tc >= 2:
                        key_prev = (q, tc - 2)
                    elif q >= 1:
                        key_prev = (q - 1, TC + tc - 2)
                    else:
                        key_prev = None
                    if tc % 2 == 0:
                        g = q * 8 + pr
                        if g >= 4:
                            qq, pp2 = divmod(g - 4, 8)
                            b.w("gpsimd", sT, d_mm_done[(qq, 2 * pp2 + 1)])
                        slot = g % 4
                        for jj in range(2):
                            ic0, ic1 = jj * 22, (jj + 1) * 22
                            b.op("gpsimd", "dma_start",
                                 h_t[:, slot * HTP + ic0 * 256:
                                     slot * HTP + ic1 * 256].rearrange(
                                     "q (c j) -> q c j", j=256),
                                 h_scr[ic0:ic1, :, tc * 128:(tc + 2) * 128].rearrange(
                                     "c q j -> q c j"),
                                 inc=sW, incn=16)
                        h_ld[(q, pr)] = sW.n
                    b.w("sync", sV, out_evac.get(key_prev, 0) if key_prev else 0)
                    b.op("sync", "dma_start", x1t[:, p * 512:(p + 1) * 512],
                         x1_scr[tc * 128:(tc + 1) * 128, q * 512:(q + 1) * 512],
                         inc=sA, incn=16)
                    x1_ld = sA.n
                    b.w("tensor", sW, h_ld[(q, pr)])
                    if q >= 2 and tc == 0:
                        b.w("tensor", sW, wdq_ld[q % 2])
                    b.w("tensor", sV, out_evac.get(key_prev, 0) if key_prev else 0)
                    hbase = ((q * 8 + pr) % 4) * HTP + (tc % 2) * 128
                    wbase = (q % 2) * ICN * 512
                    for ic in range(ICN):
                        b.op("tensor", "matmul", pd[:, :],
                             h_t[:, hbase + ic * 256: hbase + ic * 256 + 128],
                             wdsb[:, wbase + ic * 512: wbase + (ic + 1) * 512],
                             start=(ic == 0), stop=(ic == ICN - 1),
                             skip_group_check=True,
                             inc=(sT if ic == ICN - 1 else None))
                    d_mm_done[(q, tc)] = sT.n
                    b.w("vector", sT, d_mm_done[(q, tc)])
                    b.w("vector", sA, x1_ld)
                    prev_store = out_store.get((q, tc - 2) if tc >= 2 else
                                               ((q - 1, TC + tc - 2) if q >= 1 else None), 0)
                    b.w("vector", sO, prev_store)
                    b.op("vector", "tensor_mul", dtmp[:, :], pd[:, :],
                         dsb_s[:, q * 512:(q + 1) * 512])
                    b.op("vector", "tensor_add",
                         outsb[:, p * 512:(p + 1) * 512], dtmp[:, :],
                         x1t[:, p * 512:(p + 1) * 512], inc=sV)
                    out_evac[(q, tc)] = sV.n
                    b.w("sync", sV, out_evac[(q, tc)])
                    b.op("sync", "dma_start",
                         out_d[tc * 128:(tc + 1) * 128, q * 512:(q + 1) * 512],
                         outsb[:, p * 512:(p + 1) * 512], inc=sO, incn=16)
                    out_store[(q, tc)] = sO.n
        wdsb_g.__exit__(None, None, None)

        b.w("sync", sO, sO.n)

        with nc.Block() as block:
            @block.tensor
            def _(e):
                for f in b.streams["tensor"]:
                    f(e)

            @block.vector
            def _(e):
                for f in b.streams["vector"]:
                    f(e)

            @block.scalar
            def _(e):
                for f in b.streams["scalar"]:
                    f(e)

            @block.sync
            def _(e):
                for f in b.streams["sync"]:
                    f(e)

            @block.gpsimd
            def _(e):
                for f in b.streams["gpsimd"]:
                    f(e)

    return nc


def prep_inputs(L0_attn_hidden_states, L0_attn_slope_rate, L0_attn_w_qkv,
                L0_attn_w_gate, L0_attn_w_out, L0_attn_norm_w, L0_ffn_gate_w,
                L0_ffn_up_w, L0_ffn_down_w, L0_ffn_gate_s, L0_ffn_up_s, L0_ffn_down_s):
    f32 = lambda a: np.ascontiguousarray(np.asarray(a), dtype=np.float32)
    hs = f32(L0_attn_hidden_states)
    wqkv = f32(L0_attn_w_qkv)
    wgate = f32(L0_attn_w_gate)
    wout = f32(L0_attn_w_out)
    wg = f32(L0_ffn_gate_w)
    wu = f32(L0_ffn_up_w)
    wd = f32(L0_ffn_down_w)
    slope = f32(L0_attn_slope_rate).reshape(1, NH)
    nw = f32(L0_attn_norm_w)
    gsv = f32(L0_ffn_gate_s)
    usv = f32(L0_ffn_up_s)
    dsv = f32(L0_ffn_down_s)

    wqkv_img = np.ascontiguousarray(
        wqkv.reshape(NH, 3, HD, FC, 128).transpose(0, 4, 1, 3, 2).reshape(NH, 128, 3 * FC * 128))
    wgate_img = np.ascontiguousarray(
        wgate.reshape(FC, 128, FC, 128).transpose(0, 3, 2, 1).reshape(FC, 128, FC * 128))
    wout_img = np.ascontiguousarray(wout.T).reshape(FC, 128, D)
    wg_img = np.ascontiguousarray(
        wg.reshape(ICN, 128, FC, 128).transpose(0, 3, 2, 1).reshape(ICN, 128, FC * 128))
    wu_img = np.ascontiguousarray(
        wu.reshape(ICN, 128, FC, 128).transpose(0, 3, 2, 1).reshape(ICN, 128, FC * 128))
    wd_img = np.ascontiguousarray(wd.T).reshape(ICN, 128, D)

    common = {
        "wqkv": wqkv_img, "wgate": wgate_img, "wout": wout_img,
        "wgff": wg_img, "wuff": wu_img, "wdff": wd_img,
        "slopes": np.ascontiguousarray(np.broadcast_to(slope, (128, NH))),
        "nw": np.ascontiguousarray(nw.reshape(FC, 128).T),
        "gs": np.ascontiguousarray(gsv.reshape(ICN, 128).T),
        "us": np.ascontiguousarray(usv.reshape(ICN, 128).T),
        "dsb": np.ascontiguousarray(np.broadcast_to(dsv.reshape(1, D), (128, D))),
    }
    in_maps = []
    for c in range(8):
        m = dict(common)
        m["x"] = np.ascontiguousarray(hs[c])
        in_maps.append(m)
    return in_maps


def kernel(**inputs):
    in_maps = prep_inputs(**inputs)
    nc = build_core_program()
    res = run_bass_kernel_spmd(nc, in_maps, core_ids=list(range(8)))
    return np.stack([res.results[c]["out"] for c in range(8)], axis=0)
